# version 01
# baseline (speedup 1.0000x reference)
import numpy as np
from contextlib import ExitStack

import concourse.bass as bass
import concourse.mybir as mybir
from concourse.bass_utils import run_bass_kernel_spmd

F32 = mybir.dt.float32
BF16 = mybir.dt.bfloat16
I32 = mybir.dt.int32
AF = mybir.ActivationFunctionType
ALU = mybir.AluOpType
AX = mybir.AxisListType

LN_EPS = 1e-5
ATTN_EPS = 1e-6

B, N, D, FD = 2, 8192, 1024, 64
NCORES = 8
QPER = 4
T = N // QPER
NCH = T // 128
DH = D // 128
MH = (4 * D) // 128


def build_nc(taps=(), stages="ABCDEF"):
    nc = bass.Bass("TRN2", target_bir_lowering=False,
                   detect_race_conditions=False)

    x_d = nc.dram_tensor("x", [T, D], F32, kind="ExternalInput").ap()
    qkw_d = nc.dram_tensor("qkw_t", [D, 128], F32, kind="ExternalInput").ap()
    vw_d = nc.dram_tensor("vw_t", [D, D], F32, kind="ExternalInput").ap()
    ow_d = nc.dram_tensor("ow_t", [D, D], F32, kind="ExternalInput").ap()
    f1w_d = nc.dram_tensor("f1w_t", [D, 4 * D], F32, kind="ExternalInput").ap()
    f2w_d = nc.dram_tensor("f2w_t", [4 * D, D], F32, kind="ExternalInput").ap()
    g1m_d = nc.dram_tensor("g1m", [4, 2048], F32, kind="ExternalInput").ap()
    g2m_d = nc.dram_tensor("g2m", [4, 512], F32, kind="ExternalInput").ap()
    rows_d = nc.dram_tensor("rows", [9, 1024], F32, kind="ExternalInput").ap()
    f1bc_d = nc.dram_tensor("f1b_cols", [128, 32], F32, kind="ExternalInput").ap()
    wpre_d = nc.dram_tensor("w_pre", [128, 8], F32, kind="ExternalInput").ap()
    out_d = nc.dram_tensor("out", [T, D], F32, kind="ExternalOutput").ap()

    tap_d = {}
    for tname, shape, dt in [("tap_y", [T, D], F32), ("tap_pay", [128, 576], F32),
                             ("tap_xnfm", [128, 8, 2048], BF16),
                             ("tap_kvpre", [64, 1024], BF16),
                             ("tap_qke", [128, NCH * 128], BF16),
                             ("tap_mc", [128, 8, 1024], BF16)]:
        if tname in taps:
            tap_d[tname] = nc.dram_tensor(tname, shape, dt, kind="ExternalOutput").ap()

    pay_dram = nc.dram_tensor("pay_dram", [128, 576], F32, kind="Internal")
    ag_dram = nc.dram_tensor("ag_dram", [8, 128, 576], F32, kind="Internal")
    mc_dram = nc.dram_tensor("mc_dram", [1024, 1024], BF16, kind="Internal")
    f1bf_dram = nc.dram_tensor("f1bf_dram", [8, 128, 4096], BF16, kind="Internal")
    f2bf_dram = nc.dram_tensor("f2bf_dram", [32, 128, 1024], BF16, kind="Internal")

    st = ExitStack()
    e = st.enter_context

    y_t = e(nc.sbuf_tensor("y_t", [128, NCH, 1024], F32))
    qke_t = e(nc.sbuf_tensor("qke_t", [128, NCH, 128], BF16))
    qf32_t = e(nc.sbuf_tensor("qf32_t", [128, NCH, 64], F32))
    g1rep = e(nc.sbuf_tensor("g1rep", [128, 1024], BF16))
    b1rep = e(nc.sbuf_tensor("b1rep", [128, 1024], BF16))
    g2rep = e(nc.sbuf_tensor("g2rep", [128, 1024], BF16))
    b2rep = e(nc.sbuf_tensor("b2rep", [128, 1024], BF16))
    ob_rep = e(nc.sbuf_tensor("ob_rep", [128, 1024], BF16))
    f1b_cols = e(nc.sbuf_tensor("f1b_colsb", [128, 32], F32))
    mask_bf = e(nc.sbuf_tensor("mask_bf", [128, 128], BF16))
    identb = e(nc.sbuf_tensor("identb", [128, 128], BF16))
    ones_row_bf = e(nc.sbuf_tensor("ones_row_bf", [1, 128], BF16))
    ones_col_bf = e(nc.sbuf_tensor("ones_col_bf", [128, 1], BF16))
    ones_row_f32 = e(nc.sbuf_tensor("ones_row_f32", [1, 128], F32))
    qkb_row = e(nc.sbuf_tensor("qkb_row", [1, 128], BF16))
    vb_row = e(nc.sbuf_tensor("vb_row", [1, 1024], BF16))
    f2b_row = e(nc.sbuf_tensor("f2b_row", [1, 1024], BF16))
    fbias_row = e(nc.sbuf_tensor("fbias_row", [1, 1024], BF16))
    wpre_sb = e(nc.sbuf_tensor("wpre_sb", [128, 8], F32))
    kv_sb = e(nc.sbuf_tensor("kv_sb", [64, 1024], BF16))
    kvacc = e(nc.sbuf_tensor("kvacc", [64, 1024], F32))
    ks_run = e(nc.sbuf_tensor("ks_run", [1, 64], F32))
    ks_rep = e(nc.sbuf_tensor("ks_rep", [128, 64], F32))
    q0pre_bf = e(nc.sbuf_tensor("q0pre_bf", [1, 64], BF16))
    bnst = e(nc.sbuf_tensor("bnst", [128, 2, 6], F32))
    bnag = e(nc.sbuf_tensor("bnag", [128, 2], F32))
    scol = e(nc.sbuf_tensor("scol", [128, 8], F32))
    eps_col = e(nc.sbuf_tensor("eps_col", [128, 1], F32))
    den_f = e(nc.sbuf_tensor("den_f", [128, 1], F32))
    rden = e(nc.sbuf_tensor("rden", [128, 1], F32))
    den2 = e(nc.sbuf_tensor("den2", [128, 1], F32))
    tmpA = e(nc.sbuf_tensor("tmpA", [128, 1024], F32))
    tmpB = e(nc.sbuf_tensor("tmpB", [128, 1024], F32))
    xs_st = e(nc.sbuf_tensor("xs_st", [128, 2, 1024], F32))
    wstage = e(nc.sbuf_tensor("wstage", [128, 2048], F32))
    xnbf = e(nc.sbuf_tensor("xnbf", [128, 2, 1024], BF16))

    p1s = ExitStack()
    iota_t = p1s.enter_context(nc.sbuf_tensor("iota_t", [128, 128], I32))
    qzk = p1s.enter_context(nc.sbuf_tensor("qzk", [128, 128], BF16))
    am_bf = p1s.enter_context(nc.sbuf_tensor("am_bf", [128, 128], BF16))
    vtok = p1s.enter_context(nc.sbuf_tensor("vtok", [128, 2, 1024], BF16))
    num_fm = p1s.enter_context(nc.sbuf_tensor("num_fm", [128, 8, 128], BF16))
    qfm_sb = p1s.enter_context(nc.sbuf_tensor("qfm_sb", [64, 128], BF16))
    kfm_sb = p1s.enter_context(nc.sbuf_tensor("kfm_sb", [64, 128], BF16))
    xn_fm = p1s.enter_context(nc.sbuf_tensor("xn_fm", [128, 8, 2048], BF16))
    vw_bf = p1s.enter_context(nc.sbuf_tensor("vw_bf", [128, 8, 1024], BF16))
    ow_bf = p1s.enter_context(nc.sbuf_tensor("ow_bf", [128, 8, 1024], BF16))
    qkw_bf = p1s.enter_context(nc.sbuf_tensor("qkw_bf", [128, 8, 128], BF16))
    ttx = p1s.enter_context(nc.sbuf_tensor("ttx", [128, 256], BF16))
    gm_sb = p1s.enter_context(nc.sbuf_tensor("gm_sb", [4, 2560], BF16))
    ptpre = p1s.enter_context(nc.sbuf_tensor("ptpre", [128, 8, 64], F32))
    ptpre_bf = p1s.enter_context(nc.sbuf_tensor("ptpre_bf", [128, 8, 64], BF16))
    p1s.close()

    mc_bf = e(nc.sbuf_tensor("mc_bf", [128, 8, 1024], BF16))
    zfm = e(nc.sbuf_tensor("zfm", [128, 8, 512], BF16))
    h_fm = e(nc.sbuf_tensor("h_fm", [128, 32, 512], BF16))
    f1roll = e(nc.sbuf_tensor("f1roll", [128, 4, 8, 128], BF16))
    f2roll = e(nc.sbuf_tensor("f2roll", [128, 4, 1024], BF16))
    out_sb = e(nc.sbuf_tensor("out_sb", [128, 2, 1024], F32))

    pAp = ExitStack()
    repvp_ps = pAp.enter_context(nc.psum_tensor("repvp_ps", [128, 1024], F32))
    pt_ps = pAp.enter_context(nc.psum_tensor("pt_ps", [128, 1024], F32))
    trA_ps = pAp.enter_context(nc.psum_tensor("trA_ps", [128, 4096], BF16))
    vp_ps = repvp_ps

    semnames = ["sx0", "sx1", "sg16", "sgm", "smc", "str_", "so0", "so1", "sv", "ss", "stt", "sgc", "scc", "f1s0", "f1s1", "f1s2", "f1s3", "f2s0", "f2s1", "f2s2", "f2s3"]
    sems = {n: e(nc.semaphore(n)) for n in semnames}
    cnt = {n: 0 for n in semnames}
    land = {}

    def inc(instr, s, amt=1):
        cnt[s] += amt
        instr.then_inc(sems[s], amt)
        return cnt[s]

    def mark(key, s):
        land[key] = (s, cnt[s])

    agstage = tmpA[:, 0:576]
    payload = tmpB[:, 0:576]
    row_st = wstage[0:1, 1024:2048]

    def wait(eng, key):
        s, v = land[key]
        eng.wait_ge(sems[s], v)

    block = e(nc.Block())

    @block.gpsimd
    def _(g):
        inc(g.iota(iota_t[:], pattern=[[1, 128]], base=0, channel_multiplier=-1), "sgc")
        inc(g.memset(qzk[:], 0.0), "sgc")
        inc(g.memset(ones_row_bf[:], 1.0), "sgc")
        inc(g.memset(ones_col_bf[:], 1.0), "sgc")
        inc(g.memset(ones_row_f32[:], 1.0), "sgc")
        inc(g.memset(ks_run[:], 0.0), "sgc")
        inc(g.memset(eps_col[:], LN_EPS), "sgc")
        mark("iota", "sgc")
        inc(g.dma_start(out=f1b_cols[:], in_=f1bc_d), "sgm", 16)
        inc(g.dma_start(out=wpre_sb[:], in_=wpre_d), "sgm", 16)
        inc(g.dma_start(out=tmpA[0:4, :], in_=g1m_d[:, 0:1024]), "sgm", 16)
        inc(g.dma_start(out=tmpB[0:4, :], in_=g1m_d[:, 1024:2048]), "sgm", 16)
        inc(g.dma_start(out=xs_st[0:4, 0, 0:512], in_=g2m_d), "sgm", 16)
        mark("gm_f32", "sgm")
        inc(g.dma_start(out=wstage[:, 0:1024].rearrange("p (c m) -> p c m", c=8),
                        in_=qkw_d.rearrange("(c k) m -> k c m", k=128)), "sg16", 16)
        mark("qkw_f32", "sg16")
        g.wait_ge(sems["sg16"], cnt["sg16"])
        inc(g.tensor_copy(qkw_bf[:], wstage[:, 0:1024]
                          .rearrange("p (c m) -> p c m", c=8)), "sgc")
        g.wait_ge(sems["sgc"], cnt["sgc"])
        inc(g.nop(), "sgc")
        mark("qkw_bf", "sgc")

    @block.scalar
    def _(s):
        wait(s, "iota")

    @block.vector
    def _(v):
        wait(v, "iota")
        v.tensor_scalar(mask_bf[:], iota_t[:], 0, None, op0=ALU.is_ge)
        inc(v.tensor_scalar(identb[:], iota_t[:], 0, None, op0=ALU.is_equal), "sv")
        mark("masks", "sv")
        wait(v, "gm_f32")
        v.tensor_copy(gm_sb[0:4, 0:1024], tmpA[0:4, :])
        v.tensor_copy(gm_sb[0:4, 1024:2048], tmpB[0:4, :])
        inc(v.tensor_copy(gm_sb[0:4, 2048:2560], xs_st[0:4, 0, 0:512]), "sv")
        mark("gm_bf", "sv")

    rep_specs = [(0, g1rep), (1, b1rep), (2, g2rep), (3, b2rep), (4, ob_rep)]
    for ri, (rowi, dst) in enumerate(rep_specs):
        @block.gpsimd
        def _(g, ri=ri, rowi=rowi):
            if ri > 0:
                wait(g, f"rep{ri-1}_t")
            inc(g.dma_start(out=row_st[:], in_=rows_d[rowi:rowi + 1, :]), "sg16", 16)
            mark(f"rep{ri}_row", "sg16")

        @block.tensor
        def _(t, ri=ri):
            wait(t, f"rep{ri}_row")
            if ri == 0:
                wait(t, "iota")
            if ri > 0:
                wait(t, f"rep{ri-1}_v")
            for half in range(2):
                inc(t.matmul(vp_ps[:, half * 512:(half + 1) * 512], ones_row_f32[:],
                         row_st[0:1, half * 512:(half + 1) * 512], start=True, stop=True), "stt")
            mark(f"rep{ri}_t", "stt")

        @block.vector
        def _(v, ri=ri, dst=dst):
            wait(v, f"rep{ri}_t")
            inc(v.tensor_copy(dst[:], vp_ps[:]), "sv")
            mark(f"rep{ri}_v", "sv")

    bfrow_specs = [(8, qkb_row, 128), (5, vb_row, 1024), (6, f2b_row, 1024),
                   (7, fbias_row, 1024)]
    for bi, (rowi, dst, w) in enumerate(bfrow_specs):
        @block.gpsimd
        def _(g, bi=bi, rowi=rowi):
            wait(g, "rep4_v" if bi == 0 else f"brow{bi-1}_v")
            inc(g.dma_start(out=row_st[:], in_=rows_d[rowi:rowi + 1, :]), "sg16", 16)
            mark(f"brow{bi}", "sg16")

        @block.vector
        def _(v, bi=bi, dst=dst, w=w):
            wait(v, f"brow{bi}")
            inc(v.tensor_copy(dst[:], row_st[0:1, 0:w]), "sv")
            mark(f"brow{bi}_v", "sv")
    ROWS_DONE = f"brow{len(bfrow_specs)-1}_v"

    mc_view = mc_dram.ap().rearrange("(rh d g) (ch b e) -> rh ch b d g e",
                                     rh=2, d=32, g=16, ch=2, b=32, e=16)

    def tt_chain(imag):
        g1o = 1024 if imag else 0
        g2o = 2304 if imag else 2048
        tag = "i" if imag else "r"
        for p in range(8):
            @block.tensor
            def _(t, p=p):
                if p == 0 and imag:
                    wait(t, "gm_bf")
                    wait(t, "rep4_v")
                if p == 0 and not imag:
                    wait(t, "ttevi7")
                if p > 0:
                    wait(t, f"ttev{tag}{p-1}")
                inc(t.matmul(vp_ps[:, 0:256], gm_sb[0:4, g1o + p * 128:g1o + (p + 1) * 128],
                         gm_sb[0:4, g2o:g2o + 256], start=True, stop=True), "stt")
                mark(f"ttmm{tag}{p}", "stt")

            @block.vector
            def _(v, p=p):
                wait(v, f"ttmm{tag}{p}")
                if p > 0:
                    wait(v, f"ttst{tag}{p-1}")
                elif not imag:
                    wait(v, "ttsti7")
                inc(v.tensor_copy(ttx[:], vp_ps[:, 0:256]), "sv")
                mark(f"ttev{tag}{p}", "sv")

            if imag:
                @block.gpsimd
                def _(g, p=p):
                    wait(g, f"ttev{tag}{p}")
                    for bq in range(4):
                        src_ = ttx[bq * 32:(bq + 1) * 32, :].rearrange(
                            "d (g e) -> d g e", g=16)
                        inc(g.dma_start(out=mc_view[0, 1, 4 * p + bq, :, :, :],
                                        in_=src_), "sg16", 16)
                    mark(f"ttstwi{p}", "sg16")

                @block.scalar
                def _(s, p=p):
                    wait(s, f"ttstwi{p}")
                    inc(s.mul(ttx[:], ttx[:], -1.0), "ss")
                    mark(f"ttneg{p}", "ss")

                @block.gpsimd
                def _(g, p=p):
                    wait(g, f"ttneg{p}")
                    for bq in range(4):
                        src_ = ttx[bq * 32:(bq + 1) * 32, :].rearrange(
                            "d (g e) -> d g e", g=16)
                        inc(g.dma_start(out=mc_view[1, 0, 4 * p + bq, :, :, :],
                                        in_=src_), "sg16", 16)
                    mark(f"ttst{tag}{p}", "sg16")
            else:
                @block.gpsimd
                def _(g, p=p):
                    wait(g, f"ttev{tag}{p}")
                    for bq in range(4):
                        src_ = ttx[bq * 32:(bq + 1) * 32, :].rearrange(
                            "d (g e) -> d g e", g=16)
                        inc(g.dma_start(out=mc_view[0, 0, 4 * p + bq, :, :, :],
                                        in_=src_), "sg16", 16)
                        inc(g.dma_start(out=mc_view[1, 1, 4 * p + bq, :, :, :],
                                        in_=src_), "sg16", 16)
                    mark(f"ttst{tag}{p}", "sg16")

    tt_chain(imag=True)
    tt_chain(imag=False)

    @block.gpsimd
    def _(g):
        mark("st_wr", "sg16")
        for nm, src_d, dst in [("vw", vw_d, vw_bf), ("ow", ow_d, ow_bf)]:
            for i0 in range(0, 8, 2):
                g.wait_ge(sems["sgc"], cnt["sgc"])
                inc(g.dma_start(out=wstage[:].rearrange("p (c m) -> p c m", c=2),
                                in_=src_d[i0 * 128:(i0 + 2) * 128, :]
                                .rearrange("(c k) m -> k c m", k=128)), "sg16", 16)
                g.wait_ge(sems["sg16"], cnt["sg16"])
                inc(g.tensor_copy(dst[:, i0:i0 + 2, :],
                                  wstage[:].rearrange("p (c m) -> p c m", c=2)), "sgc")
        g.wait_ge(sems["sgc"], cnt["sgc"])
        inc(g.nop(), "sgc")
        mark("vwow_bf", "sgc")

    def stats_chain(src_hi, src_lo, tag, s2=False):
        @block.vector
        def _(v):
            v.bn_stats(bnst[:, 0, :], src_hi)
            inc(v.bn_stats(bnst[:, 1, :], src_lo), "sv")
            v.drain()
            v.wait_ge(sems["sv"], cnt["sv"])
            inc(v.bn_aggr(bnag[:], bnst[:]), "sv")
            v.drain()
            v.wait_ge(sems["sv"], cnt["sv"])
            inc(v.nop(), "sv")
            mark(f"sa{tag}", "sv")

        @block.scalar
        def _(s):
            wait(s, f"sa{tag}")
            inc(s.activation(scol[:, 4:5], bnag[:, 1:2], AF.Ln, bias=eps_col[:]), "ss")
            s.wait_ge(sems["ss"], cnt["ss"])
            inc(s.activation(scol[:, 0:1], scol[:, 4:5], AF.Exp, scale=-0.5), "ss")
            mark(f"ss1{tag}", "ss")

        @block.vector
        def _(v):
            wait(v, f"ss1{tag}")
            if s2:
                v.scalar_tensor_tensor(scol[:, 6:7], bnag[:, 1:2], scol[:, 0:1],
                                       scol[:, 0:1], op0=ALU.mult, op1=ALU.mult)
            inc(v.tensor_scalar(scol[:, 1:2], bnag[:, 0:1], scol[:, 0:1], -1.0,
                                op0=ALU.mult, op1=ALU.mult), "sv")
            mark(f"sb{tag}", "sv")

        if s2:
            @block.scalar
            def _(s):
                wait(s, f"sb{tag}")
                inc(s.activation(scol[:, 7:8], scol[:, 6:7], AF.Ln, bias=eps_col[:]), "ss")
                s.wait_ge(sems["ss"], cnt["ss"])
                inc(s.activation(scol[:, 2:3], scol[:, 7:8], AF.Exp, scale=-0.5), "ss")
                mark(f"ss2{tag}", "ss")

    if "A" in stages:
        @block.scalar
        def _(s):
            for c in range(2):
                inc(s.dma_start(out=xs_st[:, c, :], in_=x_d[c * 128:(c + 1) * 128, :]),
                    f"sx{c % 2}", 16)
                mark(f"xld{c}", f"sx{c % 2}")

        for c in range(NCH):
            buf = c % 2

            @block.vector
            def _(v, c=c):
                wait(v, f"xld{c}")
                if c == 0:
                    wait(v, "rep4_v")
                    wait(v, "gm_bf")

            stats_chain(xs_st[:, buf, 0:512], xs_st[:, buf, 512:1024],
                        tag=f"a{c}", s2=True)

            @block.scalar
            def _(s, c=c, buf=buf):
                wait(s, f"sb" + f"a{c}")
                s.wait_ge(sems["ss"], cnt["ss"])
                if c > 0:
                    wait(s, f"eluv{c-1}")
                inc(s.activation(tmpA[:], xs_st[:, buf, :], AF.Identity,
                                 bias=scol[:, 1:2], scale=scol[:, 0:1]), "ss")
                mark(f"y0{c}", "ss")
                s.wait_ge(sems["ss"], cnt["ss"])
                if c + 2 < NCH:
                    inc(s.dma_start(out=xs_st[:, c % 2, :],
                                    in_=x_d[(c + 2) * 128:(c + 3) * 128, :]),
                        f"sx{c % 2}", 16)
                    mark(f"xld{c+2}", f"sx{c % 2}")

            @block.vector
            def _(v, c=c):
                wait(v, f"y0{c}")
                v.tensor_tensor(y_t[:, c, :], tmpA[:], ob_rep[:], op=ALU.add)
                v.scalar_tensor_tensor(tmpB[:], tmpA[:], scol[:, 2:3], g1rep[:],
                                       op0=ALU.mult, op1=ALU.mult)
                v.drain()
                if c >= 2:
                    wait(v, f"xtrt{c-2}")
                    wait(v, f"pt{c-2}")
                inc(v.tensor_tensor(xnbf[:, c % 2, :], tmpB[:], b1rep[:], op=ALU.add), "sv")
                mark(f"xnv{c}", "sv")

            @block.tensor
            def _(t, c=c):
                wait(t, f"xnv{c}")
                if c == 0:
                    wait(t, "masks")
                if c > 0:
                    wait(t, f"xtr{c-1}")
                for i in range(8):
                    col = (i % 4) * 1024 + (i // 4) * 128
                    ti = t.transpose(trA_ps[:, col:col + 128],
                                     xnbf[:, c % 2, i * 128:(i + 1) * 128], identb[:])
                    if i == 3:
                        inc(ti, "stt")
                        t.wait_ge(sems["stt"], cnt["stt"])
                inc(t.nop(), "stt")
                mark(f"xtrt{c}", "stt")

            @block.vector
            def _(v, c=c):
                wait(v, f"xtrt{c}")
                for b in range(4):
                    cp = v.tensor_copy(
                        xn_fm[:, b:8:4, c * 128:(c + 1) * 128],
                        trA_ps[:, b * 1024:b * 1024 + 256]
                        .rearrange("p (a b) -> p a b", a=2))
                    if b == 3:
                        inc(cp, "sv")
                mark(f"xtr{c}", "sv")

            @block.tensor
            def _(t, c=c):
                wait(t, f"xtr{c}")
                if c == 0:
                    wait(t, "qkw_bf")
                    wait(t, "brow0_v")
                    wait(t, "iota")
                if c == 0:
                    wait(t, "ttevr7")
                if c > 0:
                    wait(t, f"eluv{c-1}")
                for i in range(8):
                    t.matmul(repvp_ps[:, 0:128], xn_fm[:, i, c * 128:(c + 1) * 128],
                             qkw_bf[:, i, :], start=(i == 0), stop=False)
                inc(t.matmul(repvp_ps[:, 0:128], ones_row_bf[:], qkb_row[:],
                         start=False, stop=True), "stt")
                mark(f"qkmm{c}", "stt")

            @block.vector
            def _(v, c=c):
                wait(v, f"qkmm{c}")
                inc(v.tensor_scalar(tmpA[:, 0:128], repvp_ps[:, 0:128], 0.0, None, op0=ALU.min), "sv")
                mark(f"elumin{c}", "sv")

            @block.scalar
            def _(s, c=c):
                wait(s, f"elumin{c}")
                inc(s.activation(tmpA[:, 128:256], tmpA[:, 0:128], AF.Exp), "ss")
                mark(f"eluexp{c}", "ss")

            @block.vector
            def _(v, c=c):
                wait(v, f"eluexp{c}")
                v.scalar_tensor_tensor(tmpA[:, 256:384], repvp_ps[:, 0:128], 0.0,
                                       tmpA[:, 128:256], op0=ALU.max, op1=ALU.add)
                v.drain()
                v.tensor_copy(qke_t[:, c, :], tmpA[:, 256:384])
                inc(v.tensor_copy(qf32_t[:, c, :], tmpA[:, 256:320]), "sv")
                mark(f"eluv{c}", "sv")

            @block.tensor
            def _(t, c=c):
                wait(t, f"eluv{c}")
                for i in range(8):
                    t.matmul(pt_ps[:, i * 64:(i + 1) * 64],
                             xnbf[:, c % 2, i * 128:(i + 1) * 128],
                             qke_t[:, c, 64:128],
                             start=(c == 0 and i == 0),
                             stop=(c == NCH - 1 and i == 7))
                inc(t.matmul(pt_ps[0:1, 512:576], ones_col_bf[:], qke_t[:, c, 64:128],
                         start=(c == 0), stop=(c == NCH - 1)), "stt")
                mark(f"pt{c}", "stt")

        @block.vector
        def _(v):
            wait(v, f"pt{NCH-1}")
            v.tensor_copy(payload[:, 0:512], pt_ps[:, 0:512])
            inc(v.tensor_copy(payload[0:1, 512:576], pt_ps[0:1, 512:576]), "sv")
            mark("payv", "sv")

        if "tap_xnfm" in tap_d:
            @block.gpsimd
            def _(g):
                wait(g, f"pt{NCH-1}")
                inc(g.dma_start(out=tap_d["tap_xnfm"], in_=xn_fm[:]), "sg16", 16)

        if "tap_pay" in tap_d:
            @block.gpsimd
            def _(g):
                wait(g, "payv")
                inc(g.dma_start(out=tap_d["tap_pay"], in_=payload[:]), "sg16", 16)
        if "tap_qke" in tap_d:
            @block.gpsimd
            def _(g):
                wait(g, f"eluv{NCH-1}")
                inc(g.dma_start(out=tap_d["tap_qke"]
                                .rearrange("p (a b) -> p a b", b=128),
                                in_=qke_t[:]), "sg16", 16)
        if "tap_y" in tap_d and "C" not in stages:
            @block.gpsimd
            def _(g):
                wait(g, f"xnv{NCH-1}")
                for c in range(NCH):
                    inc(g.dma_start(out=tap_d["tap_y"][c * 128:(c + 1) * 128, :],
                                    in_=y_t[:, c, :]), "sg16", 16)

    if "B" in stages or "C" in stages:
        pAp.close()
        p1p = ExitStack()
        kv_ps = p1p.enter_context(nc.psum_tensor("kv_ps", [64, 1024], F32))
        vp_ps = p1p.enter_context(nc.psum_tensor("vp_ps", [128, 1024], F32))
        no_ps = p1p.enter_context(nc.psum_tensor("no_ps", [128, 1024], F32))
        m1_ps = p1p.enter_context(nc.psum_tensor("m1_ps", [128, 512], F32))
        trB_ps = p1p.enter_context(nc.psum_tensor("trB_ps", [128, 1024], BF16))

    if "B" in stages:
        @block.gpsimd
        def _(g):
            wait(g, "payv")
            inc(g.dma_start(out=pay_dram.ap(), in_=payload[:]), "sg16", 16)
            mark("paydram", "sg16")
            wait(g, "paydram")
            inc(g.collective_compute(
                "AllGather", ALU.bypass, replica_groups=[list(range(NCORES))],
                ins=[pay_dram.ap().opt()], outs=[ag_dram.ap().opt()]), "scc")
            mark("ag", "scc")
            wait(g, "ag")

        for cc in range(NCORES):
            @block.gpsimd
            def _(g, cc=cc):
                if cc > 0:
                    wait(g, f"agacc{cc-1}")
                inc(g.dma_start(out=agstage[:], in_=ag_dram.ap()[cc, :, :]), "sg16", 16)
                mark(f"agld{cc}", "sg16")

            @block.vector
            def _(v, cc=cc):
                wait(v, f"agld{cc}")
                if cc == 0:
                    wait(v, "gm_f32")
                pt_flat = ptpre[:].rearrange("p a b -> p (a b)")
                if cc == 0:
                    v.tensor_scalar(pt_flat, agstage[:, 0:512], wpre_sb[:, 0:1],
                                    None, op0=ALU.mult)
                    inc(v.tensor_scalar(ks_run[:], agstage[0:1, 512:576],
                                        wpre_sb[0:1, 0:1], None, op0=ALU.mult), "sv")
                else:
                    v.scalar_tensor_tensor(pt_flat, agstage[:, 0:512],
                                           wpre_sb[:, cc:cc + 1], pt_flat,
                                           op0=ALU.mult, op1=ALU.add)
                    inc(v.scalar_tensor_tensor(ks_run[:], agstage[0:1, 512:576],
                                           wpre_sb[0:1, cc:cc + 1], ks_run[:],
                                           op0=ALU.mult, op1=ALU.add), "sv")
                mark(f"agacc{cc}", "sv")

        @block.vector
        def _(v):
            wait(v, f"agacc{NCORES-1}")
            v.drain()
            v.tensor_copy(ptpre_bf[:], ptpre[:])
            inc(v.tensor_copy(q0pre_bf[:], ks_run[:]), "sv")
            mark("ptbf", "sv")

        @block.tensor
        def _(t):
            wait(t, "ptbf")
            wait(t, "vwow_bf")
            for i in range(8):
                for half in range(2):
                    t.matmul(kv_ps[:, half * 512:(half + 1) * 512], ptpre_bf[:, i, :],
                             vw_bf[:, i, half * 512:(half + 1) * 512],
                             start=(i == 0), stop=False)
            for half in range(2):
                inc(t.matmul(kv_ps[:, half * 512:(half + 1) * 512], q0pre_bf[:],
                         vb_row[0:1, half * 512:(half + 1) * 512],
                         start=False, stop=True), "stt")
            mark("kvfold", "stt")

        @block.vector
        def _(v):
            wait(v, "kvfold")
            v.tensor_copy(kvacc[:], kv_ps[:])
            inc(v.tensor_copy(kv_sb[:], kv_ps[:]), "sv")
            mark("kvsb_-1", "sv")

        @block.tensor
        def _(t):
            wait(t, "ptbf")
            inc(t.matmul(m1_ps[:, 256:320], ones_row_f32[:], ks_run[:], start=True, stop=True), "stt")
            mark("ksrept_-1", "stt")

        @block.vector
        def _(v):
            wait(v, "ksrept_-1")
            inc(v.tensor_copy(ks_rep[:], m1_ps[:, 256:320]), "sv")
            mark("ksrepv_-1", "sv")

        if "tap_kvpre" in tap_d:
            @block.gpsimd
            def _(g):
                wait(g, "kvsb_-1")
                inc(g.dma_start(out=tap_d["tap_kvpre"], in_=kv_sb[:]), "sg16", 16)

    if "C" in stages:
        for c in range(NCH):
            buf = c % 2

            @block.tensor
            def _(t, c=c, buf=buf):
                if c == 0:
                    wait(t, f"xtr{NCH-1}")
                if c > 0:
                    wait(t, f"trevac{c-1}")
                inc(t.transpose(trB_ps[:, 0:128], qke_t[:, c, :], identb[:]), "stt")
                mark(f"qtr{c}", "stt")
                if c == 0:
                    wait(t, "vwow_bf")
                    wait(t, ROWS_DONE)
                    wait(t, "ttevr7")
                    wait(t, f"eluv{NCH-1}")
                else:
                    wait(t, f"vtok_s{c-1}")
                for half in range(2):
                    for i in range(8):
                        t.matmul(vp_ps[:, half * 512:(half + 1) * 512],
                                 xn_fm[:, i, c * 128:(c + 1) * 128],
                                 vw_bf[:, i, half * 512:(half + 1) * 512],
                                 start=(i == 0), stop=False)
                    inc(t.matmul(vp_ps[:, half * 512:(half + 1) * 512], ones_row_bf[:],
                             vb_row[0:1, half * 512:(half + 1) * 512],
                             start=False, stop=True), "stt")
                mark(f"vp{c}", "stt")

            @block.scalar
            def _(s, c=c, buf=buf):
                wait(s, f"vp{c}")
                inc(s.activation(vtok[:, buf, :], vp_ps[:], AF.Identity), "ss")
                mark(f"vtok_s{c}", "ss")

            @block.vector
            def _(v, c=c):
                if c > 0:
                    wait(v, f"tr{c-1}")
                inc(v.tensor_copy(qzk[:, 0:64], qke_t[:, c, 64:128]), "sv")
                mark(f"qzkv{c}", "sv")

            @block.tensor
            def _(t, c=c):
                wait(t, f"qzkv{c}")
                t.wait_ge(sems["stt"], land[f"qtr{c}"][1])
                inc(t.transpose(trB_ps[:, 512:640], qzk[:], identb[:]), "stt")
                mark(f"tr{c}", "stt")

            @block.vector
            def _(v, c=c):
                wait(v, f"tr{c}")
                v.tensor_copy(qfm_sb[:], trB_ps[0:64, 0:128])
                inc(v.tensor_copy(kfm_sb[:], trB_ps[0:64, 512:640]), "sv")
                mark(f"trevac{c}", "sv")

            @block.tensor
            def _(t, c=c):
                wait(t, f"trevac{c}")
                wait(t, f"ksrepv_{c-1}" if c == 0 else f"ksrepv{c-1}")
                inc(t.matmul(m1_ps[:, 0:128], kfm_sb[:], qfm_sb[:], start=True, stop=True), "stt")
                mark(f"A{c}", "stt")

            @block.vector
            def _(v, c=c):
                wait(v, f"A{c}")
                inc(v.tensor_tensor(am_bf[:], m1_ps[:, 0:128], mask_bf[:], op=ALU.mult), "sv")
                mark(f"am{c}", "sv")

            @block.tensor
            def _(t, c=c):
                wait(t, f"am{c}")
                t.matmul(m1_ps[:, 128:129], am_bf[:], ones_col_bf[:], start=True, stop=True)
                inc(t.matmul(m1_ps[0:1, 192:256], ones_col_bf[:], qke_t[:, c, 64:128],
                         start=True, stop=True), "stt")
                mark(f"dent{c}", "stt")

            @block.vector
            def _(v, c=c):
                wait(v, f"dent{c}")
                v.scalar_tensor_tensor(tmpB[:, 0:64], qf32_t[:, c, :], 1.0,
                                       ks_rep[:], op0=ALU.mult, op1=ALU.mult,
                                       accum_out=den2[:])
                v.scalar_tensor_tensor(ks_run[:], m1_ps[0:1, 192:256], 1.0, ks_run[:],
                                       op0=ALU.mult, op1=ALU.add)
                v.drain()
                v.scalar_tensor_tensor(den_f[:], m1_ps[:, 128:129], ATTN_EPS, den2[:],
                                       op0=ALU.add, op1=ALU.add)
                v.drain()
                inc(v.reciprocal(rden[:], den_f[:]), "sv")
                mark(f"denv{c}", "sv")

            @block.tensor
            def _(t, c=c):
                wait(t, f"denv{c}")
                inc(t.matmul(m1_ps[:, 256:320], ones_row_f32[:], ks_run[:], start=True, stop=True), "stt")
                mark(f"ksrept{c}", "stt")

            @block.vector
            def _(v, c=c):
                wait(v, f"ksrept{c}")
                inc(v.tensor_copy(ks_rep[:], m1_ps[:, 256:320]), "sv")
                mark(f"ksrepv{c}", "sv")

            @block.tensor
            def _(t, c=c, buf=buf):
                wait(t, f"vtok_s{c}")
                wait(t, f"am{c}")
                wait(t, "kvsb_-1" if c == 0 else f"kvsb_{c-1}")
                if c > 0:
                    wait(t, f"y1_{c-1}")
                for i in range(8):
                    t.matmul(no_ps[:, i * 128:(i + 1) * 128],
                             vtok[:, buf, i * 128:(i + 1) * 128], am_bf[:],
                             start=True, stop=False)
                    inc(t.matmul(no_ps[:, i * 128:(i + 1) * 128],
                             kv_sb[:, i * 128:(i + 1) * 128], qfm_sb[:],
                             start=False, stop=True), "stt")
                mark(f"num{c}", "stt")

            @block.vector
            def _(v, c=c):
                wait(v, f"num{c}")
                inc(v.tensor_copy(num_fm[:].rearrange("p a b -> p (a b)"), no_ps[:]), "sv")
                mark(f"numev{c}", "sv")

            @block.tensor
            def _(t, c=c, buf=buf):
                for half in range(2):
                    inc(t.matmul(kv_ps[:, half * 512:(half + 1) * 512],
                             qke_t[:, c, 64:128],
                             vtok[:, buf, half * 512:(half + 1) * 512],
                             start=True, stop=True), "stt")
                mark(f"kvupd{c}", "stt")

            @block.vector
            def _(v, c=c):
                wait(v, f"kvupd{c}")
                v.scalar_tensor_tensor(kvacc[:], kv_ps[:], 1.0, kvacc[:],
                                       op0=ALU.mult, op1=ALU.add)
                v.drain()
                inc(v.tensor_copy(kv_sb[:], kvacc[:]), "sv")
                mark(f"kvsb_{c}", "sv")

            @block.tensor
            def _(t, c=c):
                wait(t, f"numev{c}")
                for half in range(2):
                    for i in range(8):
                        inc(t.matmul(no_ps[:, half * 512:(half + 1) * 512], num_fm[:, i, :],
                                 ow_bf[:, i, half * 512:(half + 1) * 512],
                                 start=(i == 0), stop=(i == 7)), "stt")
                mark(f"O{c}", "stt")

            @block.vector
            def _(v, c=c):
                wait(v, f"O{c}")
                inc(v.scalar_tensor_tensor(y_t[:, c, :], no_ps[:], rden[:], y_t[:, c, :],
                                       op0=ALU.mult, op1=ALU.add), "sv")
                mark(f"y1_{c}", "sv")

        @block.gpsimd
        def _(g):
            wait(g, "ttstr7")
            wait(g, f"O{NCH-1}")
            inc(g.dma_start(out=mc_bf[:],
                            in_=mc_dram.ap().rearrange("(i p) n -> p i n", p=128)),
                "smc", 16)
            mark("mc_loaded", "smc")

        if "tap_mc" in tap_d:
            @block.gpsimd
            def _(g):
                wait(g, "mc_loaded")
                inc(g.dma_start(out=tap_d["tap_mc"], in_=mc_bf[:]), "sg16", 16)

        if "tap_y" in tap_d and "D" not in stages:
            @block.gpsimd
            def _(g):
                wait(g, f"y1_{NCH-1}")
                for c in range(NCH):
                    inc(g.dma_start(out=tap_d["tap_y"][c * 128:(c + 1) * 128, :],
                                    in_=y_t[:, c, :]), "sg16", 16)

    if "D" in stages:
        @block.gpsimd
        def _(g):
            hf1 = h_fm[:, 0:4, :].rearrange("p a b -> p (a b)")
            for i in range(8):
                for half in range(2):
                    inc(g.dma_start(out=wstage[:],
                                    in_=f1w_d[i * 128:(i + 1) * 128,
                                              half * 2048:(half + 1) * 2048]), "sg16", 16)
                    g.wait_ge(sems["sg16"], cnt["sg16"])
                    inc(g.tensor_copy(hf1, wstage[:]), "sgc")
                    g.wait_ge(sems["sgc"], cnt["sgc"])
                    inc(g.dma_start(out=f1bf_dram.ap()[i, :, half * 2048:(half + 1) * 2048],
                                    in_=hf1), "sg16", 16)
            mark("f1prep", "sg16")
            hf2b = h_fm[:, 5:7, :].rearrange("p a b -> p (a b)")
            for m in range(32):
                inc(g.dma_start(out=wstage[:, 0:1024],
                                in_=f2w_d[m * 128:(m + 1) * 128, :]), "sg16", 16)
                g.wait_ge(sems["sg16"], cnt["sg16"])
                inc(g.tensor_copy(hf2b[:, 0:1024], wstage[:, 0:1024]), "sgc")
                g.wait_ge(sems["sgc"], cnt["sgc"])
                inc(g.dma_start(out=f2bf_dram.ap()[m, :, :], in_=hf2b[:, 0:1024]),
                    "sg16", 16)
            mark("f2prep", "sg16")

    if "D" in stages:
        p1p.close()
        p2p = ExitStack()
        f2ps = [p2p.enter_context(nc.psum_tensor(f"f2ps{i}", [128, 1024], F32))
                for i in range(2)]
        tr2_ps = p2p.enter_context(nc.psum_tensor("tr2_ps", [128, 4096], BF16))
        f1ps = [f2ps[0][:, 0:512], f2ps[0][:, 512:1024]]

        for j in range(4):
            for ci in range(4):
                c = 4 * j + ci

                @block.vector
                def _(v, c=c, j=j):
                    wait(v, f"y1_{c}")
                    if j > 0:
                        wait(v, f"y2_{4*(j-1)+3}")

                stats_chain(y_t[:, c, 0:512], y_t[:, c, 512:1024], tag=f"d{c}")

                @block.scalar
                def _(s, c=c):
                    wait(s, f"sbd{c}")
                    s.wait_ge(sems["ss"], cnt["ss"])
                    inc(s.activation(tmpA[:], y_t[:, c, :], AF.Identity,
                                 bias=scol[:, 1:2], scale=scol[:, 0:1]), "ss")
                    mark(f"z2s{c}", "ss")

                @block.vector
                def _(v, c=c):
                    wait(v, f"z2s{c}")
                    v.tensor_tensor(tmpB[:], tmpA[:], g2rep[:], op=ALU.mult)
                    v.drain()
                    if c >= 2:
                        wait(v, f"ztrt{c-2}")
                    inc(v.tensor_tensor(xnbf[:, c % 2, :], tmpB[:], b2rep[:], op=ALU.add), "sv")
                    mark(f"z2v{c}", "sv")

                @block.tensor
                def _(t, c=c, ci=ci, j=j):
                    wait(t, f"z2v{c}")
                    if c > 0:
                        wait(t, f"ztr{c-1}")
                    if j > 0:
                        wait(t, f"f1mm{j-1}_31")
                    for i in range(8):
                        col = (i % 4) * 1024 + (i // 4) * 128
                        ti = t.transpose(tr2_ps[:, col:col + 128],
                                         xnbf[:, c % 2, i * 128:(i + 1) * 128], identb[:])
                        if i == 3:
                            inc(ti, "stt")
                            t.wait_ge(sems["stt"], cnt["stt"])
                    inc(t.nop(), "stt")
                    mark(f"ztrt{c}", "stt")

                @block.vector
                def _(v, c=c, ci=ci):
                    wait(v, f"ztrt{c}")
                    for b in range(4):
                        cp = v.tensor_copy(
                            zfm[:, b:8:4, ci * 128:(ci + 1) * 128],
                            tr2_ps[:, b * 1024:b * 1024 + 256]
                            .rearrange("p (a b) -> p a b", a=2))
                        if b == 3:
                            inc(cp, "sv")
                    mark(f"ztr{c}", "sv")

            for m in range(32):
                @block.gpsimd
                def _(g, j=j, m=m):
                    if m == 0:
                        wait(g, "f2prep")
                    if m >= 4:
                        wait(g, f"f1use{j}_{m-4}")
                    elif j > 0:
                        wait(g, f"f1use{j-1}_{28+m}")
                    src = f1bf_dram.ap()[:, :, m * 128:(m + 1) * 128] \
                        .rearrange("i p f -> p i f")
                    inc(g.dma_start(out=f1roll[:, m % 4, :, :], in_=src),
                        f"f1s{m % 4}", 16)
                    mark(f"f1ld{j}_{m}", f"f1s{m % 4}")

                @block.tensor
                def _(t, j=j, m=m):
                    if m == 0:
                        wait(t, "gm_f32")
                        wait(t, f"ztr{4*j+3}")
                        if j > 0:
                            wait(t, f"y2_{4*(j-1)+3}")
                    wait(t, f"f1ld{j}_{m}")
                    if m >= 2:
                        wait(t, f"gelu{j}_{m-2}")
                    elif j > 0:
                        wait(t, f"gelu{j-1}_{30+m}")
                    for i in range(8):
                        inc(t.matmul(f1ps[m % 2], f1roll[:, m % 4, i, :], zfm[:, i, :],
                                 start=(i == 0), stop=(i == 7)), "stt")
                    mark(f"f1mm{j}_{m}", "stt")
                    mark(f"f1use{j}_{m}", "stt")

                @block.scalar
                def _(s, j=j, m=m):
                    wait(s, f"f1mm{j}_{m}")
                    if j > 0:
                        wait(s, f"f2u{(j-1)*64+32+m}")
                    inc(s.activation(h_fm[:, m, :], f1ps[m % 2], AF.Gelu,
                                 bias=f1b_cols[:, m:m + 1], scale=1.0), "ss")
                    mark(f"gelu{j}_{m}", "ss")

            for sub in range(2):
                for m in range(32):
                    u = (j * 2 + sub) * 32 + m

                    @block.gpsimd
                    def _(g, u=u, m=m):
                        if u >= 4:
                            wait(g, f"f2u{u-4}")
                        inc(g.dma_start(out=f2roll[:, u % 4, :],
                                        in_=f2bf_dram.ap()[m, :, :]), f"f2s{u % 4}", 16)
                        mark(f"f2ld{u}", f"f2s{u % 4}")

                    @block.tensor
                    def _(t, j=j, sub=sub, m=m, u=u):
                        if m == 0:
                            wait(t, f"gelu{j}_31")
                            if sub == 0 and j > 0:
                                wait(t, f"y2_{4*(j-1)+3}")
                            if sub == 1:
                                wait(t, f"y2_{4*j+1}")
                        wait(t, f"f2ld{u}")
                        for ch2 in range(2):
                            for half in range(2):
                                inc(t.matmul(
                                    f2ps[ch2][:, half * 512:(half + 1) * 512],
                                    h_fm[:, m, (sub * 2 + ch2) * 128:(sub * 2 + ch2 + 1) * 128],
                                    f2roll[:, u % 4, half * 512:(half + 1) * 512],
                                    start=(m == 0), stop=False), "stt")
                        mark(f"f2u{u}", "stt")

                @block.tensor
                def _(t, j=j, sub=sub):
                    for ch2 in range(2):
                        for half in range(2):
                            inc(t.matmul(f2ps[ch2][:, half * 512:(half + 1) * 512],
                                     ones_row_bf[:],
                                     f2b_row[0:1, half * 512:(half + 1) * 512],
                                     start=False, stop=True), "stt")
                    mark(f"f2mm{j}_{sub}", "stt")

                for ch2 in range(2):
                    @block.vector
                    def _(v, j=j, sub=sub, ch2=ch2):
                        wait(v, f"f2mm{j}_{sub}")
                        c = 4 * j + sub * 2 + ch2
                        inc(v.scalar_tensor_tensor(y_t[:, c, :], f2ps[ch2][:], 1.0,
                                               y_t[:, c, :],
                                               op0=ALU.mult, op1=ALU.add), "sv")
                        mark(f"y2_{c}", "sv")

        if "tap_y" in tap_d and "E" not in stages:
            @block.gpsimd
            def _(g):
                wait(g, f"y2_{NCH-1}")
                for c in range(NCH):
                    inc(g.dma_start(out=tap_d["tap_y"][c * 128:(c + 1) * 128, :],
                                    in_=y_t[:, c, :]), "sg16", 16)
        p2p.close()

    if "E" in stages:
        p4p = ExitStack()
        wc = [p4p.enter_context(nc.psum_tensor(f"wc{i}", [128, 1024], F32))
              for i in range(2)]
        trp4_ps = p4p.enter_context(nc.psum_tensor("trp4_ps", [128, 4096], BF16))

        @block.scalar
        def _(s):
            for c in range(2):
                inc(s.dma_start(out=xs_st[:, c, :], in_=x_d[c * 128:(c + 1) * 128, :]),
                    f"sx{c % 2}", 16)
                mark(f"xld2_{c}", f"sx{c % 2}")

        for c in range(NCH):
            buf = c % 2

            @block.vector
            def _(v, c=c):
                wait(v, f"y2_{c}")

            stats_chain(y_t[:, c, 0:512], y_t[:, c, 512:1024], tag=f"e{c}")

            @block.scalar
            def _(s, c=c):
                wait(s, f"sbe{c}")
                s.wait_ge(sems["ss"], cnt["ss"])
                inc(s.activation(tmpA[:], y_t[:, c, :], AF.Identity,
                             bias=scol[:, 1:2], scale=scol[:, 0:1]), "ss")
                mark(f"z3s{c}", "ss")

            @block.vector
            def _(v, c=c, buf=buf):
                wait(v, f"z3s{c}")
                wait(v, f"xld2_{c}")
                inc(v.tensor_tensor(y_t[:, c, :], tmpA[:], xs_st[:, buf, :], op=ALU.add), "sv")
                mark(f"x2_{c}", "sv")

            if c + 2 < NCH:
                @block.scalar
                def _(s, c=c):
                    wait(s, f"x2_{c}")
                    inc(s.dma_start(out=xs_st[:, c % 2, :],
                                    in_=x_d[(c + 2) * 128:(c + 3) * 128, :]),
                        f"sx{c % 2}", 16)
                    mark(f"xld2_{c+2}", f"sx{c % 2}")

            if "F" not in stages:
                continue

            stats_chain(y_t[:, c, 0:512], y_t[:, c, 512:1024], tag=f"f{c}")

            @block.scalar
            def _(s, c=c):
                wait(s, f"sbf{c}")
                s.wait_ge(sems["ss"], cnt["ss"])
                inc(s.activation(tmpA[:], y_t[:, c, :], AF.Identity,
                             bias=scol[:, 1:2], scale=scol[:, 0:1]), "ss")
                mark(f"z4s{c}", "ss")

            @block.vector
            def _(v, c=c):
                wait(v, f"z4s{c}")
                if c >= 2:
                    wait(v, f"wtrt{c-2}")
                inc(v.tensor_copy(xnbf[:, c % 2, :], tmpA[:]), "sv")
                mark(f"z4v{c}", "sv")

            @block.tensor
            def _(t, c=c):
                wait(t, f"z4v{c}")
                if c > 0:
                    wait(t, f"wtr{c-1}")
                elif "D" in stages:
                    wait(t, "f1mm3_31")
                if c >= 4:
                    wait(t, f"wcmm{c-4}")
                for i in range(8):
                    col = (i % 4) * 1024 + (i // 4) * 128
                    ti = t.transpose(trp4_ps[:, col:col + 128],
                                     xnbf[:, c % 2, i * 128:(i + 1) * 128], identb[:])
                    if i == 3:
                        inc(ti, "stt")
                        t.wait_ge(sems["stt"], cnt["stt"])
                inc(t.nop(), "stt")
                mark(f"wtrt{c}", "stt")

            @block.vector
            def _(v, c=c):
                wait(v, f"wtrt{c}")
                for b in range(4):
                    cp = v.tensor_copy(
                        zfm[:, b:8:4, (c % 4) * 128:(c % 4 + 1) * 128],
                        trp4_ps[:, b * 1024:b * 1024 + 256]
                        .rearrange("p (a b) -> p a b", a=2))
                    if b == 3:
                        inc(cp, "sv")
                mark(f"wtr{c}", "sv")

            @block.tensor
            def _(t, c=c):
                wait(t, f"wtr{c}")
                if c == 0:
                    wait(t, "mc_loaded")
                if c > 1:
                    wait(t, f"wfin{c-2}")
                for i in range(8):
                    for half in range(2):
                        t.matmul(wc[c % 2][:, half * 512:(half + 1) * 512],
                                 zfm[:, i, (c % 4) * 128:(c % 4 + 1) * 128],
                                 mc_bf[:, i, half * 512:(half + 1) * 512],
                                 start=(i == 0), stop=False)
                for half in range(2):
                    inc(t.matmul(wc[c % 2][:, half * 512:(half + 1) * 512], ones_row_bf[:],
                             fbias_row[0:1, half * 512:(half + 1) * 512],
                             start=False, stop=True), "stt")
                mark(f"wcmm{c}", "stt")

            @block.vector
            def _(v, c=c):
                wait(v, f"wcmm{c}")
                v.bn_stats(bnst[:, 0, :], wc[c % 2][:, 0:512])
                inc(v.bn_stats(bnst[:, 1, :], wc[c % 2][:, 512:1024]), "sv")
                v.drain()
                v.wait_ge(sems["sv"], cnt["sv"])
                inc(v.bn_aggr(bnag[:], bnst[:]), "sv")
                v.drain()
                v.wait_ge(sems["sv"], cnt["sv"])
                inc(v.nop(), "sv")
                mark(f"wsa{c}", "sv")

            @block.scalar
            def _(s, c=c):
                wait(s, f"wsa{c}")
                inc(s.activation(scol[:, 4:5], bnag[:, 1:2], AF.Ln, bias=eps_col[:]), "ss")
                s.wait_ge(sems["ss"], cnt["ss"])
                inc(s.activation(scol[:, 0:1], scol[:, 4:5], AF.Exp, scale=-0.5), "ss")
                mark(f"wss{c}", "ss")

            @block.vector
            def _(v, c=c):
                wait(v, f"wss{c}")
                inc(v.tensor_scalar(scol[:, 1:2], bnag[:, 0:1], scol[:, 0:1], -1.0,
                                    op0=ALU.mult, op1=ALU.mult), "sv")
                mark(f"wsb{c}", "sv")

            @block.scalar
            def _(s, c=c):
                wait(s, f"wsb{c}")
                s.wait_ge(sems["ss"], cnt["ss"])
                inc(s.activation(tmpA[:], wc[c % 2][:], AF.Identity,
                             bias=scol[:, 1:2], scale=scol[:, 0:1]), "ss")
                mark(f"wact{c}", "ss")

            @block.vector
            def _(v, c=c):
                wait(v, f"wact{c}")
                if c >= 2:
                    wait(v, f"outst{c-2}")
                inc(v.tensor_tensor(out_sb[:, c % 2, :], tmpA[:], y_t[:, c, :], op=ALU.add), "sv")
                mark(f"wfin{c}", "sv")

            @block.scalar
            def _(s, c=c):
                wait(s, f"wfin{c}")
                inc(s.dma_start(out=out_d[c * 128:(c + 1) * 128, :],
                                in_=out_sb[:, c % 2, :]), f"so{c % 2}", 16)
                mark(f"outst{c}", f"so{c % 2}")

        if "tap_y" in tap_d and "F" not in stages:
            @block.gpsimd
            def _(g):
                wait(g, f"x2_{NCH-1}")
                for c in range(NCH):
                    inc(g.dma_start(out=tap_d["tap_y"][c * 128:(c + 1) * 128, :],
                                    in_=y_t[:, c, :]), "sg16", 16)

        if "F" in stages:
            @block.vector
            def _(v):
                v.wait_ge(sems["so0"], cnt["so0"])
                v.wait_ge(sems["so1"], cnt["so1"])
        p4p.close()

    @block.gpsimd
    def _(g):
        g.wait_ge(sems["sg16"], cnt["sg16"])

    st.close()
    return nc


def _prep_inmaps(inputs):
    f32 = np.float32
    x = np.asarray(inputs["L0_attn_x"], f32)
    qw = np.asarray(inputs["L0_attn_q_w"], f32)
    kw = np.asarray(inputs["L0_attn_k_w"], f32)
    vw = np.asarray(inputs["L0_attn_v_w"], f32)
    ow = np.asarray(inputs["L0_attn_o_w"], f32)
    f1w = np.asarray(inputs["L0_attn_f1_w"], f32)
    f2w = np.asarray(inputs["L0_attn_f2_w"], f32)
    qkw_t = np.ascontiguousarray(np.concatenate([qw, kw], axis=0).T)
    vw_t = np.ascontiguousarray(vw.T)
    ow_t = np.ascontiguousarray(ow.T)
    f1w_t = np.ascontiguousarray(f1w.T)
    f2w_t = np.ascontiguousarray(f2w.T)

    rows = np.zeros((9, 1024), f32)
    rows[0] = np.asarray(inputs["L0_attn_ln1_w"], f32)
    rows[1] = np.asarray(inputs["L0_attn_ln1_b"], f32)
    rows[2] = np.asarray(inputs["L0_attn_ln2_w"], f32)
    rows[3] = np.asarray(inputs["L0_attn_ln2_b"], f32)
    rows[4] = np.asarray(inputs["L0_attn_o_b"], f32)
    rows[5] = np.asarray(inputs["L0_attn_v_b"], f32)
    rows[6] = np.asarray(inputs["L0_attn_f2_b"], f32)
    rows[7] = np.concatenate([np.asarray(inputs["L0_ffn_bias_real"], f32),
                              np.asarray(inputs["L0_ffn_bias_imag"], f32)])
    rows[8, 0:128] = np.concatenate([np.asarray(inputs["L0_attn_q_b"], f32),
                                     np.asarray(inputs["L0_attn_k_b"], f32)])

    f1b_cols = np.ascontiguousarray(
        np.asarray(inputs["L0_attn_f1_b"], f32).reshape(32, 128).T)

    G1R = np.asarray(inputs["L0_ffn_G1_R"], f32)
    G2R = np.asarray(inputs["L0_ffn_G2_R"], f32)
    G1I = np.asarray(inputs["L0_ffn_G1_I"], f32)
    G2I = np.asarray(inputs["L0_ffn_G2_I"], f32)
    g1m = np.ascontiguousarray(np.concatenate(
        [G1R[0].transpose(1, 0, 2).reshape(4, 1024),
         G1I[0].transpose(1, 0, 2).reshape(4, 1024)], axis=1))
    g2m = np.ascontiguousarray(np.concatenate(
        [G2R[:, :, 0, :].transpose(0, 2, 1).reshape(4, 256),
         G2I[:, :, 0, :].transpose(0, 2, 1).reshape(4, 256)], axis=1))

    in_maps = []
    for core in range(NCORES):
        b, q = core // QPER, core % QPER
        w_pre = np.zeros((128, 8), f32)
        for cc in range(NCORES):
            if cc // QPER == b and cc % QPER < q:
                w_pre[:, cc] = 1.0
        in_maps.append({
            "x": np.ascontiguousarray(x[b, q * T:(q + 1) * T]),
            "qkw_t": qkw_t, "vw_t": vw_t, "ow_t": ow_t,
            "f1w_t": f1w_t, "f2w_t": f2w_t,
            "g1m": g1m, "g2m": g2m, "rows": rows, "f1b_cols": f1b_cols,
            "w_pre": w_pre,
        })
    return in_maps


_NC_CACHE = {}


def kernel(**inputs):
    key = ("ABCDEF", ())
    if key not in _NC_CACHE:
        _NC_CACHE[key] = build_nc(taps=(), stages="ABCDEF")
    nc = _NC_CACHE[key]
    in_maps = _prep_inmaps(inputs)
    res = run_bass_kernel_spmd(nc, in_maps, core_ids=list(range(NCORES)))
    out = np.zeros((B, N, D), np.float32)
    for core in range(NCORES):
        b, q = core // QPER, core % QPER
        out[b, q * T:(q + 1) * T] = res.results[core]["out"]
    return out


if __name__ == "__main__":
    nc = build_nc()
    print("build ok")


# version 04
# speedup vs baseline: 1.0129x; 1.0129x over previous
import numpy as np
from contextlib import ExitStack

import concourse.bass as bass
import concourse.mybir as mybir
from concourse.bass_utils import run_bass_kernel_spmd

F32 = mybir.dt.float32
BF16 = mybir.dt.bfloat16
I32 = mybir.dt.int32
AF = mybir.ActivationFunctionType
ALU = mybir.AluOpType
AX = mybir.AxisListType

LN_EPS = 1e-5
ATTN_EPS = 1e-6

B, N, D, FD = 2, 8192, 1024, 64
NCORES = 8
QPER = 4
T = N // QPER
NCH = T // 128
DH = D // 128
MH = (4 * D) // 128


def build_nc(taps=(), stages="ABCDEF"):
    nc = bass.Bass("TRN2", target_bir_lowering=False,
                   detect_race_conditions=False)

    x_d = nc.dram_tensor("x", [T, D], F32, kind="ExternalInput").ap()
    qkw_d = nc.dram_tensor("qkw_t", [D, 128], F32, kind="ExternalInput").ap()
    vw_d = nc.dram_tensor("vw_t", [D, D], F32, kind="ExternalInput").ap()
    ow_d = nc.dram_tensor("ow_t", [D, D], F32, kind="ExternalInput").ap()
    f1w_d = nc.dram_tensor("f1w_t", [D, 4 * D], F32, kind="ExternalInput").ap()
    f2w_d = nc.dram_tensor("f2w_t", [4 * D, D], F32, kind="ExternalInput").ap()
    g1m_d = nc.dram_tensor("g1m", [4, 2048], F32, kind="ExternalInput").ap()
    g2m_d = nc.dram_tensor("g2m", [4, 512], F32, kind="ExternalInput").ap()
    rows_d = nc.dram_tensor("rows", [9, 1024], F32, kind="ExternalInput").ap()
    f1bc_d = nc.dram_tensor("f1b_cols", [128, 32], F32, kind="ExternalInput").ap()
    wpre_d = nc.dram_tensor("w_pre", [128, 8], F32, kind="ExternalInput").ap()
    out_d = nc.dram_tensor("out", [T, D], F32, kind="ExternalOutput").ap()

    tap_d = {}
    for tname, shape, dt in [("tap_y", [T, D], F32), ("tap_pay", [128, 576], F32),
                             ("tap_xnfm", [128, 8, 2048], BF16),
                             ("tap_kvpre", [64, 1024], BF16),
                             ("tap_qke", [128, NCH * 128], BF16),
                             ("tap_mc", [128, 8, 1024], BF16)]:
        if tname in taps:
            tap_d[tname] = nc.dram_tensor(tname, shape, dt, kind="ExternalOutput").ap()

    pay_dram = nc.dram_tensor("pay_dram", [128, 576], F32, kind="Internal")
    ag_dram = nc.dram_tensor("ag_dram", [8, 128, 576], F32, kind="Internal")
    mc_dram = nc.dram_tensor("mc_dram", [1024, 1024], BF16, kind="Internal")
    f1bf_dram = nc.dram_tensor("f1bf_dram", [8, 128, 4096], BF16, kind="Internal")
    f2bf_dram = nc.dram_tensor("f2bf_dram", [32, 128, 1024], BF16, kind="Internal")

    st = ExitStack()
    e = st.enter_context

    y_t = e(nc.sbuf_tensor("y_t", [128, NCH, 1024], F32))
    qke_t = e(nc.sbuf_tensor("qke_t", [128, NCH, 128], BF16))
    qf32_t = e(nc.sbuf_tensor("qf32_t", [128, NCH, 64], F32))
    g1rep = e(nc.sbuf_tensor("g1rep", [128, 1024], BF16))
    b1rep = e(nc.sbuf_tensor("b1rep", [128, 1024], BF16))
    g2rep = e(nc.sbuf_tensor("g2rep", [128, 1024], BF16))
    b2rep = e(nc.sbuf_tensor("b2rep", [128, 1024], BF16))
    ob_rep = e(nc.sbuf_tensor("ob_rep", [128, 1024], BF16))
    f1b_cols = e(nc.sbuf_tensor("f1b_colsb", [128, 32], F32))
    mask_bf = e(nc.sbuf_tensor("mask_bf", [128, 128], BF16))
    identb = e(nc.sbuf_tensor("identb", [128, 128], BF16))
    ones_row_bf = e(nc.sbuf_tensor("ones_row_bf", [1, 128], BF16))
    ones_col_bf = e(nc.sbuf_tensor("ones_col_bf", [128, 1], BF16))
    ones_row_f32 = e(nc.sbuf_tensor("ones_row_f32", [1, 128], F32))
    qkb_row = e(nc.sbuf_tensor("qkb_row", [1, 128], BF16))
    vb_row = e(nc.sbuf_tensor("vb_row", [1, 1024], BF16))
    f2b_row = e(nc.sbuf_tensor("f2b_row", [1, 1024], BF16))
    fbias_row = e(nc.sbuf_tensor("fbias_row", [1, 1024], BF16))
    wpre_sb = e(nc.sbuf_tensor("wpre_sb", [128, 8], F32))
    kv_sb = e(nc.sbuf_tensor("kv_sb", [64, 1024], BF16))
    kvacc = e(nc.sbuf_tensor("kvacc", [64, 1024], F32))
    ks_run = e(nc.sbuf_tensor("ks_run", [1, 64], F32))
    ks_rep = e(nc.sbuf_tensor("ks_rep", [128, 64], F32))
    q0pre_bf = e(nc.sbuf_tensor("q0pre_bf", [1, 64], BF16))
    bnst = e(nc.sbuf_tensor("bnst", [128, 2, 6], F32))
    bnag = e(nc.sbuf_tensor("bnag", [128, 2], F32))
    scol = e(nc.sbuf_tensor("scol", [128, 8], F32))
    eps_col = e(nc.sbuf_tensor("eps_col", [128, 1], F32))
    den_f = e(nc.sbuf_tensor("den_f", [128, 1], F32))
    rden = e(nc.sbuf_tensor("rden", [128, 1], F32))
    den2 = e(nc.sbuf_tensor("den2", [128, 1], F32))
    tmpA = e(nc.sbuf_tensor("tmpA", [128, 1024], F32))
    tmpB = e(nc.sbuf_tensor("tmpB", [128, 1024], F32))
    xs_st = e(nc.sbuf_tensor("xs_st", [128, 2, 1024], F32))
    wstage = e(nc.sbuf_tensor("wstage", [128, 2048], F32))
    xnbf = e(nc.sbuf_tensor("xnbf", [128, 2, 1024], BF16))

    p1s = ExitStack()
    iota_t = p1s.enter_context(nc.sbuf_tensor("iota_t", [128, 128], I32))
    qzk = p1s.enter_context(nc.sbuf_tensor("qzk", [128, 128], BF16))
    am_bf = p1s.enter_context(nc.sbuf_tensor("am_bf", [128, 128], BF16))
    vtok = p1s.enter_context(nc.sbuf_tensor("vtok", [128, 2, 1024], BF16))
    num_fm = p1s.enter_context(nc.sbuf_tensor("num_fm", [128, 8, 128], BF16))
    qfm_sb = p1s.enter_context(nc.sbuf_tensor("qfm_sb", [64, 128], BF16))
    kfm_sb = p1s.enter_context(nc.sbuf_tensor("kfm_sb", [64, 128], BF16))
    xn_fm = p1s.enter_context(nc.sbuf_tensor("xn_fm", [128, 8, 2048], BF16))
    vw_bf = p1s.enter_context(nc.sbuf_tensor("vw_bf", [128, 8, 1024], BF16))
    ow_bf = p1s.enter_context(nc.sbuf_tensor("ow_bf", [128, 8, 1024], BF16))
    qkw_bf = p1s.enter_context(nc.sbuf_tensor("qkw_bf", [128, 8, 128], BF16))
    ttx = p1s.enter_context(nc.sbuf_tensor("ttx", [128, 256], BF16))
    gm_sb = p1s.enter_context(nc.sbuf_tensor("gm_sb", [4, 2560], BF16))
    ptpre = p1s.enter_context(nc.sbuf_tensor("ptpre", [128, 8, 64], F32))
    ptpre_bf = p1s.enter_context(nc.sbuf_tensor("ptpre_bf", [128, 8, 64], BF16))
    p1s.close()

    mc_bf = e(nc.sbuf_tensor("mc_bf", [128, 8, 1024], BF16))
    zfm = e(nc.sbuf_tensor("zfm", [128, 8, 512], BF16))
    h_fm = e(nc.sbuf_tensor("h_fm", [128, 32, 512], BF16))
    f1roll = e(nc.sbuf_tensor("f1roll", [128, 3, 8, 256], BF16))
    f2roll = e(nc.sbuf_tensor("f2roll", [128, 3, 2, 1024], BF16))
    out_sb = wstage

    pAp = ExitStack()
    repvp_ps = pAp.enter_context(nc.psum_tensor("repvp_ps", [128, 1024], F32))
    pt_ps = pAp.enter_context(nc.psum_tensor("pt_ps", [128, 1024], F32))
    trA_ps = pAp.enter_context(nc.psum_tensor("trA_ps", [128, 4096], BF16))
    vp_ps = repvp_ps

    semnames = ["sx0", "sx1", "sg16", "sgm", "smc", "str_", "so0", "so1", "sv", "ss", "stt", "sgc", "scc", "f1s0", "f1s1", "f1s2", "f1s3", "f2s0", "f2s1", "f2s2", "f2s3"]
    sems = {n: e(nc.semaphore(n)) for n in semnames}
    cnt = {n: 0 for n in semnames}
    land = {}

    def inc(instr, s, amt=1):
        cnt[s] += amt
        instr.then_inc(sems[s], amt)
        return cnt[s]

    def mark(key, s):
        land[key] = (s, cnt[s])

    agstage = tmpA[:, 0:576]
    payload = tmpB[:, 0:576]
    row_st = wstage[0:1, 1024:2048]

    def wait(eng, key):
        s, v = land[key]
        eng.wait_ge(sems[s], v)

    block = e(nc.Block())

    @block.gpsimd
    def _(g):
        inc(g.iota(iota_t[:], pattern=[[1, 128]], base=0, channel_multiplier=-1), "sgc")
        inc(g.memset(qzk[:], 0.0), "sgc")
        inc(g.memset(ones_row_bf[:], 1.0), "sgc")
        inc(g.memset(ones_col_bf[:], 1.0), "sgc")
        inc(g.memset(ones_row_f32[:], 1.0), "sgc")
        inc(g.memset(ks_run[:], 0.0), "sgc")
        inc(g.memset(eps_col[:], LN_EPS), "sgc")
        mark("iota", "sgc")
        inc(g.dma_start(out=f1b_cols[:], in_=f1bc_d), "sgm", 16)
        inc(g.dma_start(out=wpre_sb[:], in_=wpre_d), "sgm", 16)
        inc(g.dma_start(out=tmpA[0:4, :], in_=g1m_d[:, 0:1024]), "sgm", 16)
        inc(g.dma_start(out=tmpB[0:4, :], in_=g1m_d[:, 1024:2048]), "sgm", 16)
        inc(g.dma_start(out=xs_st[0:4, 0, 0:512], in_=g2m_d), "sgm", 16)
        mark("gm_f32", "sgm")
        inc(g.dma_start(out=wstage[:, 0:1024].rearrange("p (c m) -> p c m", c=8),
                        in_=qkw_d.rearrange("(c k) m -> k c m", k=128)), "sg16", 16)
        mark("qkw_f32", "sg16")
        g.wait_ge(sems["sg16"], cnt["sg16"])
        inc(g.tensor_copy(qkw_bf[:], wstage[:, 0:1024]
                          .rearrange("p (c m) -> p c m", c=8)), "sgc")
        g.wait_ge(sems["sgc"], cnt["sgc"])
        inc(g.nop(), "sgc")
        mark("qkw_bf", "sgc")

    @block.scalar
    def _(s):
        wait(s, "iota")

    @block.vector
    def _(v):
        wait(v, "iota")
        v.tensor_scalar(mask_bf[:], iota_t[:], 0, None, op0=ALU.is_ge)
        inc(v.tensor_scalar(identb[:], iota_t[:], 0, None, op0=ALU.is_equal), "sv")
        mark("masks", "sv")
        wait(v, "gm_f32")
        v.tensor_copy(gm_sb[0:4, 0:1024], tmpA[0:4, :])
        v.tensor_copy(gm_sb[0:4, 1024:2048], tmpB[0:4, :])
        inc(v.tensor_copy(gm_sb[0:4, 2048:2560], xs_st[0:4, 0, 0:512]), "sv")
        mark("gm_bf", "sv")

    rep_specs = [(0, g1rep), (1, b1rep), (2, g2rep), (3, b2rep), (4, ob_rep)]
    for ri, (rowi, dst) in enumerate(rep_specs):
        @block.gpsimd
        def _(g, ri=ri, rowi=rowi):
            if ri > 0:
                wait(g, f"rep{ri-1}_t")
            inc(g.dma_start(out=row_st[:], in_=rows_d[rowi:rowi + 1, :]), "sg16", 16)
            mark(f"rep{ri}_row", "sg16")

        @block.tensor
        def _(t, ri=ri):
            wait(t, f"rep{ri}_row")
            if ri == 0:
                wait(t, "iota")
            if ri > 0:
                wait(t, f"rep{ri-1}_v")
            for half in range(2):
                inc(t.matmul(vp_ps[:, half * 512:(half + 1) * 512], ones_row_f32[:],
                         row_st[0:1, half * 512:(half + 1) * 512], start=True, stop=True), "stt")
            mark(f"rep{ri}_t", "stt")

        @block.vector
        def _(v, ri=ri, dst=dst):
            wait(v, f"rep{ri}_t")
            inc(v.tensor_copy(dst[:], vp_ps[:]), "sv")
            mark(f"rep{ri}_v", "sv")

    bfrow_specs = [(8, qkb_row, 128), (5, vb_row, 1024), (6, f2b_row, 1024),
                   (7, fbias_row, 1024)]
    for bi, (rowi, dst, w) in enumerate(bfrow_specs):
        @block.gpsimd
        def _(g, bi=bi, rowi=rowi):
            wait(g, "rep4_v" if bi == 0 else f"brow{bi-1}_v")
            inc(g.dma_start(out=row_st[:], in_=rows_d[rowi:rowi + 1, :]), "sg16", 16)
            mark(f"brow{bi}", "sg16")

        @block.vector
        def _(v, bi=bi, dst=dst, w=w):
            wait(v, f"brow{bi}")
            inc(v.tensor_copy(dst[:], row_st[0:1, 0:w]), "sv")
            mark(f"brow{bi}_v", "sv")
    ROWS_DONE = f"brow{len(bfrow_specs)-1}_v"

    mc_view = mc_dram.ap().rearrange("(rh d g) (ch b e) -> rh ch b d g e",
                                     rh=2, d=32, g=16, ch=2, b=32, e=16)

    def tt_chain(imag):
        g1o = 1024 if imag else 0
        g2o = 2304 if imag else 2048
        tag = "i" if imag else "r"
        for p in range(8):
            @block.tensor
            def _(t, p=p):
                if p == 0 and imag:
                    wait(t, "gm_bf")
                    wait(t, "rep4_v")
                if p == 0 and not imag:
                    wait(t, "ttevi7")
                if p > 0:
                    wait(t, f"ttev{tag}{p-1}")
                inc(t.matmul(vp_ps[:, 0:256], gm_sb[0:4, g1o + p * 128:g1o + (p + 1) * 128],
                         gm_sb[0:4, g2o:g2o + 256], start=True, stop=True), "stt")
                mark(f"ttmm{tag}{p}", "stt")

            @block.vector
            def _(v, p=p):
                wait(v, f"ttmm{tag}{p}")
                if p > 0:
                    wait(v, f"ttst{tag}{p-1}")
                elif not imag:
                    wait(v, "ttsti7")
                inc(v.tensor_copy(ttx[:], vp_ps[:, 0:256]), "sv")
                mark(f"ttev{tag}{p}", "sv")

            if imag:
                @block.gpsimd
                def _(g, p=p):
                    wait(g, f"ttev{tag}{p}")
                    for bq in range(4):
                        src_ = ttx[bq * 32:(bq + 1) * 32, :].rearrange(
                            "d (g e) -> d g e", g=16)
                        inc(g.dma_start(out=mc_view[0, 1, 4 * p + bq, :, :, :],
                                        in_=src_), "sg16", 16)
                    mark(f"ttstwi{p}", "sg16")

                @block.scalar
                def _(s, p=p):
                    wait(s, f"ttstwi{p}")
                    inc(s.mul(ttx[:], ttx[:], -1.0), "ss")
                    mark(f"ttneg{p}", "ss")

                @block.gpsimd
                def _(g, p=p):
                    wait(g, f"ttneg{p}")
                    for bq in range(4):
                        src_ = ttx[bq * 32:(bq + 1) * 32, :].rearrange(
                            "d (g e) -> d g e", g=16)
                        inc(g.dma_start(out=mc_view[1, 0, 4 * p + bq, :, :, :],
                                        in_=src_), "sg16", 16)
                    mark(f"ttst{tag}{p}", "sg16")
            else:
                @block.gpsimd
                def _(g, p=p):
                    wait(g, f"ttev{tag}{p}")
                    for bq in range(4):
                        src_ = ttx[bq * 32:(bq + 1) * 32, :].rearrange(
                            "d (g e) -> d g e", g=16)
                        inc(g.dma_start(out=mc_view[0, 0, 4 * p + bq, :, :, :],
                                        in_=src_), "sg16", 16)
                        inc(g.dma_start(out=mc_view[1, 1, 4 * p + bq, :, :, :],
                                        in_=src_), "sg16", 16)
                    mark(f"ttst{tag}{p}", "sg16")

    tt_chain(imag=True)
    tt_chain(imag=False)

    @block.gpsimd
    def _(g):
        mark("st_wr", "sg16")
        for nm, src_d, dst in [("vw", vw_d, vw_bf), ("ow", ow_d, ow_bf)]:
            for i0 in range(0, 8, 2):
                g.wait_ge(sems["sgc"], cnt["sgc"])
                inc(g.dma_start(out=wstage[:].rearrange("p (c m) -> p c m", c=2),
                                in_=src_d[i0 * 128:(i0 + 2) * 128, :]
                                .rearrange("(c k) m -> k c m", k=128)), "sg16", 16)
                g.wait_ge(sems["sg16"], cnt["sg16"])
                inc(g.tensor_copy(dst[:, i0:i0 + 2, :],
                                  wstage[:].rearrange("p (c m) -> p c m", c=2)), "sgc")
        g.wait_ge(sems["sgc"], cnt["sgc"])
        inc(g.nop(), "sgc")
        mark("vwow_bf", "sgc")

    def stats_chain(src_hi, src_lo, tag, s2=False):
        @block.vector
        def _(v):
            v.bn_stats(bnst[:, 0, :], src_hi)
            inc(v.bn_stats(bnst[:, 1, :], src_lo), "sv")
            v.drain()
            v.wait_ge(sems["sv"], cnt["sv"])
            inc(v.bn_aggr(bnag[:], bnst[:]), "sv")
            v.drain()
            v.wait_ge(sems["sv"], cnt["sv"])
            inc(v.nop(), "sv")
            mark(f"sa{tag}", "sv")

        @block.scalar
        def _(s):
            wait(s, f"sa{tag}")
            inc(s.activation(scol[:, 4:5], bnag[:, 1:2], AF.Ln, bias=eps_col[:]), "ss")
            s.wait_ge(sems["ss"], cnt["ss"])
            inc(s.activation(scol[:, 0:1], scol[:, 4:5], AF.Exp, scale=-0.5), "ss")
            mark(f"ss1{tag}", "ss")

        @block.vector
        def _(v):
            wait(v, f"ss1{tag}")
            if s2:
                v.scalar_tensor_tensor(scol[:, 6:7], bnag[:, 1:2], scol[:, 0:1],
                                       scol[:, 0:1], op0=ALU.mult, op1=ALU.mult)
            inc(v.tensor_scalar(scol[:, 1:2], bnag[:, 0:1], scol[:, 0:1], -1.0,
                                op0=ALU.mult, op1=ALU.mult), "sv")
            mark(f"sb{tag}", "sv")

        if s2:
            @block.scalar
            def _(s):
                wait(s, f"sb{tag}")
                inc(s.activation(scol[:, 7:8], scol[:, 6:7], AF.Ln, bias=eps_col[:]), "ss")
                s.wait_ge(sems["ss"], cnt["ss"])
                inc(s.activation(scol[:, 2:3], scol[:, 7:8], AF.Exp, scale=-0.5), "ss")
                mark(f"ss2{tag}", "ss")

    if "A" in stages:
        @block.scalar
        def _(s):
            for c in range(2):
                inc(s.dma_start(out=xs_st[:, c, :], in_=x_d[c * 128:(c + 1) * 128, :]),
                    f"sx{c % 2}", 16)
                mark(f"xld{c}", f"sx{c % 2}")

        for c in range(NCH):
            buf = c % 2

            @block.vector
            def _(v, c=c):
                wait(v, f"xld{c}")
                if c == 0:
                    wait(v, "rep4_v")
                    wait(v, "gm_bf")

            stats_chain(xs_st[:, buf, 0:512], xs_st[:, buf, 512:1024],
                        tag=f"a{c}", s2=True)

            @block.scalar
            def _(s, c=c, buf=buf):
                wait(s, f"sb" + f"a{c}")
                s.wait_ge(sems["ss"], cnt["ss"])
                if c > 0:
                    wait(s, f"eluv{c-1}")
                inc(s.activation(tmpA[:], xs_st[:, buf, :], AF.Identity,
                                 bias=scol[:, 1:2], scale=scol[:, 0:1]), "ss")
                mark(f"y0{c}", "ss")
                s.wait_ge(sems["ss"], cnt["ss"])
                if c + 2 < NCH:
                    inc(s.dma_start(out=xs_st[:, c % 2, :],
                                    in_=x_d[(c + 2) * 128:(c + 3) * 128, :]),
                        f"sx{c % 2}", 16)
                    mark(f"xld{c+2}", f"sx{c % 2}")

            @block.vector
            def _(v, c=c):
                wait(v, f"y0{c}")
                v.tensor_tensor(y_t[:, c, :], tmpA[:], ob_rep[:], op=ALU.add)
                v.scalar_tensor_tensor(tmpB[:], tmpA[:], scol[:, 2:3], g1rep[:],
                                       op0=ALU.mult, op1=ALU.mult)
                v.drain()
                if c >= 2:
                    wait(v, f"xtrt{c-2}")
                    wait(v, f"pt{c-2}")
                inc(v.tensor_tensor(xnbf[:, c % 2, :], tmpB[:], b1rep[:], op=ALU.add), "sv")
                mark(f"xnv{c}", "sv")

            @block.tensor
            def _(t, c=c):
                wait(t, f"xnv{c}")
                if c == 0:
                    wait(t, "masks")
                if c > 0:
                    wait(t, f"xtr{c-1}")
                for i in range(8):
                    col = (i % 4) * 1024 + (i // 4) * 128
                    ti = t.transpose(trA_ps[:, col:col + 128],
                                     xnbf[:, c % 2, i * 128:(i + 1) * 128], identb[:])
                    if i == 3:
                        inc(ti, "stt")
                        t.wait_ge(sems["stt"], cnt["stt"])
                inc(t.nop(), "stt")
                mark(f"xtrt{c}", "stt")

            @block.vector
            def _(v, c=c):
                wait(v, f"xtrt{c}")
                for b in range(4):
                    cp = v.tensor_copy(
                        xn_fm[:, b:8:4, c * 128:(c + 1) * 128],
                        trA_ps[:, b * 1024:b * 1024 + 256]
                        .rearrange("p (a b) -> p a b", a=2))
                    if b == 3:
                        inc(cp, "sv")
                mark(f"xtr{c}", "sv")

            @block.tensor
            def _(t, c=c):
                wait(t, f"xtr{c}")
                if c == 0:
                    wait(t, "qkw_bf")
                    wait(t, "brow0_v")
                    wait(t, "iota")
                if c == 0:
                    wait(t, "ttevr7")
                if c > 0:
                    wait(t, f"eluv{c-1}")
                for i in range(8):
                    t.matmul(repvp_ps[:, 0:128], xn_fm[:, i, c * 128:(c + 1) * 128],
                             qkw_bf[:, i, :], start=(i == 0), stop=False)
                inc(t.matmul(repvp_ps[:, 0:128], ones_row_bf[:], qkb_row[:],
                         start=False, stop=True), "stt")
                mark(f"qkmm{c}", "stt")

            @block.vector
            def _(v, c=c):
                wait(v, f"qkmm{c}")
                inc(v.tensor_scalar(tmpA[:, 0:128], repvp_ps[:, 0:128], 0.0, None, op0=ALU.min), "sv")
                mark(f"elumin{c}", "sv")

            @block.scalar
            def _(s, c=c):
                wait(s, f"elumin{c}")
                inc(s.activation(tmpA[:, 128:256], tmpA[:, 0:128], AF.Exp), "ss")
                mark(f"eluexp{c}", "ss")

            @block.vector
            def _(v, c=c):
                wait(v, f"eluexp{c}")
                v.scalar_tensor_tensor(tmpA[:, 256:384], repvp_ps[:, 0:128], 0.0,
                                       tmpA[:, 128:256], op0=ALU.max, op1=ALU.add)
                v.drain()
                v.tensor_copy(qke_t[:, c, :], tmpA[:, 256:384])
                inc(v.tensor_copy(qf32_t[:, c, :], tmpA[:, 256:320]), "sv")
                mark(f"eluv{c}", "sv")

            @block.tensor
            def _(t, c=c):
                wait(t, f"eluv{c}")
                for i in range(8):
                    t.matmul(pt_ps[:, i * 64:(i + 1) * 64],
                             xnbf[:, c % 2, i * 128:(i + 1) * 128],
                             qke_t[:, c, 64:128],
                             start=(c == 0 and i == 0),
                             stop=(c == NCH - 1 and i == 7))
                inc(t.matmul(pt_ps[0:1, 512:576], ones_col_bf[:], qke_t[:, c, 64:128],
                         start=(c == 0), stop=(c == NCH - 1)), "stt")
                mark(f"pt{c}", "stt")

        @block.vector
        def _(v):
            wait(v, f"pt{NCH-1}")
            v.tensor_copy(payload[:, 0:512], pt_ps[:, 0:512])
            inc(v.tensor_copy(payload[0:1, 512:576], pt_ps[0:1, 512:576]), "sv")
            mark("payv", "sv")

        if "tap_xnfm" in tap_d:
            @block.gpsimd
            def _(g):
                wait(g, f"pt{NCH-1}")
                inc(g.dma_start(out=tap_d["tap_xnfm"], in_=xn_fm[:]), "sg16", 16)

        if "tap_pay" in tap_d:
            @block.gpsimd
            def _(g):
                wait(g, "payv")
                inc(g.dma_start(out=tap_d["tap_pay"], in_=payload[:]), "sg16", 16)
        if "tap_qke" in tap_d:
            @block.gpsimd
            def _(g):
                wait(g, f"eluv{NCH-1}")
                inc(g.dma_start(out=tap_d["tap_qke"]
                                .rearrange("p (a b) -> p a b", b=128),
                                in_=qke_t[:]), "sg16", 16)
        if "tap_y" in tap_d and "C" not in stages:
            @block.gpsimd
            def _(g):
                wait(g, f"xnv{NCH-1}")
                for c in range(NCH):
                    inc(g.dma_start(out=tap_d["tap_y"][c * 128:(c + 1) * 128, :],
                                    in_=y_t[:, c, :]), "sg16", 16)

    if "B" in stages or "C" in stages:
        pAp.close()
        p1p = ExitStack()
        kv_ps = p1p.enter_context(nc.psum_tensor("kv_ps", [64, 1024], F32))
        vp_ps = p1p.enter_context(nc.psum_tensor("vp_ps", [128, 1024], F32))
        no_ps = p1p.enter_context(nc.psum_tensor("no_ps", [128, 1024], F32))
        m1_ps = p1p.enter_context(nc.psum_tensor("m1_ps", [128, 512], F32))
        trB_ps = p1p.enter_context(nc.psum_tensor("trB_ps", [128, 1024], BF16))

    if "B" in stages:
        @block.gpsimd
        def _(g):
            wait(g, "payv")
            inc(g.dma_start(out=pay_dram.ap(), in_=payload[:]), "sg16", 16)
            mark("paydram", "sg16")
            wait(g, "paydram")
            inc(g.collective_compute(
                "AllGather", ALU.bypass, replica_groups=[list(range(NCORES))],
                ins=[pay_dram.ap().opt()], outs=[ag_dram.ap().opt()]), "scc")
            mark("ag", "scc")
            wait(g, "ag")

        for cc in range(NCORES):
            @block.gpsimd
            def _(g, cc=cc):
                if cc > 0:
                    wait(g, f"agacc{cc-1}")
                inc(g.dma_start(out=agstage[:], in_=ag_dram.ap()[cc, :, :]), "sg16", 16)
                mark(f"agld{cc}", "sg16")

            @block.vector
            def _(v, cc=cc):
                wait(v, f"agld{cc}")
                if cc == 0:
                    wait(v, "gm_f32")
                pt_flat = ptpre[:].rearrange("p a b -> p (a b)")
                if cc == 0:
                    v.tensor_scalar(pt_flat, agstage[:, 0:512], wpre_sb[:, 0:1],
                                    None, op0=ALU.mult)
                    inc(v.tensor_scalar(ks_run[:], agstage[0:1, 512:576],
                                        wpre_sb[0:1, 0:1], None, op0=ALU.mult), "sv")
                else:
                    v.scalar_tensor_tensor(pt_flat, agstage[:, 0:512],
                                           wpre_sb[:, cc:cc + 1], pt_flat,
                                           op0=ALU.mult, op1=ALU.add)
                    inc(v.scalar_tensor_tensor(ks_run[:], agstage[0:1, 512:576],
                                           wpre_sb[0:1, cc:cc + 1], ks_run[:],
                                           op0=ALU.mult, op1=ALU.add), "sv")
                mark(f"agacc{cc}", "sv")

        @block.vector
        def _(v):
            wait(v, f"agacc{NCORES-1}")
            v.drain()
            v.tensor_copy(ptpre_bf[:], ptpre[:])
            inc(v.tensor_copy(q0pre_bf[:], ks_run[:]), "sv")
            mark("ptbf", "sv")

        @block.tensor
        def _(t):
            wait(t, "ptbf")
            wait(t, "vwow_bf")
            for i in range(8):
                for half in range(2):
                    t.matmul(kv_ps[:, half * 512:(half + 1) * 512], ptpre_bf[:, i, :],
                             vw_bf[:, i, half * 512:(half + 1) * 512],
                             start=(i == 0), stop=False)
            for half in range(2):
                inc(t.matmul(kv_ps[:, half * 512:(half + 1) * 512], q0pre_bf[:],
                         vb_row[0:1, half * 512:(half + 1) * 512],
                         start=False, stop=True), "stt")
            mark("kvfold", "stt")

        @block.vector
        def _(v):
            wait(v, "kvfold")
            v.tensor_copy(kvacc[:], kv_ps[:])
            inc(v.tensor_copy(kv_sb[:], kv_ps[:]), "sv")
            mark("kvsb_-1", "sv")

        @block.tensor
        def _(t):
            wait(t, "ptbf")
            inc(t.matmul(m1_ps[:, 256:320], ones_row_f32[:], ks_run[:], start=True, stop=True), "stt")
            mark("ksrept_-1", "stt")

        @block.vector
        def _(v):
            wait(v, "ksrept_-1")
            inc(v.tensor_copy(ks_rep[:], m1_ps[:, 256:320]), "sv")
            mark("ksrepv_-1", "sv")

        if "tap_kvpre" in tap_d:
            @block.gpsimd
            def _(g):
                wait(g, "kvsb_-1")
                inc(g.dma_start(out=tap_d["tap_kvpre"], in_=kv_sb[:]), "sg16", 16)

    if "C" in stages:
        for c in range(NCH):
            buf = c % 2

            @block.tensor
            def _(t, c=c, buf=buf):
                if c == 0:
                    wait(t, f"xtr{NCH-1}")
                if c > 0:
                    wait(t, f"trevac{c-1}")
                inc(t.transpose(trB_ps[:, 0:128], qke_t[:, c, :], identb[:]), "stt")
                mark(f"qtr{c}", "stt")
                if c == 0:
                    wait(t, "vwow_bf")
                    wait(t, ROWS_DONE)
                    wait(t, "ttevr7")
                    wait(t, f"eluv{NCH-1}")
                else:
                    wait(t, f"vtok_s{c-1}")
                for half in range(2):
                    for i in range(8):
                        t.matmul(vp_ps[:, half * 512:(half + 1) * 512],
                                 xn_fm[:, i, c * 128:(c + 1) * 128],
                                 vw_bf[:, i, half * 512:(half + 1) * 512],
                                 start=(i == 0), stop=False)
                    inc(t.matmul(vp_ps[:, half * 512:(half + 1) * 512], ones_row_bf[:],
                             vb_row[0:1, half * 512:(half + 1) * 512],
                             start=False, stop=True), "stt")
                mark(f"vp{c}", "stt")

            @block.scalar
            def _(s, c=c, buf=buf):
                wait(s, f"vp{c}")
                inc(s.activation(vtok[:, buf, :], vp_ps[:], AF.Identity), "ss")
                mark(f"vtok_s{c}", "ss")

            @block.vector
            def _(v, c=c):
                if c > 0:
                    wait(v, f"tr{c-1}")
                inc(v.tensor_copy(qzk[:, 0:64], qke_t[:, c, 64:128]), "sv")
                mark(f"qzkv{c}", "sv")

            @block.tensor
            def _(t, c=c):
                wait(t, f"qzkv{c}")
                t.wait_ge(sems["stt"], land[f"qtr{c}"][1])
                inc(t.transpose(trB_ps[:, 512:640], qzk[:], identb[:]), "stt")
                mark(f"tr{c}", "stt")

            @block.vector
            def _(v, c=c):
                wait(v, f"tr{c}")
                v.tensor_copy(qfm_sb[:], trB_ps[0:64, 0:128])
                inc(v.tensor_copy(kfm_sb[:], trB_ps[0:64, 512:640]), "sv")
                mark(f"trevac{c}", "sv")

            @block.tensor
            def _(t, c=c):
                wait(t, f"trevac{c}")
                wait(t, f"ksrepv_{c-1}" if c == 0 else f"ksrepv{c-1}")
                inc(t.matmul(m1_ps[:, 0:128], kfm_sb[:], qfm_sb[:], start=True, stop=True), "stt")
                mark(f"A{c}", "stt")

            @block.vector
            def _(v, c=c):
                wait(v, f"A{c}")
                inc(v.tensor_tensor(am_bf[:], m1_ps[:, 0:128], mask_bf[:], op=ALU.mult), "sv")
                mark(f"am{c}", "sv")

            @block.tensor
            def _(t, c=c):
                wait(t, f"am{c}")
                t.matmul(m1_ps[:, 128:129], am_bf[:], ones_col_bf[:], start=True, stop=True)
                inc(t.matmul(m1_ps[0:1, 192:256], ones_col_bf[:], qke_t[:, c, 64:128],
                         start=True, stop=True), "stt")
                mark(f"dent{c}", "stt")

            @block.vector
            def _(v, c=c):
                wait(v, f"dent{c}")
                v.scalar_tensor_tensor(tmpB[:, 0:64], qf32_t[:, c, :], 1.0,
                                       ks_rep[:], op0=ALU.mult, op1=ALU.mult,
                                       accum_out=den2[:])
                v.scalar_tensor_tensor(ks_run[:], m1_ps[0:1, 192:256], 1.0, ks_run[:],
                                       op0=ALU.mult, op1=ALU.add)
                v.drain()
                v.scalar_tensor_tensor(den_f[:], m1_ps[:, 128:129], ATTN_EPS, den2[:],
                                       op0=ALU.add, op1=ALU.add)
                v.drain()
                inc(v.reciprocal(rden[:], den_f[:]), "sv")
                mark(f"denv{c}", "sv")

            @block.tensor
            def _(t, c=c):
                wait(t, f"denv{c}")
                inc(t.matmul(m1_ps[:, 256:320], ones_row_f32[:], ks_run[:], start=True, stop=True), "stt")
                mark(f"ksrept{c}", "stt")

            @block.vector
            def _(v, c=c):
                wait(v, f"ksrept{c}")
                inc(v.tensor_copy(ks_rep[:], m1_ps[:, 256:320]), "sv")
                mark(f"ksrepv{c}", "sv")

            @block.tensor
            def _(t, c=c, buf=buf):
                wait(t, f"vtok_s{c}")
                wait(t, f"am{c}")
                wait(t, "kvsb_-1" if c == 0 else f"kvsb_{c-1}")
                if c > 0:
                    wait(t, f"y1_{c-1}")
                for i in range(8):
                    t.matmul(no_ps[:, i * 128:(i + 1) * 128],
                             vtok[:, buf, i * 128:(i + 1) * 128], am_bf[:],
                             start=True, stop=False)
                    inc(t.matmul(no_ps[:, i * 128:(i + 1) * 128],
                             kv_sb[:, i * 128:(i + 1) * 128], qfm_sb[:],
                             start=False, stop=True), "stt")
                mark(f"num{c}", "stt")

            @block.vector
            def _(v, c=c):
                wait(v, f"num{c}")
                inc(v.tensor_copy(num_fm[:].rearrange("p a b -> p (a b)"), no_ps[:]), "sv")
                mark(f"numev{c}", "sv")

            @block.tensor
            def _(t, c=c, buf=buf):
                for half in range(2):
                    inc(t.matmul(kv_ps[:, half * 512:(half + 1) * 512],
                             qke_t[:, c, 64:128],
                             vtok[:, buf, half * 512:(half + 1) * 512],
                             start=True, stop=True), "stt")
                mark(f"kvupd{c}", "stt")

            @block.vector
            def _(v, c=c):
                wait(v, f"kvupd{c}")
                v.scalar_tensor_tensor(kvacc[:], kv_ps[:], 1.0, kvacc[:],
                                       op0=ALU.mult, op1=ALU.add)
                v.drain()
                inc(v.tensor_copy(kv_sb[:], kvacc[:]), "sv")
                mark(f"kvsb_{c}", "sv")

            @block.tensor
            def _(t, c=c):
                wait(t, f"numev{c}")
                for half in range(2):
                    for i in range(8):
                        inc(t.matmul(no_ps[:, half * 512:(half + 1) * 512], num_fm[:, i, :],
                                 ow_bf[:, i, half * 512:(half + 1) * 512],
                                 start=(i == 0), stop=(i == 7)), "stt")
                mark(f"O{c}", "stt")

            @block.vector
            def _(v, c=c):
                wait(v, f"O{c}")
                inc(v.scalar_tensor_tensor(y_t[:, c, :], no_ps[:], rden[:], y_t[:, c, :],
                                       op0=ALU.mult, op1=ALU.add), "sv")
                mark(f"y1_{c}", "sv")

        @block.gpsimd
        def _(g):
            wait(g, "ttstr7")
            wait(g, f"O{NCH-1}")
            inc(g.dma_start(out=mc_bf[:],
                            in_=mc_dram.ap().rearrange("(i p) n -> p i n", p=128)),
                "smc", 16)
            mark("mc_loaded", "smc")

        if "tap_mc" in tap_d:
            @block.gpsimd
            def _(g):
                wait(g, "mc_loaded")
                inc(g.dma_start(out=tap_d["tap_mc"], in_=mc_bf[:]), "sg16", 16)

        if "tap_y" in tap_d and "D" not in stages:
            @block.gpsimd
            def _(g):
                wait(g, f"y1_{NCH-1}")
                for c in range(NCH):
                    inc(g.dma_start(out=tap_d["tap_y"][c * 128:(c + 1) * 128, :],
                                    in_=y_t[:, c, :]), "sg16", 16)

    if "D" in stages:
        @block.gpsimd
        def _(g):
            hf1 = h_fm[:, 0:4, :].rearrange("p a b -> p (a b)")
            for i in range(8):
                for half in range(2):
                    inc(g.dma_start(out=wstage[:],
                                    in_=f1w_d[i * 128:(i + 1) * 128,
                                              half * 2048:(half + 1) * 2048]), "sg16", 16)
                    g.wait_ge(sems["sg16"], cnt["sg16"])
                    inc(g.tensor_copy(hf1, wstage[:]), "sgc")
                    g.wait_ge(sems["sgc"], cnt["sgc"])
                    inc(g.dma_start(out=f1bf_dram.ap()[i, :, half * 2048:(half + 1) * 2048],
                                    in_=hf1), "sg16", 16)
            mark("f1prep", "sg16")
            hf2b = h_fm[:, 5:7, :].rearrange("p a b -> p (a b)")
            for m in range(32):
                inc(g.dma_start(out=wstage[:, 0:1024],
                                in_=f2w_d[m * 128:(m + 1) * 128, :]), "sg16", 16)
                g.wait_ge(sems["sg16"], cnt["sg16"])
                inc(g.tensor_copy(hf2b[:, 0:1024], wstage[:, 0:1024]), "sgc")
                g.wait_ge(sems["sgc"], cnt["sgc"])
                inc(g.dma_start(out=f2bf_dram.ap()[m, :, :], in_=hf2b[:, 0:1024]),
                    "sg16", 16)
            mark("f2prep", "sg16")

    if "D" in stages:
        p1p.close()
        p2p = ExitStack()
        f2ps = [p2p.enter_context(nc.psum_tensor(f"f2ps{i}", [128, 1024], F32))
                for i in range(2)]
        tr2_ps = p2p.enter_context(nc.psum_tensor("tr2_ps", [128, 4096], BF16))
        f1ps = [f2ps[0][:, 0:512], f2ps[0][:, 512:1024]]

        for j in range(4):
            for ci in range(4):
                c = 4 * j + ci

                @block.vector
                def _(v, c=c, j=j):
                    wait(v, f"y1_{c}")
                    if j > 0:
                        wait(v, f"y2_{4*(j-1)+3}")

                stats_chain(y_t[:, c, 0:512], y_t[:, c, 512:1024], tag=f"d{c}")

                @block.scalar
                def _(s, c=c):
                    wait(s, f"sbd{c}")
                    s.wait_ge(sems["ss"], cnt["ss"])
                    inc(s.activation(tmpA[:], y_t[:, c, :], AF.Identity,
                                 bias=scol[:, 1:2], scale=scol[:, 0:1]), "ss")
                    mark(f"z2s{c}", "ss")

                @block.vector
                def _(v, c=c):
                    wait(v, f"z2s{c}")
                    v.tensor_tensor(tmpB[:], tmpA[:], g2rep[:], op=ALU.mult)
                    v.drain()
                    if c >= 2:
                        wait(v, f"ztrt{c-2}")
                    inc(v.tensor_tensor(xnbf[:, c % 2, :], tmpB[:], b2rep[:], op=ALU.add), "sv")
                    mark(f"z2v{c}", "sv")

                @block.tensor
                def _(t, c=c, ci=ci, j=j):
                    wait(t, f"z2v{c}")
                    if c > 0:
                        wait(t, f"ztr{c-1}")
                    if j > 0:
                        wait(t, f"f1mm{j-1}_31")
                    for i in range(8):
                        col = (i % 4) * 1024 + (i // 4) * 128
                        ti = t.transpose(tr2_ps[:, col:col + 128],
                                         xnbf[:, c % 2, i * 128:(i + 1) * 128], identb[:])
                        if i == 3:
                            inc(ti, "stt")
                            t.wait_ge(sems["stt"], cnt["stt"])
                    inc(t.nop(), "stt")
                    mark(f"ztrt{c}", "stt")

                @block.vector
                def _(v, c=c, ci=ci):
                    wait(v, f"ztrt{c}")
                    for b in range(4):
                        cp = v.tensor_copy(
                            zfm[:, b:8:4, ci * 128:(ci + 1) * 128],
                            tr2_ps[:, b * 1024:b * 1024 + 256]
                            .rearrange("p (a b) -> p a b", a=2))
                        if b == 3:
                            inc(cp, "sv")
                    mark(f"ztr{c}", "sv")

            for m in range(32):
                if m % 2 == 0:
                    m2 = m // 2
                    g1 = j * 16 + m2
                    @block.gpsimd
                    def _(g, j=j, m2=m2, g1=g1):
                        if g1 == 0:
                            wait(g, "f2prep")
                        if g1 >= 3:
                            pj, pm2 = (g1 - 3) // 16, (g1 - 3) % 16
                            wait(g, f"f1use{pj}_{2*pm2+1}")
                        src = f1bf_dram.ap()[:, :, m2 * 256:(m2 + 1) * 256] \
                            .rearrange("i p f -> p i f")
                        inc(g.dma_start(out=f1roll[:, g1 % 3, :, :], in_=src),
                            f"f1s{g1 % 3}", 16)
                        mark(f"f1ld{j}_{m2}", f"f1s{g1 % 3}")

                @block.tensor
                def _(t, j=j, m=m):
                    if m == 0:
                        wait(t, "gm_f32")
                        wait(t, f"ztr{4*j+3}")
                        if j > 0:
                            wait(t, f"y2_{4*(j-1)+3}")
                    wait(t, f"f1ld{j}_{m//2}")
                    if m >= 2:
                        wait(t, f"gelu{j}_{m-2}")
                    elif j > 0:
                        wait(t, f"gelu{j-1}_{30+m}")
                    for i in range(8):
                        inc(t.matmul(f1ps[m % 2],
                                 f1roll[:, (j * 16 + m // 2) % 3, i,
                                        (m % 2) * 128:(m % 2 + 1) * 128],
                                 zfm[:, i, :],
                                 start=(i == 0), stop=(i == 7)), "stt")
                    mark(f"f1mm{j}_{m}", "stt")
                    mark(f"f1use{j}_{m}", "stt")

                @block.scalar
                def _(s, j=j, m=m):
                    wait(s, f"f1mm{j}_{m}")
                    if j > 0:
                        wait(s, f"f2u{(j-1)*64+32+m}")
                    inc(s.activation(h_fm[:, m, :], f1ps[m % 2], AF.Gelu,
                                 bias=f1b_cols[:, m:m + 1], scale=1.0), "ss")
                    mark(f"gelu{j}_{m}", "ss")

            for sub in range(2):
                for m in range(32):
                    u = (j * 2 + sub) * 32 + m
                    if m % 2 == 0:
                        u2 = ((j * 2 + sub) * 32 + m) // 2

                        @block.sync
                        def _(sy, u2=u2, m=m):
                            if u2 == 0:
                                wait(sy, "f2prep")
                            if u2 >= 3:
                                wait(sy, f"f2u{2*(u2-3)+1}")
                            inc(sy.dma_start(
                                out=f2roll[:, u2 % 3, :, :],
                                in_=f2bf_dram.ap()[m:m + 2, :, :]
                                .rearrange("m p f -> p m f")), f"f2s{u2 % 3}", 16)
                            mark(f"f2ld{u2}", f"f2s{u2 % 3}")

                    @block.tensor
                    def _(t, j=j, sub=sub, m=m, u=u):
                        if m == 0:
                            wait(t, f"gelu{j}_31")
                            if sub == 0 and j > 0:
                                wait(t, f"y2_{4*(j-1)+3}")
                            if sub == 1:
                                wait(t, f"y2_{4*j+1}")
                        wait(t, f"f2ld{u//2}")
                        for ch2 in range(2):
                            for half in range(2):
                                inc(t.matmul(
                                    f2ps[ch2][:, half * 512:(half + 1) * 512],
                                    h_fm[:, m, (sub * 2 + ch2) * 128:(sub * 2 + ch2 + 1) * 128],
                                    f2roll[:, (u // 2) % 3, m % 2,
                                           half * 512:(half + 1) * 512],
                                    start=(m == 0), stop=False), "stt")
                        mark(f"f2u{u}", "stt")

                @block.tensor
                def _(t, j=j, sub=sub):
                    for ch2 in range(2):
                        for half in range(2):
                            inc(t.matmul(f2ps[ch2][:, half * 512:(half + 1) * 512],
                                     ones_row_bf[:],
                                     f2b_row[0:1, half * 512:(half + 1) * 512],
                                     start=False, stop=True), "stt")
                    mark(f"f2mm{j}_{sub}", "stt")

                for ch2 in range(2):
                    @block.vector
                    def _(v, j=j, sub=sub, ch2=ch2):
                        wait(v, f"f2mm{j}_{sub}")
                        c = 4 * j + sub * 2 + ch2
                        inc(v.scalar_tensor_tensor(y_t[:, c, :], f2ps[ch2][:], 1.0,
                                               y_t[:, c, :],
                                               op0=ALU.mult, op1=ALU.add), "sv")
                        mark(f"y2_{c}", "sv")

        if "tap_y" in tap_d and "E" not in stages:
            @block.gpsimd
            def _(g):
                wait(g, f"y2_{NCH-1}")
                for c in range(NCH):
                    inc(g.dma_start(out=tap_d["tap_y"][c * 128:(c + 1) * 128, :],
                                    in_=y_t[:, c, :]), "sg16", 16)
        p2p.close()

    if "E" in stages:
        p4p = ExitStack()
        wc = [p4p.enter_context(nc.psum_tensor(f"wc{i}", [128, 1024], F32))
              for i in range(2)]
        trp4_ps = p4p.enter_context(nc.psum_tensor("trp4_ps", [128, 4096], BF16))

        @block.scalar
        def _(s):
            for c in range(2):
                inc(s.dma_start(out=xs_st[:, c, :], in_=x_d[c * 128:(c + 1) * 128, :]),
                    f"sx{c % 2}", 16)
                mark(f"xld2_{c}", f"sx{c % 2}")

        for c in range(NCH):
            buf = c % 2

            @block.vector
            def _(v, c=c):
                wait(v, f"y2_{c}")

            stats_chain(y_t[:, c, 0:512], y_t[:, c, 512:1024], tag=f"e{c}")

            @block.scalar
            def _(s, c=c):
                wait(s, f"sbe{c}")
                s.wait_ge(sems["ss"], cnt["ss"])
                inc(s.activation(tmpA[:], y_t[:, c, :], AF.Identity,
                             bias=scol[:, 1:2], scale=scol[:, 0:1]), "ss")
                mark(f"z3s{c}", "ss")

            @block.vector
            def _(v, c=c, buf=buf):
                wait(v, f"z3s{c}")
                wait(v, f"xld2_{c}")
                inc(v.tensor_tensor(y_t[:, c, :], tmpA[:], xs_st[:, buf, :], op=ALU.add), "sv")
                mark(f"x2_{c}", "sv")

            if c + 2 < NCH:
                @block.scalar
                def _(s, c=c):
                    wait(s, f"x2_{c}")
                    inc(s.dma_start(out=xs_st[:, c % 2, :],
                                    in_=x_d[(c + 2) * 128:(c + 3) * 128, :]),
                        f"sx{c % 2}", 16)
                    mark(f"xld2_{c+2}", f"sx{c % 2}")

            if "F" not in stages:
                continue

            stats_chain(y_t[:, c, 0:512], y_t[:, c, 512:1024], tag=f"f{c}")

            @block.scalar
            def _(s, c=c):
                wait(s, f"sbf{c}")
                s.wait_ge(sems["ss"], cnt["ss"])
                inc(s.activation(tmpA[:], y_t[:, c, :], AF.Identity,
                             bias=scol[:, 1:2], scale=scol[:, 0:1]), "ss")
                mark(f"z4s{c}", "ss")

            @block.vector
            def _(v, c=c):
                wait(v, f"z4s{c}")
                if c >= 2:
                    wait(v, f"wtrt{c-2}")
                inc(v.tensor_copy(xnbf[:, c % 2, :], tmpA[:]), "sv")
                mark(f"z4v{c}", "sv")

            @block.tensor
            def _(t, c=c):
                wait(t, f"z4v{c}")
                if c > 0:
                    wait(t, f"wtr{c-1}")
                elif "D" in stages:
                    wait(t, "f1mm3_31")
                if c >= 4:
                    wait(t, f"wcmm{c-4}")
                for i in range(8):
                    col = (i % 4) * 1024 + (i // 4) * 128
                    ti = t.transpose(trp4_ps[:, col:col + 128],
                                     xnbf[:, c % 2, i * 128:(i + 1) * 128], identb[:])
                    if i == 3:
                        inc(ti, "stt")
                        t.wait_ge(sems["stt"], cnt["stt"])
                inc(t.nop(), "stt")
                mark(f"wtrt{c}", "stt")

            @block.vector
            def _(v, c=c):
                wait(v, f"wtrt{c}")
                for b in range(4):
                    cp = v.tensor_copy(
                        zfm[:, b:8:4, (c % 4) * 128:(c % 4 + 1) * 128],
                        trp4_ps[:, b * 1024:b * 1024 + 256]
                        .rearrange("p (a b) -> p a b", a=2))
                    if b == 3:
                        inc(cp, "sv")
                mark(f"wtr{c}", "sv")

            @block.tensor
            def _(t, c=c):
                wait(t, f"wtr{c}")
                if c == 0:
                    wait(t, "mc_loaded")
                if c > 1:
                    wait(t, f"wfin{c-2}")
                for i in range(8):
                    for half in range(2):
                        t.matmul(wc[c % 2][:, half * 512:(half + 1) * 512],
                                 zfm[:, i, (c % 4) * 128:(c % 4 + 1) * 128],
                                 mc_bf[:, i, half * 512:(half + 1) * 512],
                                 start=(i == 0), stop=False)
                for half in range(2):
                    inc(t.matmul(wc[c % 2][:, half * 512:(half + 1) * 512], ones_row_bf[:],
                             fbias_row[0:1, half * 512:(half + 1) * 512],
                             start=False, stop=True), "stt")
                mark(f"wcmm{c}", "stt")

            @block.vector
            def _(v, c=c):
                wait(v, f"wcmm{c}")
                v.bn_stats(bnst[:, 0, :], wc[c % 2][:, 0:512])
                inc(v.bn_stats(bnst[:, 1, :], wc[c % 2][:, 512:1024]), "sv")
                v.drain()
                v.wait_ge(sems["sv"], cnt["sv"])
                inc(v.bn_aggr(bnag[:], bnst[:]), "sv")
                v.drain()
                v.wait_ge(sems["sv"], cnt["sv"])
                inc(v.nop(), "sv")
                mark(f"wsa{c}", "sv")

            @block.scalar
            def _(s, c=c):
                wait(s, f"wsa{c}")
                inc(s.activation(scol[:, 4:5], bnag[:, 1:2], AF.Ln, bias=eps_col[:]), "ss")
                s.wait_ge(sems["ss"], cnt["ss"])
                inc(s.activation(scol[:, 0:1], scol[:, 4:5], AF.Exp, scale=-0.5), "ss")
                mark(f"wss{c}", "ss")

            @block.vector
            def _(v, c=c):
                wait(v, f"wss{c}")
                inc(v.tensor_scalar(scol[:, 1:2], bnag[:, 0:1], scol[:, 0:1], -1.0,
                                    op0=ALU.mult, op1=ALU.mult), "sv")
                mark(f"wsb{c}", "sv")

            @block.scalar
            def _(s, c=c):
                wait(s, f"wsb{c}")
                s.wait_ge(sems["ss"], cnt["ss"])
                inc(s.activation(tmpA[:], wc[c % 2][:], AF.Identity,
                             bias=scol[:, 1:2], scale=scol[:, 0:1]), "ss")
                mark(f"wact{c}", "ss")

            @block.vector
            def _(v, c=c):
                wait(v, f"wact{c}")
                if c >= 2:
                    wait(v, f"outst{c-2}")
                inc(v.tensor_tensor(out_sb[:, (c % 2) * 1024:(c % 2 + 1) * 1024], tmpA[:], y_t[:, c, :], op=ALU.add), "sv")
                mark(f"wfin{c}", "sv")

            @block.scalar
            def _(s, c=c):
                wait(s, f"wfin{c}")
                inc(s.dma_start(out=out_d[c * 128:(c + 1) * 128, :],
                                in_=out_sb[:, (c % 2) * 1024:(c % 2 + 1) * 1024]), f"so{c % 2}", 16)
                mark(f"outst{c}", f"so{c % 2}")

        if "tap_y" in tap_d and "F" not in stages:
            @block.gpsimd
            def _(g):
                wait(g, f"x2_{NCH-1}")
                for c in range(NCH):
                    inc(g.dma_start(out=tap_d["tap_y"][c * 128:(c + 1) * 128, :],
                                    in_=y_t[:, c, :]), "sg16", 16)

        if "F" in stages:
            @block.vector
            def _(v):
                v.wait_ge(sems["so0"], cnt["so0"])
                v.wait_ge(sems["so1"], cnt["so1"])
        p4p.close()

    @block.gpsimd
    def _(g):
        g.wait_ge(sems["sg16"], cnt["sg16"])

    st.close()
    return nc


def _prep_inmaps(inputs):
    f32 = np.float32
    x = np.asarray(inputs["L0_attn_x"], f32)
    qw = np.asarray(inputs["L0_attn_q_w"], f32)
    kw = np.asarray(inputs["L0_attn_k_w"], f32)
    vw = np.asarray(inputs["L0_attn_v_w"], f32)
    ow = np.asarray(inputs["L0_attn_o_w"], f32)
    f1w = np.asarray(inputs["L0_attn_f1_w"], f32)
    f2w = np.asarray(inputs["L0_attn_f2_w"], f32)
    qkw_t = np.ascontiguousarray(np.concatenate([qw, kw], axis=0).T)
    vw_t = np.ascontiguousarray(vw.T)
    ow_t = np.ascontiguousarray(ow.T)
    f1w_t = np.ascontiguousarray(f1w.T)
    f2w_t = np.ascontiguousarray(f2w.T)

    rows = np.zeros((9, 1024), f32)
    rows[0] = np.asarray(inputs["L0_attn_ln1_w"], f32)
    rows[1] = np.asarray(inputs["L0_attn_ln1_b"], f32)
    rows[2] = np.asarray(inputs["L0_attn_ln2_w"], f32)
    rows[3] = np.asarray(inputs["L0_attn_ln2_b"], f32)
    rows[4] = np.asarray(inputs["L0_attn_o_b"], f32)
    rows[5] = np.asarray(inputs["L0_attn_v_b"], f32)
    rows[6] = np.asarray(inputs["L0_attn_f2_b"], f32)
    rows[7] = np.concatenate([np.asarray(inputs["L0_ffn_bias_real"], f32),
                              np.asarray(inputs["L0_ffn_bias_imag"], f32)])
    rows[8, 0:128] = np.concatenate([np.asarray(inputs["L0_attn_q_b"], f32),
                                     np.asarray(inputs["L0_attn_k_b"], f32)])

    f1b_cols = np.ascontiguousarray(
        np.asarray(inputs["L0_attn_f1_b"], f32).reshape(32, 128).T)

    G1R = np.asarray(inputs["L0_ffn_G1_R"], f32)
    G2R = np.asarray(inputs["L0_ffn_G2_R"], f32)
    G1I = np.asarray(inputs["L0_ffn_G1_I"], f32)
    G2I = np.asarray(inputs["L0_ffn_G2_I"], f32)
    g1m = np.ascontiguousarray(np.concatenate(
        [G1R[0].transpose(1, 0, 2).reshape(4, 1024),
         G1I[0].transpose(1, 0, 2).reshape(4, 1024)], axis=1))
    g2m = np.ascontiguousarray(np.concatenate(
        [G2R[:, :, 0, :].transpose(0, 2, 1).reshape(4, 256),
         G2I[:, :, 0, :].transpose(0, 2, 1).reshape(4, 256)], axis=1))

    in_maps = []
    for core in range(NCORES):
        b, q = core // QPER, core % QPER
        w_pre = np.zeros((128, 8), f32)
        for cc in range(NCORES):
            if cc // QPER == b and cc % QPER < q:
                w_pre[:, cc] = 1.0
        in_maps.append({
            "x": np.ascontiguousarray(x[b, q * T:(q + 1) * T]),
            "qkw_t": qkw_t, "vw_t": vw_t, "ow_t": ow_t,
            "f1w_t": f1w_t, "f2w_t": f2w_t,
            "g1m": g1m, "g2m": g2m, "rows": rows, "f1b_cols": f1b_cols,
            "w_pre": w_pre,
        })
    return in_maps


_NC_CACHE = {}


def kernel(**inputs):
    key = ("ABCDEF", ())
    if key not in _NC_CACHE:
        _NC_CACHE[key] = build_nc(taps=(), stages="ABCDEF")
    nc = _NC_CACHE[key]
    in_maps = _prep_inmaps(inputs)
    res = run_bass_kernel_spmd(nc, in_maps, core_ids=list(range(NCORES)))
    out = np.zeros((B, N, D), np.float32)
    for core in range(NCORES):
        b, q = core // QPER, core % QPER
        out[b, q * T:(q + 1) * T] = res.results[core]["out"]
    return out


if __name__ == "__main__":
    nc = build_nc()
    print("build ok")


# version 05
# speedup vs baseline: 1.0151x; 1.0022x over previous
import numpy as np
from contextlib import ExitStack

import concourse.bass as bass
import concourse.mybir as mybir
from concourse.bass_utils import run_bass_kernel_spmd

F32 = mybir.dt.float32
BF16 = mybir.dt.bfloat16
I32 = mybir.dt.int32
AF = mybir.ActivationFunctionType
ALU = mybir.AluOpType
AX = mybir.AxisListType

LN_EPS = 1e-5
ATTN_EPS = 1e-6

B, N, D, FD = 2, 8192, 1024, 64
NCORES = 8
QPER = 4
T = N // QPER
NCH = T // 128
DH = D // 128
MH = (4 * D) // 128


def build_nc(taps=(), stages="ABCDEF"):
    nc = bass.Bass("TRN2", target_bir_lowering=False,
                   detect_race_conditions=False)

    x_d = nc.dram_tensor("x", [T, D], F32, kind="ExternalInput").ap()
    qkw_d = nc.dram_tensor("qkw_t", [D, 128], F32, kind="ExternalInput").ap()
    vw_d = nc.dram_tensor("vw_t", [D, D], F32, kind="ExternalInput").ap()
    ow_d = nc.dram_tensor("ow_t", [D, D], F32, kind="ExternalInput").ap()
    f1w_d = nc.dram_tensor("f1w_t", [D, 4 * D], F32, kind="ExternalInput").ap()
    f2w_d = nc.dram_tensor("f2w_t", [4 * D, D], F32, kind="ExternalInput").ap()
    g1m_d = nc.dram_tensor("g1m", [4, 2048], F32, kind="ExternalInput").ap()
    g2m_d = nc.dram_tensor("g2m", [4, 512], F32, kind="ExternalInput").ap()
    rows_d = nc.dram_tensor("rows", [9, 1024], F32, kind="ExternalInput").ap()
    f1bc_d = nc.dram_tensor("f1b_cols", [128, 32], F32, kind="ExternalInput").ap()
    wpre_d = nc.dram_tensor("w_pre", [128, 8], F32, kind="ExternalInput").ap()
    out_d = nc.dram_tensor("out", [T, D], F32, kind="ExternalOutput").ap()

    tap_d = {}
    for tname, shape, dt in [("tap_y", [T, D], F32), ("tap_pay", [128, 576], F32),
                             ("tap_xnfm", [128, 8, 2048], BF16),
                             ("tap_kvpre", [64, 1024], BF16),
                             ("tap_qke", [128, NCH * 128], BF16),
                             ("tap_mc", [128, 8, 1024], BF16)]:
        if tname in taps:
            tap_d[tname] = nc.dram_tensor(tname, shape, dt, kind="ExternalOutput").ap()

    pay_dram = nc.dram_tensor("pay_dram", [128, 576], F32, kind="Internal")
    ag_dram = nc.dram_tensor("ag_dram", [8, 128, 576], F32, kind="Internal")
    mc_dram = nc.dram_tensor("mc_dram", [1024, 1024], BF16, kind="Internal")
    f1bf_dram = nc.dram_tensor("f1bf_dram", [8, 128, 4096], BF16, kind="Internal")
    f2bf_dram = nc.dram_tensor("f2bf_dram", [32, 128, 1024], BF16, kind="Internal")

    st = ExitStack()
    e = st.enter_context

    y_t = e(nc.sbuf_tensor("y_t", [128, NCH, 1024], F32))
    qke_t = e(nc.sbuf_tensor("qke_t", [128, NCH, 128], BF16))
    qf32_t = e(nc.sbuf_tensor("qf32_t", [128, NCH, 64], F32))
    tmpA_w = qf32_t[:].rearrange("p a b -> p (a b)")
    g1rep = e(nc.sbuf_tensor("g1rep", [128, 1024], BF16))
    b1rep = e(nc.sbuf_tensor("b1rep", [128, 1024], BF16))
    g2rep = e(nc.sbuf_tensor("g2rep", [128, 1024], BF16))
    b2rep = e(nc.sbuf_tensor("b2rep", [128, 1024], BF16))
    ob_rep = e(nc.sbuf_tensor("ob_rep", [128, 1024], BF16))
    f1b_cols = e(nc.sbuf_tensor("f1b_colsb", [128, 32], F32))
    mask_bf = e(nc.sbuf_tensor("mask_bf", [128, 128], BF16))
    identb = e(nc.sbuf_tensor("identb", [128, 128], BF16))
    ones_row_bf = e(nc.sbuf_tensor("ones_row_bf", [1, 128], BF16))
    ones_col_bf = e(nc.sbuf_tensor("ones_col_bf", [128, 1], BF16))
    ones_row_f32 = e(nc.sbuf_tensor("ones_row_f32", [1, 128], F32))
    qkb_row = e(nc.sbuf_tensor("qkb_row", [1, 128], BF16))
    vb_row = e(nc.sbuf_tensor("vb_row", [1, 1024], BF16))
    f2b_row = e(nc.sbuf_tensor("f2b_row", [1, 1024], BF16))
    fbias_row = e(nc.sbuf_tensor("fbias_row", [1, 1024], BF16))
    wpre_sb = e(nc.sbuf_tensor("wpre_sb", [128, 8], F32))
    kv_sb = e(nc.sbuf_tensor("kv_sb", [64, 1024], BF16))
    kvacc = e(nc.sbuf_tensor("kvacc", [64, 1024], F32))
    ks_run = e(nc.sbuf_tensor("ks_run", [1, 64], F32))
    ks_rep = e(nc.sbuf_tensor("ks_rep", [128, 64], F32))
    q0pre_bf = e(nc.sbuf_tensor("q0pre_bf", [1, 64], BF16))
    bnst = e(nc.sbuf_tensor("bnst", [128, 2, 6], F32))
    scol_t = e(nc.sbuf_tensor("scol_t", [128, NCH, 8], F32))
    bnag_t = e(nc.sbuf_tensor("bnag_t", [128, NCH, 2], F32))
    elu_sc = e(nc.sbuf_tensor("elu_sc", [128, 256], F32))
    eps_col = e(nc.sbuf_tensor("eps_col", [128, 1], F32))
    den_f = e(nc.sbuf_tensor("den_f", [128, 1], F32))
    rden = e(nc.sbuf_tensor("rden", [128, 1], F32))
    den2 = e(nc.sbuf_tensor("den2", [128, 1], F32))
    tmpA = e(nc.sbuf_tensor("tmpA", [128, 1024], F32))
    tmpB = e(nc.sbuf_tensor("tmpB", [128, 1024], F32))
    xs_st = e(nc.sbuf_tensor("xs_st", [128, 2, 1024], F32))
    wstage = e(nc.sbuf_tensor("wstage", [128, 2048], F32))
    xnbf = e(nc.sbuf_tensor("xnbf", [128, 2, 1024], BF16))

    p1s = ExitStack()
    iota_t = p1s.enter_context(nc.sbuf_tensor("iota_t", [128, 128], I32))
    qzk = p1s.enter_context(nc.sbuf_tensor("qzk", [128, 128], BF16))
    am_bf = p1s.enter_context(nc.sbuf_tensor("am_bf", [128, 128], BF16))
    vtok = p1s.enter_context(nc.sbuf_tensor("vtok", [128, 2, 1024], BF16))
    num_fm = p1s.enter_context(nc.sbuf_tensor("num_fm", [128, 8, 128], BF16))
    qfm_sb = p1s.enter_context(nc.sbuf_tensor("qfm_sb", [64, 128], BF16))
    kfm_sb = p1s.enter_context(nc.sbuf_tensor("kfm_sb", [64, 128], BF16))
    xn_fm = p1s.enter_context(nc.sbuf_tensor("xn_fm", [128, 8, 2048], BF16))
    vw_bf = p1s.enter_context(nc.sbuf_tensor("vw_bf", [128, 8, 1024], BF16))
    ow_bf = p1s.enter_context(nc.sbuf_tensor("ow_bf", [128, 8, 1024], BF16))
    qkw_bf = p1s.enter_context(nc.sbuf_tensor("qkw_bf", [128, 8, 128], BF16))
    ttx = p1s.enter_context(nc.sbuf_tensor("ttx", [128, 256], BF16))
    gm_sb = p1s.enter_context(nc.sbuf_tensor("gm_sb", [4, 2560], BF16))
    ptpre = p1s.enter_context(nc.sbuf_tensor("ptpre", [128, 8, 64], F32))
    ptpre_bf = num_fm[:, :, 0:64]
    p1s.close()

    mc_bf = e(nc.sbuf_tensor("mc_bf", [128, 8, 1024], BF16))
    zfm = e(nc.sbuf_tensor("zfm", [128, 8, 512], BF16))
    h_fm = e(nc.sbuf_tensor("h_fm", [128, 32, 512], BF16))
    f1roll = e(nc.sbuf_tensor("f1roll", [128, 3, 8, 256], BF16))
    f2roll = e(nc.sbuf_tensor("f2roll", [128, 3, 2, 1024], BF16))
    out_sb = wstage

    pAp = ExitStack()
    repvp_ps = pAp.enter_context(nc.psum_tensor("repvp_ps", [128, 1024], F32))
    pt_ps = pAp.enter_context(nc.psum_tensor("pt_ps", [128, 1024], F32))
    trA_ps = pAp.enter_context(nc.psum_tensor("trA_ps", [128, 4096], BF16))
    vp_ps = repvp_ps

    semnames = ["sx0", "sx1", "sg16", "sgm", "smc", "str_", "so0", "so1", "sv", "ss", "stt", "sgc", "scc", "f1s0", "f1s1", "f1s2", "f1s3", "f2s0", "f2s1", "f2s2", "f2s3"]
    sems = {n: e(nc.semaphore(n)) for n in semnames}
    cnt = {n: 0 for n in semnames}
    land = {}

    def inc(instr, s, amt=1):
        cnt[s] += amt
        instr.then_inc(sems[s], amt)
        return cnt[s]

    def mark(key, s):
        land[key] = (s, cnt[s])

    agstage = tmpA[:, 0:576]
    payload = tmpB[:, 0:576]
    row_st = wstage[0:1, 1024:2048]

    def wait(eng, key):
        s, v = land[key]
        eng.wait_ge(sems[s], v)

    block = e(nc.Block())

    @block.gpsimd
    def _(g):
        inc(g.iota(iota_t[:], pattern=[[1, 128]], base=0, channel_multiplier=-1), "sgc")
        inc(g.memset(qzk[:], 0.0), "sgc")
        inc(g.memset(ones_row_bf[:], 1.0), "sgc")
        inc(g.memset(ones_col_bf[:], 1.0), "sgc")
        inc(g.memset(ones_row_f32[:], 1.0), "sgc")
        inc(g.memset(ks_run[:], 0.0), "sgc")
        inc(g.memset(eps_col[:], LN_EPS), "sgc")
        mark("iota", "sgc")
        inc(g.dma_start(out=f1b_cols[:], in_=f1bc_d), "sgm", 16)
        inc(g.dma_start(out=wpre_sb[:], in_=wpre_d), "sgm", 16)
        inc(g.dma_start(out=tmpA[0:4, :], in_=g1m_d[:, 0:1024]), "sgm", 16)
        inc(g.dma_start(out=tmpB[0:4, :], in_=g1m_d[:, 1024:2048]), "sgm", 16)
        inc(g.dma_start(out=xs_st[0:4, 0, 0:512], in_=g2m_d), "sgm", 16)
        mark("gm_f32", "sgm")
        inc(g.dma_start(out=wstage[:, 0:1024].rearrange("p (c m) -> p c m", c=8),
                        in_=qkw_d.rearrange("(c k) m -> k c m", k=128)), "sg16", 16)
        mark("qkw_f32", "sg16")
        g.wait_ge(sems["sg16"], cnt["sg16"])
        inc(g.tensor_copy(qkw_bf[:], wstage[:, 0:1024]
                          .rearrange("p (c m) -> p c m", c=8)), "sgc")
        g.wait_ge(sems["sgc"], cnt["sgc"])
        inc(g.nop(), "sgc")
        mark("qkw_bf", "sgc")

    @block.scalar
    def _(s):
        wait(s, "iota")

    @block.vector
    def _(v):
        wait(v, "iota")
        v.tensor_scalar(mask_bf[:], iota_t[:], 0, None, op0=ALU.is_ge)
        inc(v.tensor_scalar(identb[:], iota_t[:], 0, None, op0=ALU.is_equal), "sv")
        mark("masks", "sv")
        wait(v, "gm_f32")
        v.tensor_copy(gm_sb[0:4, 0:1024], tmpA[0:4, :])
        v.tensor_copy(gm_sb[0:4, 1024:2048], tmpB[0:4, :])
        inc(v.tensor_copy(gm_sb[0:4, 2048:2560], xs_st[0:4, 0, 0:512]), "sv")
        mark("gm_bf", "sv")

    rep_specs = [(0, g1rep), (1, b1rep), (2, g2rep), (3, b2rep), (4, ob_rep)]
    for ri, (rowi, dst) in enumerate(rep_specs):
        @block.gpsimd
        def _(g, ri=ri, rowi=rowi):
            if ri > 0:
                wait(g, f"rep{ri-1}_t")
            inc(g.dma_start(out=row_st[:], in_=rows_d[rowi:rowi + 1, :]), "sg16", 16)
            mark(f"rep{ri}_row", "sg16")

        @block.tensor
        def _(t, ri=ri):
            wait(t, f"rep{ri}_row")
            if ri == 0:
                wait(t, "iota")
            if ri > 0:
                wait(t, f"rep{ri-1}_v")
            for half in range(2):
                inc(t.matmul(vp_ps[:, half * 512:(half + 1) * 512], ones_row_f32[:],
                         row_st[0:1, half * 512:(half + 1) * 512], start=True, stop=True), "stt")
            mark(f"rep{ri}_t", "stt")

        @block.vector
        def _(v, ri=ri, dst=dst):
            wait(v, f"rep{ri}_t")
            inc(v.tensor_copy(dst[:], vp_ps[:]), "sv")
            mark(f"rep{ri}_v", "sv")

    bfrow_specs = [(8, qkb_row, 128), (5, vb_row, 1024), (6, f2b_row, 1024),
                   (7, fbias_row, 1024)]
    for bi, (rowi, dst, w) in enumerate(bfrow_specs):
        @block.gpsimd
        def _(g, bi=bi, rowi=rowi):
            wait(g, "rep4_v" if bi == 0 else f"brow{bi-1}_v")
            inc(g.dma_start(out=row_st[:], in_=rows_d[rowi:rowi + 1, :]), "sg16", 16)
            mark(f"brow{bi}", "sg16")

        @block.vector
        def _(v, bi=bi, dst=dst, w=w):
            wait(v, f"brow{bi}")
            inc(v.tensor_copy(dst[:], row_st[0:1, 0:w]), "sv")
            mark(f"brow{bi}_v", "sv")
    ROWS_DONE = f"brow{len(bfrow_specs)-1}_v"

    mc_view = mc_dram.ap().rearrange("(rh d g) (ch b e) -> rh ch b d g e",
                                     rh=2, d=32, g=16, ch=2, b=32, e=16)

    def tt_chain(imag):
        g1o = 1024 if imag else 0
        g2o = 2304 if imag else 2048
        tag = "i" if imag else "r"
        for p in range(8):
            @block.tensor
            def _(t, p=p):
                if p == 0 and imag:
                    wait(t, "gm_bf")
                    wait(t, "rep4_v")
                if p == 0 and not imag:
                    wait(t, "ttevi7")
                if p > 0:
                    wait(t, f"ttev{tag}{p-1}")
                inc(t.matmul(vp_ps[:, 0:256], gm_sb[0:4, g1o + p * 128:g1o + (p + 1) * 128],
                         gm_sb[0:4, g2o:g2o + 256], start=True, stop=True), "stt")
                mark(f"ttmm{tag}{p}", "stt")

            @block.vector
            def _(v, p=p):
                wait(v, f"ttmm{tag}{p}")
                if p > 0:
                    wait(v, f"ttst{tag}{p-1}")
                elif not imag:
                    wait(v, "ttsti7")
                inc(v.tensor_copy(ttx[:], vp_ps[:, 0:256]), "sv")
                mark(f"ttev{tag}{p}", "sv")

            if imag:
                @block.gpsimd
                def _(g, p=p):
                    wait(g, f"ttev{tag}{p}")
                    for bq in range(4):
                        src_ = ttx[bq * 32:(bq + 1) * 32, :].rearrange(
                            "d (g e) -> d g e", g=16)
                        inc(g.dma_start(out=mc_view[0, 1, 4 * p + bq, :, :, :],
                                        in_=src_), "sg16", 16)
                    mark(f"ttstwi{p}", "sg16")

                @block.scalar
                def _(s, p=p):
                    wait(s, f"ttstwi{p}")
                    inc(s.mul(ttx[:], ttx[:], -1.0), "ss")
                    mark(f"ttneg{p}", "ss")

                @block.gpsimd
                def _(g, p=p):
                    wait(g, f"ttneg{p}")
                    for bq in range(4):
                        src_ = ttx[bq * 32:(bq + 1) * 32, :].rearrange(
                            "d (g e) -> d g e", g=16)
                        inc(g.dma_start(out=mc_view[1, 0, 4 * p + bq, :, :, :],
                                        in_=src_), "sg16", 16)
                    mark(f"ttst{tag}{p}", "sg16")
            else:
                @block.gpsimd
                def _(g, p=p):
                    wait(g, f"ttev{tag}{p}")
                    for bq in range(4):
                        src_ = ttx[bq * 32:(bq + 1) * 32, :].rearrange(
                            "d (g e) -> d g e", g=16)
                        inc(g.dma_start(out=mc_view[0, 0, 4 * p + bq, :, :, :],
                                        in_=src_), "sg16", 16)
                        inc(g.dma_start(out=mc_view[1, 1, 4 * p + bq, :, :, :],
                                        in_=src_), "sg16", 16)
                    mark(f"ttst{tag}{p}", "sg16")

    tt_chain(imag=True)
    tt_chain(imag=False)

    @block.gpsimd
    def _(g):
        mark("st_wr", "sg16")
        for nm, src_d, dst in [("vw", vw_d, vw_bf), ("ow", ow_d, ow_bf)]:
            for i0 in range(0, 8, 2):
                g.wait_ge(sems["sgc"], cnt["sgc"])
                inc(g.dma_start(out=wstage[:].rearrange("p (c m) -> p c m", c=2),
                                in_=src_d[i0 * 128:(i0 + 2) * 128, :]
                                .rearrange("(c k) m -> k c m", k=128)), "sg16", 16)
                g.wait_ge(sems["sg16"], cnt["sg16"])
                inc(g.tensor_copy(dst[:, i0:i0 + 2, :],
                                  wstage[:].rearrange("p (c m) -> p c m", c=2)), "sgc")
        g.wait_ge(sems["sgc"], cnt["sgc"])
        inc(g.nop(), "sgc")
        mark("vwow_bf", "sgc")

    def stats_chain(src_hi, src_lo, tag, sc, bg, s2=False):
        @block.vector
        def _(v):
            v.wait_ge(sems["sv"], cnt["sv"])
            v.bn_stats(bnst[:, 0, :], src_hi)
            inc(v.bn_stats(bnst[:, 1, :], src_lo), "sv")
            v.drain()
            v.wait_ge(sems["sv"], cnt["sv"])
            inc(v.bn_aggr(bg, bnst[:]), "sv")
            v.drain()
            v.wait_ge(sems["sv"], cnt["sv"])
            inc(v.tensor_scalar(sc[:, 3:4], bg[:, 0:1], -1.0, None, op0=ALU.mult), "sv")
            mark(f"sa{tag}", "sv")

        @block.scalar
        def _(s):
            wait(s, f"sa{tag}")
            inc(s.activation(sc[:, 4:5], bg[:, 1:2], AF.Ln, bias=eps_col[:]), "ss")
            s.wait_ge(sems["ss"], cnt["ss"])
            inc(s.activation(sc[:, 0:1], sc[:, 4:5], AF.Exp, scale=-0.5), "ss")
            s.wait_ge(sems["ss"], cnt["ss"])
            inc(s.activation(sc[:, 1:2], sc[:, 3:4], AF.Identity, scale=sc[:, 0:1]), "ss")
            if s2:
                inc(s.activation(sc[:, 5:6], bg[:, 1:2], AF.Identity, scale=sc[:, 0:1]), "ss")
                s.wait_ge(sems["ss"], cnt["ss"])
                inc(s.activation(sc[:, 6:7], sc[:, 5:6], AF.Identity, scale=sc[:, 0:1]), "ss")
                s.wait_ge(sems["ss"], cnt["ss"])
                inc(s.activation(sc[:, 7:8], sc[:, 6:7], AF.Ln, bias=eps_col[:]), "ss")
                s.wait_ge(sems["ss"], cnt["ss"])
                inc(s.activation(sc[:, 2:3], sc[:, 7:8], AF.Exp, scale=-0.5), "ss")
            mark(f"ssd{tag}", "ss")

    if "A" in stages:
        @block.scalar
        def _(s):
            for c in range(2):
                inc(s.dma_start(out=xs_st[:, c, :], in_=x_d[c * 128:(c + 1) * 128, :]),
                    f"sx{c % 2}", 16)
                mark(f"xld{c}", f"sx{c % 2}")

        for c in range(NCH):
            buf = c % 2

            @block.vector
            def _(v, c=c):
                wait(v, f"xld{c}")
                if c == 0:
                    wait(v, "rep4_v")
                    wait(v, "gm_bf")

            stats_chain(xs_st[:, buf, 0:512], xs_st[:, buf, 512:1024],
                        tag=f"a{c}", sc=scol_t[:, c, :], bg=bnag_t[:, c, :], s2=True)
            @block.scalar
            def _(s, c=c, buf=buf):
                wait(s, f"ssda{c}")
                if c >= 1:
                    wait(s, f"xnv{c-1}")
                inc(s.activation(tmpA[:], xs_st[:, buf, :], AF.Identity,
                                 bias=scol_t[:, c, 1:2], scale=scol_t[:, c, 0:1]), "ss")
                mark(f"y0{c}", "ss")
                s.wait_ge(sems["ss"], cnt["ss"])
                if c + 2 < NCH:
                    inc(s.dma_start(out=xs_st[:, c % 2, :],
                                    in_=x_d[(c + 2) * 128:(c + 3) * 128, :]),
                        f"sx{c % 2}", 16)
                    mark(f"xld{c+2}", f"sx{c % 2}")

            @block.vector
            def _(v, c=c):
                wait(v, f"y0{c}")
                v.tensor_tensor(y_t[:, c, :], tmpA[:], ob_rep[:], op=ALU.add)
                v.scalar_tensor_tensor(tmpB[:], tmpA[:], scol_t[:, c, 2:3], g1rep[:],
                                       op0=ALU.mult, op1=ALU.mult)
                v.drain()
                if c >= 2:
                    wait(v, f"xtrt{c-2}")
                    wait(v, f"pt{c-2}")
                inc(v.tensor_tensor(xnbf[:, c % 2, :], tmpB[:], b1rep[:], op=ALU.add), "sv")
                mark(f"xnv{c}", "sv")

            @block.tensor
            def _(t, c=c):
                wait(t, f"xnv{c}")
                if c == 0:
                    wait(t, "masks")
                if c > 0:
                    wait(t, f"xtr{c-1}")
                for i in range(8):
                    col = (i % 4) * 1024 + (i // 4) * 128
                    ti = t.transpose(trA_ps[:, col:col + 128],
                                     xnbf[:, c % 2, i * 128:(i + 1) * 128], identb[:])
                    if i == 3:
                        inc(ti, "stt")
                        t.wait_ge(sems["stt"], cnt["stt"])
                inc(t.nop(), "stt")
                mark(f"xtrt{c}", "stt")

            @block.vector
            def _(v, c=c):
                wait(v, f"xtrt{c}")
                for b in range(4):
                    cp = v.tensor_copy(
                        xn_fm[:, b:8:4, c * 128:(c + 1) * 128],
                        trA_ps[:, b * 1024:b * 1024 + 256]
                        .rearrange("p (a b) -> p a b", a=2))
                    if b == 3:
                        inc(cp, "sv")
                mark(f"xtr{c}", "sv")

            @block.tensor
            def _(t, c=c):
                wait(t, f"xtr{c}")
                if c == 0:
                    wait(t, "qkw_bf")
                    wait(t, "brow0_v")
                    wait(t, "iota")
                if c == 0:
                    wait(t, "ttevr7")
                if c > 0:
                    wait(t, f"eluv{c-1}")
                for i in range(8):
                    t.matmul(repvp_ps[:, 0:128], xn_fm[:, i, c * 128:(c + 1) * 128],
                             qkw_bf[:, i, :], start=(i == 0), stop=False)
                inc(t.matmul(repvp_ps[:, 0:128], ones_row_bf[:], qkb_row[:],
                         start=False, stop=True), "stt")
                mark(f"qkmm{c}", "stt")

            @block.vector
            def _(v, c=c):
                wait(v, f"qkmm{c}")
                inc(v.tensor_scalar(elu_sc[:, 0:128], repvp_ps[:, 0:128], 0.0, None, op0=ALU.min), "sv")
                mark(f"elumin{c}", "sv")

            @block.scalar
            def _(s, c=c):
                wait(s, f"elumin{c}")
                inc(s.activation(elu_sc[:, 128:256], elu_sc[:, 0:128], AF.Exp), "ss")
                mark(f"eluexp{c}", "ss")

            @block.vector
            def _(v, c=c):
                wait(v, f"eluexp{c}")
                v.scalar_tensor_tensor(elu_sc[:, 0:128], repvp_ps[:, 0:128], 0.0,
                                       elu_sc[:, 128:256], op0=ALU.max, op1=ALU.add)
                v.drain()
                v.tensor_copy(qke_t[:, c, :], elu_sc[:, 0:128])
                inc(v.tensor_copy(qf32_t[:, c, :], elu_sc[:, 0:64]), "sv")
                mark(f"eluv{c}", "sv")

            @block.tensor
            def _(t, c=c):
                wait(t, f"eluv{c}")
                for i in range(8):
                    t.matmul(pt_ps[:, i * 64:(i + 1) * 64],
                             xnbf[:, c % 2, i * 128:(i + 1) * 128],
                             qke_t[:, c, 64:128],
                             start=(c == 0 and i == 0),
                             stop=(c == NCH - 1 and i == 7))
                inc(t.matmul(pt_ps[0:1, 512:576], ones_col_bf[:], qke_t[:, c, 64:128],
                         start=(c == 0), stop=(c == NCH - 1)), "stt")
                mark(f"pt{c}", "stt")

        @block.vector
        def _(v):
            wait(v, f"pt{NCH-1}")
            v.tensor_copy(payload[:, 0:512], pt_ps[:, 0:512])
            inc(v.tensor_copy(payload[0:1, 512:576], pt_ps[0:1, 512:576]), "sv")
            mark("payv", "sv")

        if "tap_xnfm" in tap_d:
            @block.gpsimd
            def _(g):
                wait(g, f"pt{NCH-1}")
                inc(g.dma_start(out=tap_d["tap_xnfm"], in_=xn_fm[:]), "sg16", 16)

        if "tap_pay" in tap_d:
            @block.gpsimd
            def _(g):
                wait(g, "payv")
                inc(g.dma_start(out=tap_d["tap_pay"], in_=payload[:]), "sg16", 16)
        if "tap_qke" in tap_d:
            @block.gpsimd
            def _(g):
                wait(g, f"eluv{NCH-1}")
                inc(g.dma_start(out=tap_d["tap_qke"]
                                .rearrange("p (a b) -> p a b", b=128),
                                in_=qke_t[:]), "sg16", 16)
        if "tap_y" in tap_d and "C" not in stages:
            @block.gpsimd
            def _(g):
                wait(g, f"xnv{NCH-1}")
                for c in range(NCH):
                    inc(g.dma_start(out=tap_d["tap_y"][c * 128:(c + 1) * 128, :],
                                    in_=y_t[:, c, :]), "sg16", 16)

    if "B" in stages or "C" in stages:
        pAp.close()
        p1p = ExitStack()
        kv_ps = p1p.enter_context(nc.psum_tensor("kv_ps", [64, 1024], F32))
        vp_ps = p1p.enter_context(nc.psum_tensor("vp_ps", [128, 1024], F32))
        no_ps = p1p.enter_context(nc.psum_tensor("no_ps", [128, 1024], F32))
        m1_ps = p1p.enter_context(nc.psum_tensor("m1_ps", [128, 512], F32))
        trB_ps = p1p.enter_context(nc.psum_tensor("trB_ps", [128, 1024], BF16))

    if "B" in stages:
        @block.gpsimd
        def _(g):
            wait(g, "payv")
            inc(g.dma_start(out=pay_dram.ap(), in_=payload[:]), "sg16", 16)
            mark("paydram", "sg16")
            wait(g, "paydram")
            inc(g.collective_compute(
                "AllGather", ALU.bypass, replica_groups=[list(range(NCORES))],
                ins=[pay_dram.ap().opt()], outs=[ag_dram.ap().opt()]), "scc")
            mark("ag", "scc")
            wait(g, "ag")

        for cc in range(NCORES):
            @block.gpsimd
            def _(g, cc=cc):
                if cc > 0:
                    wait(g, f"agacc{cc-1}")
                inc(g.dma_start(out=agstage[:], in_=ag_dram.ap()[cc, :, :]), "sg16", 16)
                mark(f"agld{cc}", "sg16")

            @block.vector
            def _(v, cc=cc):
                wait(v, f"agld{cc}")
                if cc == 0:
                    wait(v, "gm_f32")
                pt_flat = ptpre[:].rearrange("p a b -> p (a b)")
                if cc == 0:
                    v.tensor_scalar(pt_flat, agstage[:, 0:512], wpre_sb[:, 0:1],
                                    None, op0=ALU.mult)
                    inc(v.tensor_scalar(ks_run[:], agstage[0:1, 512:576],
                                        wpre_sb[0:1, 0:1], None, op0=ALU.mult), "sv")
                else:
                    v.scalar_tensor_tensor(pt_flat, agstage[:, 0:512],
                                           wpre_sb[:, cc:cc + 1], pt_flat,
                                           op0=ALU.mult, op1=ALU.add)
                    inc(v.scalar_tensor_tensor(ks_run[:], agstage[0:1, 512:576],
                                           wpre_sb[0:1, cc:cc + 1], ks_run[:],
                                           op0=ALU.mult, op1=ALU.add), "sv")
                mark(f"agacc{cc}", "sv")

        @block.vector
        def _(v):
            wait(v, f"agacc{NCORES-1}")
            v.drain()
            v.tensor_copy(ptpre_bf, ptpre[:])
            inc(v.tensor_copy(q0pre_bf[:], ks_run[:]), "sv")
            mark("ptbf", "sv")

        @block.tensor
        def _(t):
            wait(t, "ptbf")
            wait(t, "vwow_bf")
            for i in range(8):
                for half in range(2):
                    t.matmul(kv_ps[:, half * 512:(half + 1) * 512], num_fm[:, i, 0:64],
                             vw_bf[:, i, half * 512:(half + 1) * 512],
                             start=(i == 0), stop=False)
            for half in range(2):
                inc(t.matmul(kv_ps[:, half * 512:(half + 1) * 512], q0pre_bf[:],
                         vb_row[0:1, half * 512:(half + 1) * 512],
                         start=False, stop=True), "stt")
            mark("kvfold", "stt")

        @block.vector
        def _(v):
            wait(v, "kvfold")
            v.tensor_copy(kvacc[:], kv_ps[:])
            inc(v.tensor_copy(kv_sb[:], kv_ps[:]), "sv")
            mark("kvsb_-1", "sv")

        @block.tensor
        def _(t):
            wait(t, "ptbf")
            inc(t.matmul(m1_ps[:, 256:320], ones_row_f32[:], ks_run[:], start=True, stop=True), "stt")
            mark("ksrept_-1", "stt")

        @block.vector
        def _(v):
            wait(v, "ksrept_-1")
            inc(v.tensor_copy(ks_rep[:], m1_ps[:, 256:320]), "sv")
            mark("ksrepv_-1", "sv")

        if "tap_kvpre" in tap_d:
            @block.gpsimd
            def _(g):
                wait(g, "kvsb_-1")
                inc(g.dma_start(out=tap_d["tap_kvpre"], in_=kv_sb[:]), "sg16", 16)

    if "C" in stages:
        for c in range(NCH):
            buf = c % 2

            @block.tensor
            def _(t, c=c, buf=buf):
                if c == 0:
                    wait(t, f"xtr{NCH-1}")
                if c > 0:
                    wait(t, f"trevac{c-1}")
                inc(t.transpose(trB_ps[:, 0:128], qke_t[:, c, :], identb[:]), "stt")
                mark(f"qtr{c}", "stt")
                if c == 0:
                    wait(t, "vwow_bf")
                    wait(t, ROWS_DONE)
                    wait(t, "ttevr7")
                    wait(t, f"eluv{NCH-1}")
                else:
                    wait(t, f"vtok_s{c-1}")
                for half in range(2):
                    for i in range(8):
                        t.matmul(vp_ps[:, half * 512:(half + 1) * 512],
                                 xn_fm[:, i, c * 128:(c + 1) * 128],
                                 vw_bf[:, i, half * 512:(half + 1) * 512],
                                 start=(i == 0), stop=False)
                    inc(t.matmul(vp_ps[:, half * 512:(half + 1) * 512], ones_row_bf[:],
                             vb_row[0:1, half * 512:(half + 1) * 512],
                             start=False, stop=True), "stt")
                mark(f"vp{c}", "stt")

            @block.scalar
            def _(s, c=c, buf=buf):
                wait(s, f"vp{c}")
                inc(s.activation(vtok[:, buf, :], vp_ps[:], AF.Identity), "ss")
                mark(f"vtok_s{c}", "ss")

            @block.vector
            def _(v, c=c):
                if c > 0:
                    wait(v, f"tr{c-1}")
                inc(v.tensor_copy(qzk[:, 0:64], qke_t[:, c, 64:128]), "sv")
                mark(f"qzkv{c}", "sv")

            @block.tensor
            def _(t, c=c):
                wait(t, f"qzkv{c}")
                t.wait_ge(sems["stt"], land[f"qtr{c}"][1])
                inc(t.transpose(trB_ps[:, 512:640], qzk[:], identb[:]), "stt")
                mark(f"tr{c}", "stt")

            @block.vector
            def _(v, c=c):
                wait(v, f"tr{c}")
                v.tensor_copy(qfm_sb[:], trB_ps[0:64, 0:128])
                inc(v.tensor_copy(kfm_sb[:], trB_ps[0:64, 512:640]), "sv")
                mark(f"trevac{c}", "sv")

            @block.tensor
            def _(t, c=c):
                wait(t, f"trevac{c}")
                wait(t, f"ksrepv_{c-1}" if c == 0 else f"ksrepv{c-1}")
                inc(t.matmul(m1_ps[:, 0:128], kfm_sb[:], qfm_sb[:], start=True, stop=True), "stt")
                mark(f"A{c}", "stt")

            @block.vector
            def _(v, c=c):
                wait(v, f"A{c}")
                inc(v.tensor_tensor(am_bf[:], m1_ps[:, 0:128], mask_bf[:], op=ALU.mult), "sv")
                mark(f"am{c}", "sv")

            @block.tensor
            def _(t, c=c):
                wait(t, f"am{c}")
                t.matmul(m1_ps[:, 128:129], am_bf[:], ones_col_bf[:], start=True, stop=True)
                inc(t.matmul(m1_ps[0:1, 192:256], ones_col_bf[:], qke_t[:, c, 64:128],
                         start=True, stop=True), "stt")
                mark(f"dent{c}", "stt")

            @block.vector
            def _(v, c=c):
                wait(v, f"dent{c}")
                v.scalar_tensor_tensor(tmpB[:, 0:64], qf32_t[:, c, :], 1.0,
                                       ks_rep[:], op0=ALU.mult, op1=ALU.mult,
                                       accum_out=den2[:])
                v.scalar_tensor_tensor(ks_run[:], m1_ps[0:1, 192:256], 1.0, ks_run[:],
                                       op0=ALU.mult, op1=ALU.add)
                v.drain()
                v.scalar_tensor_tensor(den_f[:], m1_ps[:, 128:129], ATTN_EPS, den2[:],
                                       op0=ALU.add, op1=ALU.add)
                v.drain()
                inc(v.reciprocal(rden[:], den_f[:]), "sv")
                mark(f"denv{c}", "sv")

            @block.tensor
            def _(t, c=c):
                wait(t, f"denv{c}")
                inc(t.matmul(m1_ps[:, 256:320], ones_row_f32[:], ks_run[:], start=True, stop=True), "stt")
                mark(f"ksrept{c}", "stt")

            @block.vector
            def _(v, c=c):
                wait(v, f"ksrept{c}")
                inc(v.tensor_copy(ks_rep[:], m1_ps[:, 256:320]), "sv")
                mark(f"ksrepv{c}", "sv")

            @block.tensor
            def _(t, c=c, buf=buf):
                wait(t, f"vtok_s{c}")
                wait(t, f"am{c}")
                wait(t, "kvsb_-1" if c == 0 else f"kvsb_{c-1}")
                if c > 0:
                    wait(t, f"y1_{c-1}")
                for i in range(8):
                    t.matmul(no_ps[:, i * 128:(i + 1) * 128],
                             vtok[:, buf, i * 128:(i + 1) * 128], am_bf[:],
                             start=True, stop=False)
                    inc(t.matmul(no_ps[:, i * 128:(i + 1) * 128],
                             kv_sb[:, i * 128:(i + 1) * 128], qfm_sb[:],
                             start=False, stop=True), "stt")
                mark(f"num{c}", "stt")

            @block.vector
            def _(v, c=c):
                wait(v, f"num{c}")
                inc(v.tensor_copy(num_fm[:].rearrange("p a b -> p (a b)"), no_ps[:]), "sv")
                mark(f"numev{c}", "sv")

            @block.tensor
            def _(t, c=c, buf=buf):
                for half in range(2):
                    inc(t.matmul(kv_ps[:, half * 512:(half + 1) * 512],
                             qke_t[:, c, 64:128],
                             vtok[:, buf, half * 512:(half + 1) * 512],
                             start=True, stop=True), "stt")
                mark(f"kvupd{c}", "stt")

            @block.vector
            def _(v, c=c):
                wait(v, f"kvupd{c}")
                v.scalar_tensor_tensor(kvacc[:], kv_ps[:], 1.0, kvacc[:],
                                       op0=ALU.mult, op1=ALU.add)
                v.drain()
                inc(v.tensor_copy(kv_sb[:], kvacc[:]), "sv")
                mark(f"kvsb_{c}", "sv")

            @block.tensor
            def _(t, c=c):
                wait(t, f"numev{c}")
                for half in range(2):
                    for i in range(8):
                        inc(t.matmul(no_ps[:, half * 512:(half + 1) * 512], num_fm[:, i, :],
                                 ow_bf[:, i, half * 512:(half + 1) * 512],
                                 start=(i == 0), stop=(i == 7)), "stt")
                mark(f"O{c}", "stt")

            @block.vector
            def _(v, c=c):
                wait(v, f"O{c}")
                inc(v.scalar_tensor_tensor(y_t[:, c, :], no_ps[:], rden[:], y_t[:, c, :],
                                       op0=ALU.mult, op1=ALU.add), "sv")
                mark(f"y1_{c}", "sv")

        @block.gpsimd
        def _(g):
            wait(g, "ttstr7")
            wait(g, f"O{NCH-1}")
            inc(g.dma_start(out=mc_bf[:],
                            in_=mc_dram.ap().rearrange("(i p) n -> p i n", p=128)),
                "smc", 16)
            mark("mc_loaded", "smc")

        if "tap_mc" in tap_d:
            @block.gpsimd
            def _(g):
                wait(g, "mc_loaded")
                inc(g.dma_start(out=tap_d["tap_mc"], in_=mc_bf[:]), "sg16", 16)

        if "tap_y" in tap_d and "D" not in stages:
            @block.gpsimd
            def _(g):
                wait(g, f"y1_{NCH-1}")
                for c in range(NCH):
                    inc(g.dma_start(out=tap_d["tap_y"][c * 128:(c + 1) * 128, :],
                                    in_=y_t[:, c, :]), "sg16", 16)

    if "D" in stages:
        @block.gpsimd
        def _(g):
            hf1 = h_fm[:, 0:4, :].rearrange("p a b -> p (a b)")
            for i in range(8):
                for half in range(2):
                    inc(g.dma_start(out=wstage[:],
                                    in_=f1w_d[i * 128:(i + 1) * 128,
                                              half * 2048:(half + 1) * 2048]), "sg16", 16)
                    g.wait_ge(sems["sg16"], cnt["sg16"])
                    inc(g.tensor_copy(hf1, wstage[:]), "sgc")
                    g.wait_ge(sems["sgc"], cnt["sgc"])
                    inc(g.dma_start(out=f1bf_dram.ap()[i, :, half * 2048:(half + 1) * 2048],
                                    in_=hf1), "sg16", 16)
            mark("f1prep", "sg16")
            hf2b = h_fm[:, 5:7, :].rearrange("p a b -> p (a b)")
            for m in range(32):
                inc(g.dma_start(out=wstage[:, 0:1024],
                                in_=f2w_d[m * 128:(m + 1) * 128, :]), "sg16", 16)
                g.wait_ge(sems["sg16"], cnt["sg16"])
                inc(g.tensor_copy(hf2b[:, 0:1024], wstage[:, 0:1024]), "sgc")
                g.wait_ge(sems["sgc"], cnt["sgc"])
                inc(g.dma_start(out=f2bf_dram.ap()[m, :, :], in_=hf2b[:, 0:1024]),
                    "sg16", 16)
            mark("f2prep", "sg16")

    if "D" in stages:
        p1p.close()
        p2p = ExitStack()
        f2ps = [p2p.enter_context(nc.psum_tensor(f"f2ps{i}", [128, 1024], F32))
                for i in range(2)]
        tr2_ps = p2p.enter_context(nc.psum_tensor("tr2_ps", [128, 4096], BF16))
        f1ps = [f2ps[0][:, 0:512], f2ps[0][:, 512:1024]]

        for j in range(4):
            for ci in range(4):
                c = 4 * j + ci

                @block.vector
                def _(v, c=c, j=j):
                    wait(v, f"y1_{c}")
                    if j > 0:
                        wait(v, f"y2_{4*(j-1)+3}")

                stats_chain(y_t[:, c, 0:512], y_t[:, c, 512:1024], tag=f"d{c}",
                            sc=scol_t[:, c, :], bg=bnag_t[:, c, :])

                @block.scalar
                def _(s, c=c):
                    wait(s, f"ssdd{c}")
                    inc(s.activation(tmpA[:], y_t[:, c, :], AF.Identity,
                                 bias=scol_t[:, c, 1:2], scale=scol_t[:, c, 0:1]), "ss")
                    mark(f"z2s{c}", "ss")

                @block.vector
                def _(v, c=c):
                    wait(v, f"z2s{c}")
                    v.tensor_tensor(tmpB[:], tmpA[:], g2rep[:], op=ALU.mult)
                    v.drain()
                    if c >= 2:
                        wait(v, f"ztrt{c-2}")
                    inc(v.tensor_tensor(xnbf[:, c % 2, :], tmpB[:], b2rep[:], op=ALU.add), "sv")
                    mark(f"z2v{c}", "sv")

                @block.tensor
                def _(t, c=c, ci=ci, j=j):
                    wait(t, f"z2v{c}")
                    if c > 0:
                        wait(t, f"ztr{c-1}")
                    if j > 0:
                        wait(t, f"f1mm{j-1}_31")
                    for i in range(8):
                        col = (i % 4) * 1024 + (i // 4) * 128
                        ti = t.transpose(tr2_ps[:, col:col + 128],
                                         xnbf[:, c % 2, i * 128:(i + 1) * 128], identb[:])
                        if i == 3:
                            inc(ti, "stt")
                            t.wait_ge(sems["stt"], cnt["stt"])
                    inc(t.nop(), "stt")
                    mark(f"ztrt{c}", "stt")

                @block.vector
                def _(v, c=c, ci=ci):
                    wait(v, f"ztrt{c}")
                    for b in range(4):
                        cp = v.tensor_copy(
                            zfm[:, b:8:4, ci * 128:(ci + 1) * 128],
                            tr2_ps[:, b * 1024:b * 1024 + 256]
                            .rearrange("p (a b) -> p a b", a=2))
                        if b == 3:
                            inc(cp, "sv")
                    mark(f"ztr{c}", "sv")

            for m in range(32):
                if m % 2 == 0:
                    m2 = m // 2
                    g1 = j * 16 + m2
                    @block.gpsimd
                    def _(g, j=j, m2=m2, g1=g1):
                        if g1 == 0:
                            wait(g, "f2prep")
                        if g1 >= 3:
                            pj, pm2 = (g1 - 3) // 16, (g1 - 3) % 16
                            wait(g, f"f1use{pj}_{2*pm2+1}")
                        src = f1bf_dram.ap()[:, :, m2 * 256:(m2 + 1) * 256] \
                            .rearrange("i p f -> p i f")
                        inc(g.dma_start(out=f1roll[:, g1 % 3, :, :], in_=src),
                            f"f1s{g1 % 3}", 16)
                        mark(f"f1ld{j}_{m2}", f"f1s{g1 % 3}")

                @block.tensor
                def _(t, j=j, m=m):
                    if m == 0:
                        wait(t, "gm_f32")
                        wait(t, f"ztr{4*j+3}")
                        if j > 0:
                            wait(t, f"y2_{4*(j-1)+3}")
                    wait(t, f"f1ld{j}_{m//2}")
                    if m >= 2:
                        wait(t, f"gelu{j}_{m-2}")
                    elif j > 0:
                        wait(t, f"gelu{j-1}_{30+m}")
                    for i in range(8):
                        inc(t.matmul(f1ps[m % 2],
                                 f1roll[:, (j * 16 + m // 2) % 3, i,
                                        (m % 2) * 128:(m % 2 + 1) * 128],
                                 zfm[:, i, :],
                                 start=(i == 0), stop=(i == 7)), "stt")
                    mark(f"f1mm{j}_{m}", "stt")
                    mark(f"f1use{j}_{m}", "stt")

                @block.scalar
                def _(s, j=j, m=m):
                    wait(s, f"f1mm{j}_{m}")
                    if j > 0:
                        wait(s, f"f2u{(j-1)*64+32+m}")
                    inc(s.activation(h_fm[:, m, :], f1ps[m % 2], AF.Gelu,
                                 bias=f1b_cols[:, m:m + 1], scale=1.0), "ss")
                    mark(f"gelu{j}_{m}", "ss")

            for sub in range(2):
                for m in range(32):
                    u = (j * 2 + sub) * 32 + m
                    if m % 2 == 0:
                        u2 = ((j * 2 + sub) * 32 + m) // 2

                        @block.sync
                        def _(sy, u2=u2, m=m):
                            if u2 == 0:
                                wait(sy, "f2prep")
                            if u2 >= 3:
                                wait(sy, f"f2u{2*(u2-3)+1}")
                            inc(sy.dma_start(
                                out=f2roll[:, u2 % 3, :, :],
                                in_=f2bf_dram.ap()[m:m + 2, :, :]
                                .rearrange("m p f -> p m f")), f"f2s{u2 % 3}", 16)
                            mark(f"f2ld{u2}", f"f2s{u2 % 3}")

                    @block.tensor
                    def _(t, j=j, sub=sub, m=m, u=u):
                        if m == 0:
                            wait(t, f"gelu{j}_31")
                            if sub == 0 and j > 0:
                                wait(t, f"y2_{4*(j-1)+3}")
                            if sub == 1:
                                wait(t, f"y2_{4*j+1}")
                        wait(t, f"f2ld{u//2}")
                        for ch2 in range(2):
                            for half in range(2):
                                inc(t.matmul(
                                    f2ps[ch2][:, half * 512:(half + 1) * 512],
                                    h_fm[:, m, (sub * 2 + ch2) * 128:(sub * 2 + ch2 + 1) * 128],
                                    f2roll[:, (u // 2) % 3, m % 2,
                                           half * 512:(half + 1) * 512],
                                    start=(m == 0), stop=False), "stt")
                        mark(f"f2u{u}", "stt")

                @block.tensor
                def _(t, j=j, sub=sub):
                    for ch2 in range(2):
                        for half in range(2):
                            inc(t.matmul(f2ps[ch2][:, half * 512:(half + 1) * 512],
                                     ones_row_bf[:],
                                     f2b_row[0:1, half * 512:(half + 1) * 512],
                                     start=False, stop=True), "stt")
                    mark(f"f2mm{j}_{sub}", "stt")

                for ch2 in range(2):
                    @block.vector
                    def _(v, j=j, sub=sub, ch2=ch2):
                        wait(v, f"f2mm{j}_{sub}")
                        c = 4 * j + sub * 2 + ch2
                        inc(v.scalar_tensor_tensor(y_t[:, c, :], f2ps[ch2][:], 1.0,
                                               y_t[:, c, :],
                                               op0=ALU.mult, op1=ALU.add), "sv")
                        mark(f"y2_{c}", "sv")

        if "tap_y" in tap_d and "E" not in stages:
            @block.gpsimd
            def _(g):
                wait(g, f"y2_{NCH-1}")
                for c in range(NCH):
                    inc(g.dma_start(out=tap_d["tap_y"][c * 128:(c + 1) * 128, :],
                                    in_=y_t[:, c, :]), "sg16", 16)
        p2p.close()

    if "E" in stages:
        p4p = ExitStack()
        wc = [p4p.enter_context(nc.psum_tensor(f"wc{i}", [128, 1024], F32))
              for i in range(2)]
        trp4_ps = p4p.enter_context(nc.psum_tensor("trp4_ps", [128, 4096], BF16))

        @block.scalar
        def _(s):
            for c in range(2):
                inc(s.dma_start(out=xs_st[:, c, :], in_=x_d[c * 128:(c + 1) * 128, :]),
                    f"sx{c % 2}", 16)
                mark(f"xld2_{c}", f"sx{c % 2}")

        for c in range(NCH):
            buf = c % 2

            @block.vector
            def _(v, c=c):
                wait(v, f"y2_{c}")

            stats_chain(y_t[:, c, 0:512], y_t[:, c, 512:1024], tag=f"e{c}",
                        sc=scol_t[:, c, :], bg=bnag_t[:, c, :])

            @block.scalar
            def _(s, c=c):
                wait(s, f"ssde{c}")
                inc(s.activation(tmpA[:], y_t[:, c, :], AF.Identity,
                             bias=scol_t[:, c, 1:2], scale=scol_t[:, c, 0:1]), "ss")
                mark(f"z3s{c}", "ss")

            @block.vector
            def _(v, c=c, buf=buf):
                wait(v, f"z3s{c}")
                wait(v, f"xld2_{c}")
                inc(v.tensor_tensor(y_t[:, c, :], tmpA[:], xs_st[:, buf, :], op=ALU.add), "sv")
                mark(f"x2_{c}", "sv")

            if c + 2 < NCH:
                @block.sync
                def _(sy, c=c):
                    wait(sy, f"x2_{c}")
                    inc(sy.dma_start(out=xs_st[:, c % 2, :],
                                     in_=x_d[(c + 2) * 128:(c + 3) * 128, :]),
                        f"sx{c % 2}", 16)
                    mark(f"xld2_{c+2}", f"sx{c % 2}")

            if "F" not in stages:
                continue

            stats_chain(y_t[:, c, 0:512], y_t[:, c, 512:1024], tag=f"f{c}",
                        sc=scol_t[:, c, :], bg=bnag_t[:, c, :])

            @block.scalar
            def _(s, c=c):
                wait(s, f"ssdf{c}")
                inc(s.activation(tmpB[:], y_t[:, c, :], AF.Identity,
                             bias=scol_t[:, c, 1:2], scale=scol_t[:, c, 0:1]), "ss")
                mark(f"z4s{c}", "ss")

            @block.vector
            def _(v, c=c):
                wait(v, f"z4s{c}")
                if c >= 2:
                    wait(v, f"wtrt{c-2}")
                inc(v.tensor_copy(xnbf[:, c % 2, :], tmpB[:]), "sv")
                mark(f"z4v{c}", "sv")

            @block.tensor
            def _(t, c=c):
                wait(t, f"z4v{c}")
                if c > 0:
                    wait(t, f"wtr{c-1}")
                elif "D" in stages:
                    wait(t, "f1mm3_31")
                if c >= 4:
                    wait(t, f"wcmm{c-4}")
                for i in range(8):
                    col = (i % 4) * 1024 + (i // 4) * 128
                    ti = t.transpose(trp4_ps[:, col:col + 128],
                                     xnbf[:, c % 2, i * 128:(i + 1) * 128], identb[:])
                    if i == 3:
                        inc(ti, "stt")
                        t.wait_ge(sems["stt"], cnt["stt"])
                inc(t.nop(), "stt")
                mark(f"wtrt{c}", "stt")

            @block.vector
            def _(v, c=c):
                wait(v, f"wtrt{c}")
                for b in range(4):
                    cp = v.tensor_copy(
                        zfm[:, b:8:4, (c % 4) * 128:(c % 4 + 1) * 128],
                        trp4_ps[:, b * 1024:b * 1024 + 256]
                        .rearrange("p (a b) -> p a b", a=2))
                    if b == 3:
                        inc(cp, "sv")
                mark(f"wtr{c}", "sv")

            @block.tensor
            def _(t, c=c):
                wait(t, f"wtr{c}")
                if c == 0:
                    wait(t, "mc_loaded")
                if c > 1:
                    wait(t, f"wfin{c-2}")
                for i in range(8):
                    for half in range(2):
                        t.matmul(wc[c % 2][:, half * 512:(half + 1) * 512],
                                 zfm[:, i, (c % 4) * 128:(c % 4 + 1) * 128],
                                 mc_bf[:, i, half * 512:(half + 1) * 512],
                                 start=(i == 0), stop=False)
                for half in range(2):
                    inc(t.matmul(wc[c % 2][:, half * 512:(half + 1) * 512], ones_row_bf[:],
                             fbias_row[0:1, half * 512:(half + 1) * 512],
                             start=False, stop=True), "stt")
                mark(f"wcmm{c}", "stt")

            @block.vector
            def _(v, c=c):
                wait(v, f"wcmm{c}")

            stats_chain(wc[c % 2][:, 0:512], wc[c % 2][:, 512:1024], tag=f"w{c}",
                        sc=scol_t[:, c, :], bg=bnag_t[:, c, :])

            @block.scalar
            def _(s, c=c):
                wait(s, f"ssdw{c}")
                inc(s.activation(tmpA_w[:], wc[c % 2][:], AF.Identity,
                             bias=scol_t[:, c, 1:2], scale=scol_t[:, c, 0:1]), "ss")
                mark(f"wact{c}", "ss")

            @block.vector
            def _(v, c=c):
                wait(v, f"wact{c}")
                if c >= 2:
                    wait(v, f"outst{c-2}")
                inc(v.tensor_tensor(out_sb[:, (c % 2) * 1024:(c % 2 + 1) * 1024], tmpA_w[:], y_t[:, c, :], op=ALU.add), "sv")
                mark(f"wfin{c}", "sv")

            @block.sync
            def _(sy, c=c):
                wait(sy, f"wfin{c}")
                inc(sy.dma_start(out=out_d[c * 128:(c + 1) * 128, :],
                                 in_=out_sb[:, (c % 2) * 1024:(c % 2 + 1) * 1024]), f"so{c % 2}", 16)
                mark(f"outst{c}", f"so{c % 2}")

        if "tap_y" in tap_d and "F" not in stages:
            @block.gpsimd
            def _(g):
                wait(g, f"x2_{NCH-1}")
                for c in range(NCH):
                    inc(g.dma_start(out=tap_d["tap_y"][c * 128:(c + 1) * 128, :],
                                    in_=y_t[:, c, :]), "sg16", 16)

        if "F" in stages:
            @block.vector
            def _(v):
                v.wait_ge(sems["so0"], cnt["so0"])
                v.wait_ge(sems["so1"], cnt["so1"])
        p4p.close()

    @block.gpsimd
    def _(g):
        g.wait_ge(sems["sg16"], cnt["sg16"])

    st.close()
    return nc


def _prep_inmaps(inputs):
    f32 = np.float32
    x = np.asarray(inputs["L0_attn_x"], f32)
    qw = np.asarray(inputs["L0_attn_q_w"], f32)
    kw = np.asarray(inputs["L0_attn_k_w"], f32)
    vw = np.asarray(inputs["L0_attn_v_w"], f32)
    ow = np.asarray(inputs["L0_attn_o_w"], f32)
    f1w = np.asarray(inputs["L0_attn_f1_w"], f32)
    f2w = np.asarray(inputs["L0_attn_f2_w"], f32)
    qkw_t = np.ascontiguousarray(np.concatenate([qw, kw], axis=0).T)
    vw_t = np.ascontiguousarray(vw.T)
    ow_t = np.ascontiguousarray(ow.T)
    f1w_t = np.ascontiguousarray(f1w.T)
    f2w_t = np.ascontiguousarray(f2w.T)

    rows = np.zeros((9, 1024), f32)
    rows[0] = np.asarray(inputs["L0_attn_ln1_w"], f32)
    rows[1] = np.asarray(inputs["L0_attn_ln1_b"], f32)
    rows[2] = np.asarray(inputs["L0_attn_ln2_w"], f32)
    rows[3] = np.asarray(inputs["L0_attn_ln2_b"], f32)
    rows[4] = np.asarray(inputs["L0_attn_o_b"], f32)
    rows[5] = np.asarray(inputs["L0_attn_v_b"], f32)
    rows[6] = np.asarray(inputs["L0_attn_f2_b"], f32)
    rows[7] = np.concatenate([np.asarray(inputs["L0_ffn_bias_real"], f32),
                              np.asarray(inputs["L0_ffn_bias_imag"], f32)])
    rows[8, 0:128] = np.concatenate([np.asarray(inputs["L0_attn_q_b"], f32),
                                     np.asarray(inputs["L0_attn_k_b"], f32)])

    f1b_cols = np.ascontiguousarray(
        np.asarray(inputs["L0_attn_f1_b"], f32).reshape(32, 128).T)

    G1R = np.asarray(inputs["L0_ffn_G1_R"], f32)
    G2R = np.asarray(inputs["L0_ffn_G2_R"], f32)
    G1I = np.asarray(inputs["L0_ffn_G1_I"], f32)
    G2I = np.asarray(inputs["L0_ffn_G2_I"], f32)
    g1m = np.ascontiguousarray(np.concatenate(
        [G1R[0].transpose(1, 0, 2).reshape(4, 1024),
         G1I[0].transpose(1, 0, 2).reshape(4, 1024)], axis=1))
    g2m = np.ascontiguousarray(np.concatenate(
        [G2R[:, :, 0, :].transpose(0, 2, 1).reshape(4, 256),
         G2I[:, :, 0, :].transpose(0, 2, 1).reshape(4, 256)], axis=1))

    in_maps = []
    for core in range(NCORES):
        b, q = core // QPER, core % QPER
        w_pre = np.zeros((128, 8), f32)
        for cc in range(NCORES):
            if cc // QPER == b and cc % QPER < q:
                w_pre[:, cc] = 1.0
        in_maps.append({
            "x": np.ascontiguousarray(x[b, q * T:(q + 1) * T]),
            "qkw_t": qkw_t, "vw_t": vw_t, "ow_t": ow_t,
            "f1w_t": f1w_t, "f2w_t": f2w_t,
            "g1m": g1m, "g2m": g2m, "rows": rows, "f1b_cols": f1b_cols,
            "w_pre": w_pre,
        })
    return in_maps


_NC_CACHE = {}


def kernel(**inputs):
    key = ("ABCDEF", ())
    if key not in _NC_CACHE:
        _NC_CACHE[key] = build_nc(taps=(), stages="ABCDEF")
    nc = _NC_CACHE[key]
    in_maps = _prep_inmaps(inputs)
    res = run_bass_kernel_spmd(nc, in_maps, core_ids=list(range(NCORES)))
    out = np.zeros((B, N, D), np.float32)
    for core in range(NCORES):
        b, q = core // QPER, core % QPER
        out[b, q * T:(q + 1) * T] = res.results[core]["out"]
    return out


if __name__ == "__main__":
    nc = build_nc()
    print("build ok")


# version 06
# speedup vs baseline: 1.0575x; 1.0417x over previous
import numpy as np
from contextlib import ExitStack

import concourse.bass as bass
import concourse.mybir as mybir
from concourse.bass_utils import run_bass_kernel_spmd

F32 = mybir.dt.float32
BF16 = mybir.dt.bfloat16
I32 = mybir.dt.int32
AF = mybir.ActivationFunctionType
ALU = mybir.AluOpType
AX = mybir.AxisListType

LN_EPS = 1e-5
ATTN_EPS = 1e-6

B, N, D, FD = 2, 8192, 1024, 64
NCORES = 8
QPER = 4
T = N // QPER
NCH = T // 128
DH = D // 128
MH = (4 * D) // 128


def build_nc(taps=(), stages="ABCDEF"):
    nc = bass.Bass("TRN2", target_bir_lowering=False,
                   detect_race_conditions=False)

    x_d = nc.dram_tensor("x", [T, D], F32, kind="ExternalInput").ap()
    qkw_d = nc.dram_tensor("qkw_t", [D, 128], F32, kind="ExternalInput").ap()
    vw_d = nc.dram_tensor("vw_t", [D, D], F32, kind="ExternalInput").ap()
    ow_d = nc.dram_tensor("ow_t", [D, D], F32, kind="ExternalInput").ap()
    f1w_d = nc.dram_tensor("f1w_t", [D, 4 * D], F32, kind="ExternalInput").ap()
    f2w_d = nc.dram_tensor("f2w_t", [4 * D, D], F32, kind="ExternalInput").ap()
    g1m_d = nc.dram_tensor("g1m", [4, 2048], F32, kind="ExternalInput").ap()
    g2m_d = nc.dram_tensor("g2m", [4, 512], F32, kind="ExternalInput").ap()
    rows_d = nc.dram_tensor("rows", [9, 1024], F32, kind="ExternalInput").ap()
    f1bc_d = nc.dram_tensor("f1b_cols", [128, 32], F32, kind="ExternalInput").ap()
    wpre_d = nc.dram_tensor("w_pre", [128, 8], F32, kind="ExternalInput").ap()
    out_d = nc.dram_tensor("out", [T, D], F32, kind="ExternalOutput").ap()

    tap_d = {}
    for tname, shape, dt in [("tap_y", [T, D], F32), ("tap_pay", [128, 576], F32),
                             ("tap_xnfm", [128, 8, 2048], BF16),
                             ("tap_kvpre", [64, 1024], BF16),
                             ("tap_qke", [128, NCH * 128], BF16),
                             ("tap_mc", [128, 8, 1024], BF16)]:
        if tname in taps:
            tap_d[tname] = nc.dram_tensor(tname, shape, dt, kind="ExternalOutput").ap()

    pay_dram = nc.dram_tensor("pay_dram", [128, 576], F32, kind="Internal")
    ag_dram = nc.dram_tensor("ag_dram", [8, 128, 576], F32, kind="Internal")
    mc_dram = nc.dram_tensor("mc_dram", [1024, 1024], BF16, kind="Internal")
    f1bf_dram = nc.dram_tensor("f1bf_dram", [8, 128, 4096], BF16, kind="Internal")
    f2bf_dram = nc.dram_tensor("f2bf_dram", [32, 128, 1024], BF16, kind="Internal")

    st = ExitStack()
    e = st.enter_context

    y_t = e(nc.sbuf_tensor("y_t", [128, NCH, 1024], F32))
    qke_t = e(nc.sbuf_tensor("qke_t", [128, NCH, 128], BF16))
    qf32_t = e(nc.sbuf_tensor("qf32_t", [128, NCH, 64], F32))
    tmpA_w = qf32_t[:].rearrange("p a b -> p (a b)")
    g1rep = e(nc.sbuf_tensor("g1rep", [128, 1024], BF16))
    b1rep = e(nc.sbuf_tensor("b1rep", [128, 1024], BF16))
    g2rep = e(nc.sbuf_tensor("g2rep", [128, 1024], BF16))
    b2rep = e(nc.sbuf_tensor("b2rep", [128, 1024], BF16))
    ob_rep = e(nc.sbuf_tensor("ob_rep", [128, 1024], BF16))
    f1b_cols = e(nc.sbuf_tensor("f1b_colsb", [128, 32], F32))
    mask_bf = e(nc.sbuf_tensor("mask_bf", [128, 128], BF16))
    identb = e(nc.sbuf_tensor("identb", [128, 128], BF16))
    ones_row_bf = e(nc.sbuf_tensor("ones_row_bf", [1, 128], BF16))
    ones_col_bf = e(nc.sbuf_tensor("ones_col_bf", [128, 1], BF16))
    ones_row_f32 = e(nc.sbuf_tensor("ones_row_f32", [1, 128], F32))
    qkb_row = e(nc.sbuf_tensor("qkb_row", [1, 128], BF16))
    vb_row = e(nc.sbuf_tensor("vb_row", [1, 1024], BF16))
    f2b_row = e(nc.sbuf_tensor("f2b_row", [1, 1024], BF16))
    fbias_row = e(nc.sbuf_tensor("fbias_row", [1, 1024], BF16))
    wpre_sb = e(nc.sbuf_tensor("wpre_sb", [128, 8], F32))
    kv_sb = e(nc.sbuf_tensor("kv_sb", [64, 1024], BF16))
    kvacc = e(nc.sbuf_tensor("kvacc", [64, 1024], F32))
    ks_run = e(nc.sbuf_tensor("ks_run", [1, 64], F32))
    ks_rep = e(nc.sbuf_tensor("ks_rep", [128, 64], F32))
    q0pre_bf = e(nc.sbuf_tensor("q0pre_bf", [1, 64], BF16))
    bnst = e(nc.sbuf_tensor("bnst", [128, 2, 6], F32))
    scol_t = e(nc.sbuf_tensor("scol_t", [128, NCH, 8], F32))
    bnag_t = e(nc.sbuf_tensor("bnag_t", [128, NCH, 2], F32))
    elu_sc = e(nc.sbuf_tensor("elu_sc", [128, 256], F32))
    eps_col = e(nc.sbuf_tensor("eps_col", [128, 1], F32))
    den_f = e(nc.sbuf_tensor("den_f", [128, 1], F32))
    rden = e(nc.sbuf_tensor("rden", [128, 1], F32))
    den2 = e(nc.sbuf_tensor("den2", [128, 1], F32))
    tmpA = e(nc.sbuf_tensor("tmpA", [128, 1024], F32))
    tmpB = e(nc.sbuf_tensor("tmpB", [128, 1024], F32))
    xs_st = e(nc.sbuf_tensor("xs_st", [128, 2, 1024], F32))
    wstage = e(nc.sbuf_tensor("wstage", [128, 2048], F32))
    xnbf = e(nc.sbuf_tensor("xnbf", [128, 2, 1024], BF16))

    p1s = ExitStack()
    iota_t = p1s.enter_context(nc.sbuf_tensor("iota_t", [128, 128], I32))
    qzk = p1s.enter_context(nc.sbuf_tensor("qzk", [128, 128], BF16))
    am_bf = p1s.enter_context(nc.sbuf_tensor("am_bf", [128, 128], BF16))
    vtok = p1s.enter_context(nc.sbuf_tensor("vtok", [128, 2, 1024], BF16))
    num_fm = p1s.enter_context(nc.sbuf_tensor("num_fm", [128, 8, 128], BF16))
    qfm_sb = p1s.enter_context(nc.sbuf_tensor("qfm_sb", [64, 128], BF16))
    kfm_sb = p1s.enter_context(nc.sbuf_tensor("kfm_sb", [64, 128], BF16))
    xn_fm = p1s.enter_context(nc.sbuf_tensor("xn_fm", [128, 8, 2048], BF16))
    vw_bf = p1s.enter_context(nc.sbuf_tensor("vw_bf", [128, 8, 1024], BF16))
    ow_bf = p1s.enter_context(nc.sbuf_tensor("ow_bf", [128, 8, 1024], BF16))
    qkw_bf = p1s.enter_context(nc.sbuf_tensor("qkw_bf", [128, 8, 128], BF16))
    ttx = p1s.enter_context(nc.sbuf_tensor("ttx", [128, 256], BF16))
    gm_sb = p1s.enter_context(nc.sbuf_tensor("gm_sb", [4, 2560], BF16))
    ptpre = p1s.enter_context(nc.sbuf_tensor("ptpre", [128, 8, 64], F32))
    ptpre_bf = num_fm[:, :, 0:64]
    p1s.close()

    mc_bf = e(nc.sbuf_tensor("mc_bf", [128, 8, 1024], BF16))
    zfm = e(nc.sbuf_tensor("zfm", [128, 8, 512], BF16))
    h_fm = e(nc.sbuf_tensor("h_fm", [128, 32, 512], BF16))
    f1roll = e(nc.sbuf_tensor("f1roll", [128, 3, 8, 256], BF16))
    f2roll = e(nc.sbuf_tensor("f2roll", [128, 3, 2, 1024], BF16))
    out_sb = wstage

    pAp = ExitStack()
    repvp_ps = pAp.enter_context(nc.psum_tensor("repvp_ps", [128, 1024], F32))
    pt_ps = pAp.enter_context(nc.psum_tensor("pt_ps", [128, 1024], F32))
    trA_ps = pAp.enter_context(nc.psum_tensor("trA_ps", [128, 4096], BF16))
    vp_ps = repvp_ps

    semnames = ["sx0", "sx1", "sg16", "sgm", "smc", "str_", "so0", "so1", "sv", "ss", "stt", "sgc", "scc", "f1s0", "f1s1", "f1s2", "f1s3", "f2s0", "f2s1", "f2s2", "f2s3"]
    sems = {n: e(nc.semaphore(n)) for n in semnames}
    cnt = {n: 0 for n in semnames}
    land = {}

    def inc(instr, s, amt=1):
        cnt[s] += amt
        instr.then_inc(sems[s], amt)
        return cnt[s]

    def mark(key, s):
        land[key] = (s, cnt[s])

    agstage = tmpA[:, 0:576]
    payload = tmpB[:, 0:576]
    row_st = wstage[0:1, 1024:2048]

    def wait(eng, key):
        s, v = land[key]
        eng.wait_ge(sems[s], v)

    block = e(nc.Block())

    @block.gpsimd
    def _(g):
        inc(g.iota(iota_t[:], pattern=[[1, 128]], base=0, channel_multiplier=-1), "sgc")
        inc(g.memset(qzk[:], 0.0), "sgc")
        inc(g.memset(ones_row_bf[:], 1.0), "sgc")
        inc(g.memset(ones_col_bf[:], 1.0), "sgc")
        inc(g.memset(ones_row_f32[:], 1.0), "sgc")
        inc(g.memset(ks_run[:], 0.0), "sgc")
        inc(g.memset(eps_col[:], LN_EPS), "sgc")
        mark("iota", "sgc")
        inc(g.dma_start(out=f1b_cols[:], in_=f1bc_d), "sgm", 16)
        inc(g.dma_start(out=wpre_sb[:], in_=wpre_d), "sgm", 16)
        inc(g.dma_start(out=tmpA[0:4, :], in_=g1m_d[:, 0:1024]), "sgm", 16)
        inc(g.dma_start(out=tmpB[0:4, :], in_=g1m_d[:, 1024:2048]), "sgm", 16)
        inc(g.dma_start(out=xs_st[0:4, 0, 0:512], in_=g2m_d), "sgm", 16)
        mark("gm_f32", "sgm")
        inc(g.dma_start(out=wstage[:, 0:1024].rearrange("p (c m) -> p c m", c=8),
                        in_=qkw_d.rearrange("(c k) m -> k c m", k=128)), "sg16", 16)
        mark("qkw_f32", "sg16")
        g.wait_ge(sems["sg16"], cnt["sg16"])
        inc(g.tensor_copy(qkw_bf[:], wstage[:, 0:1024]
                          .rearrange("p (c m) -> p c m", c=8)), "sgc")
        g.wait_ge(sems["sgc"], cnt["sgc"])
        inc(g.nop(), "sgc")
        mark("qkw_bf", "sgc")

    @block.scalar
    def _(s):
        wait(s, "iota")

    @block.vector
    def _(v):
        wait(v, "iota")
        v.tensor_scalar(mask_bf[:], iota_t[:], 0, None, op0=ALU.is_ge)
        inc(v.tensor_scalar(identb[:], iota_t[:], 0, None, op0=ALU.is_equal), "sv")
        mark("masks", "sv")
        wait(v, "gm_f32")
        v.tensor_copy(gm_sb[0:4, 0:1024], tmpA[0:4, :])
        v.tensor_copy(gm_sb[0:4, 1024:2048], tmpB[0:4, :])
        inc(v.tensor_copy(gm_sb[0:4, 2048:2560], xs_st[0:4, 0, 0:512]), "sv")
        mark("gm_bf", "sv")

    rep_specs = [(0, g1rep), (1, b1rep), (2, g2rep), (3, b2rep), (4, ob_rep)]
    for ri, (rowi, dst) in enumerate(rep_specs):
        @block.gpsimd
        def _(g, ri=ri, rowi=rowi):
            if ri > 0:
                wait(g, f"rep{ri-1}_t")
            inc(g.dma_start(out=row_st[:], in_=rows_d[rowi:rowi + 1, :]), "sg16", 16)
            mark(f"rep{ri}_row", "sg16")

        @block.tensor
        def _(t, ri=ri):
            wait(t, f"rep{ri}_row")
            if ri == 0:
                wait(t, "iota")
            if ri > 0:
                wait(t, f"rep{ri-1}_v")
            for half in range(2):
                inc(t.matmul(vp_ps[:, half * 512:(half + 1) * 512], ones_row_f32[:],
                         row_st[0:1, half * 512:(half + 1) * 512], start=True, stop=True), "stt")
            mark(f"rep{ri}_t", "stt")

        @block.vector
        def _(v, ri=ri, dst=dst):
            wait(v, f"rep{ri}_t")
            inc(v.tensor_copy(dst[:], vp_ps[:]), "sv")
            mark(f"rep{ri}_v", "sv")

    bfrow_specs = [(8, qkb_row, 128), (5, vb_row, 1024), (6, f2b_row, 1024),
                   (7, fbias_row, 1024)]
    for bi, (rowi, dst, w) in enumerate(bfrow_specs):
        @block.gpsimd
        def _(g, bi=bi, rowi=rowi):
            wait(g, "rep4_v" if bi == 0 else f"brow{bi-1}_v")
            inc(g.dma_start(out=row_st[:], in_=rows_d[rowi:rowi + 1, :]), "sg16", 16)
            mark(f"brow{bi}", "sg16")

        @block.vector
        def _(v, bi=bi, dst=dst, w=w):
            wait(v, f"brow{bi}")
            inc(v.tensor_copy(dst[:], row_st[0:1, 0:w]), "sv")
            mark(f"brow{bi}_v", "sv")
    ROWS_DONE = f"brow{len(bfrow_specs)-1}_v"

    mc_view = mc_dram.ap().rearrange("(rh d g) (ch b e) -> rh ch b d g e",
                                     rh=2, d=32, g=16, ch=2, b=32, e=16)

    def tt_chain(imag):
        g1o = 1024 if imag else 0
        g2o = 2304 if imag else 2048
        tag = "i" if imag else "r"
        for p in range(8):
            @block.tensor
            def _(t, p=p):
                if p == 0 and imag:
                    wait(t, "gm_bf")
                    wait(t, "rep4_v")
                if p == 0 and not imag:
                    wait(t, "ttevi7")
                if p > 0:
                    wait(t, f"ttev{tag}{p-1}")
                inc(t.matmul(vp_ps[:, 0:256], gm_sb[0:4, g1o + p * 128:g1o + (p + 1) * 128],
                         gm_sb[0:4, g2o:g2o + 256], start=True, stop=True), "stt")
                mark(f"ttmm{tag}{p}", "stt")

            @block.vector
            def _(v, p=p):
                wait(v, f"ttmm{tag}{p}")
                if p > 0:
                    wait(v, f"ttst{tag}{p-1}")
                elif not imag:
                    wait(v, "ttsti7")
                inc(v.tensor_copy(ttx[:], vp_ps[:, 0:256]), "sv")
                mark(f"ttev{tag}{p}", "sv")

            if imag:
                @block.gpsimd
                def _(g, p=p):
                    wait(g, f"ttev{tag}{p}")
                    for bq in range(4):
                        src_ = ttx[bq * 32:(bq + 1) * 32, :].rearrange(
                            "d (g e) -> d g e", g=16)
                        inc(g.dma_start(out=mc_view[0, 1, 4 * p + bq, :, :, :],
                                        in_=src_), "sg16", 16)
                    mark(f"ttstwi{p}", "sg16")

                @block.scalar
                def _(s, p=p):
                    wait(s, f"ttstwi{p}")
                    inc(s.mul(ttx[:], ttx[:], -1.0), "ss")
                    mark(f"ttneg{p}", "ss")

                @block.gpsimd
                def _(g, p=p):
                    wait(g, f"ttneg{p}")
                    for bq in range(4):
                        src_ = ttx[bq * 32:(bq + 1) * 32, :].rearrange(
                            "d (g e) -> d g e", g=16)
                        inc(g.dma_start(out=mc_view[1, 0, 4 * p + bq, :, :, :],
                                        in_=src_), "sg16", 16)
                    mark(f"ttst{tag}{p}", "sg16")
            else:
                @block.gpsimd
                def _(g, p=p):
                    wait(g, f"ttev{tag}{p}")
                    for bq in range(4):
                        src_ = ttx[bq * 32:(bq + 1) * 32, :].rearrange(
                            "d (g e) -> d g e", g=16)
                        inc(g.dma_start(out=mc_view[0, 0, 4 * p + bq, :, :, :],
                                        in_=src_), "sg16", 16)
                        inc(g.dma_start(out=mc_view[1, 1, 4 * p + bq, :, :, :],
                                        in_=src_), "sg16", 16)
                    mark(f"ttst{tag}{p}", "sg16")

    tt_chain(imag=True)
    tt_chain(imag=False)

    @block.gpsimd
    def _(g):
        mark("st_wr", "sg16")
        for nm, src_d, dst in [("vw", vw_d, vw_bf), ("ow", ow_d, ow_bf)]:
            for i0 in range(0, 8, 2):
                g.wait_ge(sems["sgc"], cnt["sgc"])
                inc(g.dma_start(out=wstage[:].rearrange("p (c m) -> p c m", c=2),
                                in_=src_d[i0 * 128:(i0 + 2) * 128, :]
                                .rearrange("(c k) m -> k c m", k=128)), "sg16", 16)
                g.wait_ge(sems["sg16"], cnt["sg16"])
                inc(g.tensor_copy(dst[:, i0:i0 + 2, :],
                                  wstage[:].rearrange("p (c m) -> p c m", c=2)), "sgc")
        g.wait_ge(sems["sgc"], cnt["sgc"])
        inc(g.nop(), "sgc")
        mark("vwow_bf", "sgc")

    def stats_chain(src_hi, src_lo, tag, sc, bg, s2=False):
        @block.vector
        def _(v):
            v.wait_ge(sems["sv"], cnt["sv"])
            v.bn_stats(bnst[:, 0, :], src_hi)
            inc(v.bn_stats(bnst[:, 1, :], src_lo), "sv")
            v.drain()
            v.wait_ge(sems["sv"], cnt["sv"])
            inc(v.bn_aggr(bg, bnst[:]), "sv")
            mark(f"sa{tag}", "sv")

        @block.scalar
        def _(s):
            wait(s, f"sa{tag}")
            inc(s.activation(sc[:, 4:5], bg[:, 1:2], AF.Sqrt, bias=eps_col[:]), "ss")
            mark(f"sq{tag}", "ss")

        @block.vector
        def _(v):
            wait(v, f"sq{tag}")
            v.reciprocal(sc[:, 0:1], sc[:, 4:5])
            v.drain()
            inc(v.scalar_tensor_tensor(sc[:, 1:2], bg[:, 0:1], -1.0, sc[:, 0:1],
                                       op0=ALU.mult, op1=ALU.mult), "sv")
            if s2:
                v.drain()
                v.wait_ge(sems["sv"], cnt["sv"])
                v.scalar_tensor_tensor(sc[:, 5:6], bg[:, 1:2], 1.0, sc[:, 0:1],
                                       op0=ALU.mult, op1=ALU.mult)
                v.drain()
                inc(v.scalar_tensor_tensor(sc[:, 6:7], sc[:, 5:6], 1.0, sc[:, 0:1],
                                           op0=ALU.mult, op1=ALU.mult), "sv")
                mark(f"su{tag}", "sv")
            else:
                mark(f"ssd{tag}", "sv")

        if s2:
            @block.scalar
            def _(s):
                wait(s, f"su{tag}")
                inc(s.activation(sc[:, 7:8], sc[:, 6:7], AF.Sqrt, bias=eps_col[:]), "ss")
                mark(f"ssd{tag}", "ss")

    if "A" in stages:
        @block.scalar
        def _(s):
            for c in range(2):
                inc(s.dma_start(out=xs_st[:, c, :], in_=x_d[c * 128:(c + 1) * 128, :]),
                    f"sx{c % 2}", 16)
                mark(f"xld{c}", f"sx{c % 2}")

        for c in range(NCH):
            buf = c % 2

            @block.vector
            def _(v, c=c):
                wait(v, f"xld{c}")
                if c == 0:
                    wait(v, "rep4_v")
                    wait(v, "gm_bf")

            stats_chain(xs_st[:, buf, 0:512], xs_st[:, buf, 512:1024],
                        tag=f"a{c}", sc=scol_t[:, c, :], bg=bnag_t[:, c, :], s2=True)
            @block.vector
            def _(v, c=c, buf=buf):
                wait(v, f"ssda{c}")
                v.wait_ge(sems["sv"], cnt["sv"])
                v.reciprocal(scol_t[:, c, 2:3], scol_t[:, c, 7:8])
                v.tensor_scalar(tmpA[:], xs_st[:, buf, :], scol_t[:, c, 0:1],
                                scol_t[:, c, 1:2], op0=ALU.mult, op1=ALU.add)
                v.drain()
                v.tensor_tensor(y_t[:, c, :], tmpA[:], ob_rep[:], op=ALU.add)
                v.scalar_tensor_tensor(tmpB[:], tmpA[:], scol_t[:, c, 2:3], g1rep[:],
                                       op0=ALU.mult, op1=ALU.mult)
                v.drain()
                if c >= 2:
                    wait(v, f"xtrt{c-2}")
                    wait(v, f"pt{c-2}")
                inc(v.tensor_tensor(xnbf[:, c % 2, :], tmpB[:], b1rep[:], op=ALU.add), "sv")
                mark(f"xnv{c}", "sv")

            if c + 2 < NCH:
                @block.scalar
                def _(s, c=c):
                    wait(s, f"xnv{c}")
                    inc(s.dma_start(out=xs_st[:, c % 2, :],
                                    in_=x_d[(c + 2) * 128:(c + 3) * 128, :]),
                        f"sx{c % 2}", 16)
                    mark(f"xld{c+2}", f"sx{c % 2}")

            @block.tensor
            def _(t, c=c):
                wait(t, f"xnv{c}")
                if c == 0:
                    wait(t, "masks")
                if c > 0:
                    wait(t, f"xtr{c-1}")
                for i in range(8):
                    col = (i % 4) * 1024 + (i // 4) * 128
                    ti = t.transpose(trA_ps[:, col:col + 128],
                                     xnbf[:, c % 2, i * 128:(i + 1) * 128], identb[:])
                    if i == 3:
                        inc(ti, "stt")
                        t.wait_ge(sems["stt"], cnt["stt"])
                inc(t.nop(), "stt")
                mark(f"xtrt{c}", "stt")

            @block.vector
            def _(v, c=c):
                wait(v, f"xtrt{c}")
                for b in range(4):
                    cp = v.tensor_copy(
                        xn_fm[:, b:8:4, c * 128:(c + 1) * 128],
                        trA_ps[:, b * 1024:b * 1024 + 256]
                        .rearrange("p (a b) -> p a b", a=2))
                    if b == 3:
                        inc(cp, "sv")
                mark(f"xtr{c}", "sv")

            @block.tensor
            def _(t, c=c):
                wait(t, f"xtr{c}")
                if c == 0:
                    wait(t, "qkw_bf")
                    wait(t, "brow0_v")
                    wait(t, "iota")
                if c == 0:
                    wait(t, "ttevr7")
                if c > 0:
                    wait(t, f"eluv{c-1}")
                for i in range(8):
                    t.matmul(repvp_ps[:, 0:128], xn_fm[:, i, c * 128:(c + 1) * 128],
                             qkw_bf[:, i, :], start=(i == 0), stop=False)
                inc(t.matmul(repvp_ps[:, 0:128], ones_row_bf[:], qkb_row[:],
                         start=False, stop=True), "stt")
                mark(f"qkmm{c}", "stt")

            @block.vector
            def _(v, c=c):
                wait(v, f"qkmm{c}")
                inc(v.tensor_scalar(elu_sc[:, 0:128], repvp_ps[:, 0:128], 0.0, None, op0=ALU.min), "sv")
                mark(f"elumin{c}", "sv")

            @block.scalar
            def _(s, c=c):
                wait(s, f"elumin{c}")
                inc(s.activation(elu_sc[:, 128:256], elu_sc[:, 0:128], AF.Exp), "ss")
                mark(f"eluexp{c}", "ss")

            @block.vector
            def _(v, c=c):
                wait(v, f"eluexp{c}")
                v.scalar_tensor_tensor(elu_sc[:, 0:128], repvp_ps[:, 0:128], 0.0,
                                       elu_sc[:, 128:256], op0=ALU.max, op1=ALU.add)
                v.drain()
                v.tensor_copy(qke_t[:, c, :], elu_sc[:, 0:128])
                inc(v.tensor_copy(qf32_t[:, c, :], elu_sc[:, 0:64]), "sv")
                mark(f"eluv{c}", "sv")

            @block.tensor
            def _(t, c=c):
                wait(t, f"eluv{c}")
                for i in range(8):
                    t.matmul(pt_ps[:, i * 64:(i + 1) * 64],
                             xnbf[:, c % 2, i * 128:(i + 1) * 128],
                             qke_t[:, c, 64:128],
                             start=(c == 0 and i == 0),
                             stop=(c == NCH - 1 and i == 7))
                inc(t.matmul(pt_ps[0:1, 512:576], ones_col_bf[:], qke_t[:, c, 64:128],
                         start=(c == 0), stop=(c == NCH - 1)), "stt")
                mark(f"pt{c}", "stt")

        @block.vector
        def _(v):
            wait(v, f"pt{NCH-1}")
            v.tensor_copy(payload[:, 0:512], pt_ps[:, 0:512])
            inc(v.tensor_copy(payload[0:1, 512:576], pt_ps[0:1, 512:576]), "sv")
            mark("payv", "sv")

        if "tap_xnfm" in tap_d:
            @block.gpsimd
            def _(g):
                wait(g, f"pt{NCH-1}")
                inc(g.dma_start(out=tap_d["tap_xnfm"], in_=xn_fm[:]), "sg16", 16)

        if "tap_pay" in tap_d:
            @block.gpsimd
            def _(g):
                wait(g, "payv")
                inc(g.dma_start(out=tap_d["tap_pay"], in_=payload[:]), "sg16", 16)
        if "tap_qke" in tap_d:
            @block.gpsimd
            def _(g):
                wait(g, f"eluv{NCH-1}")
                inc(g.dma_start(out=tap_d["tap_qke"]
                                .rearrange("p (a b) -> p a b", b=128),
                                in_=qke_t[:]), "sg16", 16)
        if "tap_y" in tap_d and "C" not in stages:
            @block.gpsimd
            def _(g):
                wait(g, f"xnv{NCH-1}")
                for c in range(NCH):
                    inc(g.dma_start(out=tap_d["tap_y"][c * 128:(c + 1) * 128, :],
                                    in_=y_t[:, c, :]), "sg16", 16)

    if "B" in stages or "C" in stages:
        pAp.close()
        p1p = ExitStack()
        kv_ps = p1p.enter_context(nc.psum_tensor("kv_ps", [64, 1024], F32))
        vp_ps = p1p.enter_context(nc.psum_tensor("vp_ps", [128, 1024], F32))
        no_ps = p1p.enter_context(nc.psum_tensor("no_ps", [128, 1024], F32))
        m1_ps = p1p.enter_context(nc.psum_tensor("m1_ps", [128, 512], F32))
        trB_ps = p1p.enter_context(nc.psum_tensor("trB_ps", [128, 1024], BF16))

    if "B" in stages:
        @block.gpsimd
        def _(g):
            wait(g, "payv")
            inc(g.dma_start(out=pay_dram.ap(), in_=payload[:]), "sg16", 16)
            mark("paydram", "sg16")
            wait(g, "paydram")
            inc(g.collective_compute(
                "AllGather", ALU.bypass, replica_groups=[list(range(NCORES))],
                ins=[pay_dram.ap().opt()], outs=[ag_dram.ap().opt()]), "scc")
            mark("ag", "scc")
            wait(g, "ag")

        for cc in range(NCORES):
            @block.gpsimd
            def _(g, cc=cc):
                if cc > 0:
                    wait(g, f"agacc{cc-1}")
                inc(g.dma_start(out=agstage[:], in_=ag_dram.ap()[cc, :, :]), "sg16", 16)
                mark(f"agld{cc}", "sg16")

            @block.vector
            def _(v, cc=cc):
                wait(v, f"agld{cc}")
                if cc == 0:
                    wait(v, "gm_f32")
                pt_flat = ptpre[:].rearrange("p a b -> p (a b)")
                if cc == 0:
                    v.tensor_scalar(pt_flat, agstage[:, 0:512], wpre_sb[:, 0:1],
                                    None, op0=ALU.mult)
                    inc(v.tensor_scalar(ks_run[:], agstage[0:1, 512:576],
                                        wpre_sb[0:1, 0:1], None, op0=ALU.mult), "sv")
                else:
                    v.scalar_tensor_tensor(pt_flat, agstage[:, 0:512],
                                           wpre_sb[:, cc:cc + 1], pt_flat,
                                           op0=ALU.mult, op1=ALU.add)
                    inc(v.scalar_tensor_tensor(ks_run[:], agstage[0:1, 512:576],
                                           wpre_sb[0:1, cc:cc + 1], ks_run[:],
                                           op0=ALU.mult, op1=ALU.add), "sv")
                mark(f"agacc{cc}", "sv")

        @block.vector
        def _(v):
            wait(v, f"agacc{NCORES-1}")
            v.drain()
            v.tensor_copy(ptpre_bf, ptpre[:])
            inc(v.tensor_copy(q0pre_bf[:], ks_run[:]), "sv")
            mark("ptbf", "sv")

        @block.tensor
        def _(t):
            wait(t, "ptbf")
            wait(t, "vwow_bf")
            for i in range(8):
                for half in range(2):
                    t.matmul(kv_ps[:, half * 512:(half + 1) * 512], num_fm[:, i, 0:64],
                             vw_bf[:, i, half * 512:(half + 1) * 512],
                             start=(i == 0), stop=False)
            for half in range(2):
                inc(t.matmul(kv_ps[:, half * 512:(half + 1) * 512], q0pre_bf[:],
                         vb_row[0:1, half * 512:(half + 1) * 512],
                         start=False, stop=True), "stt")
            mark("kvfold", "stt")

        @block.vector
        def _(v):
            wait(v, "kvfold")
            v.tensor_copy(kvacc[:], kv_ps[:])
            inc(v.tensor_copy(kv_sb[:], kv_ps[:]), "sv")
            mark("kvsb_-1", "sv")

        @block.tensor
        def _(t):
            wait(t, "ptbf")
            inc(t.matmul(m1_ps[:, 256:320], ones_row_f32[:], ks_run[:], start=True, stop=True), "stt")
            mark("ksrept_-1", "stt")

        @block.vector
        def _(v):
            wait(v, "ksrept_-1")
            inc(v.tensor_copy(ks_rep[:], m1_ps[:, 256:320]), "sv")
            mark("ksrepv_-1", "sv")

        if "tap_kvpre" in tap_d:
            @block.gpsimd
            def _(g):
                wait(g, "kvsb_-1")
                inc(g.dma_start(out=tap_d["tap_kvpre"], in_=kv_sb[:]), "sg16", 16)

    if "C" in stages:
        for c in range(NCH):
            buf = c % 2

            @block.tensor
            def _(t, c=c, buf=buf):
                if c == 0:
                    wait(t, f"xtr{NCH-1}")
                if c > 0:
                    wait(t, f"trevac{c-1}")
                inc(t.transpose(trB_ps[:, 0:128], qke_t[:, c, :], identb[:]), "stt")
                mark(f"qtr{c}", "stt")
                if c == 0:
                    wait(t, "vwow_bf")
                    wait(t, ROWS_DONE)
                    wait(t, "ttevr7")
                    wait(t, f"eluv{NCH-1}")
                else:
                    wait(t, f"vtok_s{c-1}")
                for half in range(2):
                    for i in range(8):
                        t.matmul(vp_ps[:, half * 512:(half + 1) * 512],
                                 xn_fm[:, i, c * 128:(c + 1) * 128],
                                 vw_bf[:, i, half * 512:(half + 1) * 512],
                                 start=(i == 0), stop=False)
                    inc(t.matmul(vp_ps[:, half * 512:(half + 1) * 512], ones_row_bf[:],
                             vb_row[0:1, half * 512:(half + 1) * 512],
                             start=False, stop=True), "stt")
                mark(f"vp{c}", "stt")

            @block.scalar
            def _(s, c=c, buf=buf):
                wait(s, f"vp{c}")
                inc(s.activation(vtok[:, buf, :], vp_ps[:], AF.Identity), "ss")
                mark(f"vtok_s{c}", "ss")

            @block.vector
            def _(v, c=c):
                if c > 0:
                    wait(v, f"tr{c-1}")
                inc(v.tensor_copy(qzk[:, 0:64], qke_t[:, c, 64:128]), "sv")
                mark(f"qzkv{c}", "sv")

            @block.tensor
            def _(t, c=c):
                wait(t, f"qzkv{c}")
                t.wait_ge(sems["stt"], land[f"qtr{c}"][1])
                inc(t.transpose(trB_ps[:, 512:640], qzk[:], identb[:]), "stt")
                mark(f"tr{c}", "stt")

            @block.vector
            def _(v, c=c):
                wait(v, f"tr{c}")
                v.tensor_copy(qfm_sb[:], trB_ps[0:64, 0:128])
                inc(v.tensor_copy(kfm_sb[:], trB_ps[0:64, 512:640]), "sv")
                mark(f"trevac{c}", "sv")

            @block.tensor
            def _(t, c=c):
                wait(t, f"trevac{c}")
                wait(t, f"ksrepv_{c-1}" if c == 0 else f"ksrepv{c-1}")
                inc(t.matmul(m1_ps[:, 0:128], kfm_sb[:], qfm_sb[:], start=True, stop=True), "stt")
                mark(f"A{c}", "stt")

            @block.vector
            def _(v, c=c):
                wait(v, f"A{c}")
                inc(v.tensor_tensor(am_bf[:], m1_ps[:, 0:128], mask_bf[:], op=ALU.mult), "sv")
                mark(f"am{c}", "sv")

            @block.tensor
            def _(t, c=c):
                wait(t, f"am{c}")
                t.matmul(m1_ps[:, 128:129], am_bf[:], ones_col_bf[:], start=True, stop=True)
                inc(t.matmul(m1_ps[0:1, 192:256], ones_col_bf[:], qke_t[:, c, 64:128],
                         start=True, stop=True), "stt")
                mark(f"dent{c}", "stt")

            @block.vector
            def _(v, c=c):
                wait(v, f"dent{c}")
                v.scalar_tensor_tensor(tmpB[:, 0:64], qf32_t[:, c, :], 1.0,
                                       ks_rep[:], op0=ALU.mult, op1=ALU.mult,
                                       accum_out=den2[:])
                v.scalar_tensor_tensor(ks_run[:], m1_ps[0:1, 192:256], 1.0, ks_run[:],
                                       op0=ALU.mult, op1=ALU.add)
                v.drain()
                v.scalar_tensor_tensor(den_f[:], m1_ps[:, 128:129], ATTN_EPS, den2[:],
                                       op0=ALU.add, op1=ALU.add)
                v.drain()
                inc(v.reciprocal(rden[:], den_f[:]), "sv")
                mark(f"denv{c}", "sv")

            @block.tensor
            def _(t, c=c):
                wait(t, f"denv{c}")
                inc(t.matmul(m1_ps[:, 256:320], ones_row_f32[:], ks_run[:], start=True, stop=True), "stt")
                mark(f"ksrept{c}", "stt")

            @block.vector
            def _(v, c=c):
                wait(v, f"ksrept{c}")
                inc(v.tensor_copy(ks_rep[:], m1_ps[:, 256:320]), "sv")
                mark(f"ksrepv{c}", "sv")

            @block.tensor
            def _(t, c=c, buf=buf):
                wait(t, f"vtok_s{c}")
                wait(t, f"am{c}")
                wait(t, "kvsb_-1" if c == 0 else f"kvsb_{c-1}")
                if c > 0:
                    wait(t, f"y1_{c-1}")
                for i in range(8):
                    t.matmul(no_ps[:, i * 128:(i + 1) * 128],
                             vtok[:, buf, i * 128:(i + 1) * 128], am_bf[:],
                             start=True, stop=False)
                    inc(t.matmul(no_ps[:, i * 128:(i + 1) * 128],
                             kv_sb[:, i * 128:(i + 1) * 128], qfm_sb[:],
                             start=False, stop=True), "stt")
                mark(f"num{c}", "stt")

            @block.vector
            def _(v, c=c):
                wait(v, f"num{c}")
                inc(v.tensor_copy(num_fm[:].rearrange("p a b -> p (a b)"), no_ps[:]), "sv")
                mark(f"numev{c}", "sv")

            @block.tensor
            def _(t, c=c, buf=buf):
                for half in range(2):
                    inc(t.matmul(kv_ps[:, half * 512:(half + 1) * 512],
                             qke_t[:, c, 64:128],
                             vtok[:, buf, half * 512:(half + 1) * 512],
                             start=True, stop=True), "stt")
                mark(f"kvupd{c}", "stt")

            @block.vector
            def _(v, c=c):
                wait(v, f"kvupd{c}")
                v.scalar_tensor_tensor(kvacc[:], kv_ps[:], 1.0, kvacc[:],
                                       op0=ALU.mult, op1=ALU.add)
                v.drain()
                inc(v.tensor_copy(kv_sb[:], kvacc[:]), "sv")
                mark(f"kvsb_{c}", "sv")

            @block.tensor
            def _(t, c=c):
                wait(t, f"numev{c}")
                for half in range(2):
                    for i in range(8):
                        inc(t.matmul(no_ps[:, half * 512:(half + 1) * 512], num_fm[:, i, :],
                                 ow_bf[:, i, half * 512:(half + 1) * 512],
                                 start=(i == 0), stop=(i == 7)), "stt")
                mark(f"O{c}", "stt")

            @block.vector
            def _(v, c=c):
                wait(v, f"O{c}")
                inc(v.scalar_tensor_tensor(y_t[:, c, :], no_ps[:], rden[:], y_t[:, c, :],
                                       op0=ALU.mult, op1=ALU.add), "sv")
                mark(f"y1_{c}", "sv")

        @block.gpsimd
        def _(g):
            wait(g, "ttstr7")
            wait(g, f"O{NCH-1}")
            inc(g.dma_start(out=mc_bf[:],
                            in_=mc_dram.ap().rearrange("(i p) n -> p i n", p=128)),
                "smc", 16)
            mark("mc_loaded", "smc")

        if "tap_mc" in tap_d:
            @block.gpsimd
            def _(g):
                wait(g, "mc_loaded")
                inc(g.dma_start(out=tap_d["tap_mc"], in_=mc_bf[:]), "sg16", 16)

        if "tap_y" in tap_d and "D" not in stages:
            @block.gpsimd
            def _(g):
                wait(g, f"y1_{NCH-1}")
                for c in range(NCH):
                    inc(g.dma_start(out=tap_d["tap_y"][c * 128:(c + 1) * 128, :],
                                    in_=y_t[:, c, :]), "sg16", 16)

    if "D" in stages:
        @block.gpsimd
        def _(g):
            hf1 = h_fm[:, 0:4, :].rearrange("p a b -> p (a b)")
            for i in range(8):
                for half in range(2):
                    inc(g.dma_start(out=wstage[:],
                                    in_=f1w_d[i * 128:(i + 1) * 128,
                                              half * 2048:(half + 1) * 2048]), "sg16", 16)
                    g.wait_ge(sems["sg16"], cnt["sg16"])
                    inc(g.tensor_copy(hf1, wstage[:]), "sgc")
                    g.wait_ge(sems["sgc"], cnt["sgc"])
                    inc(g.dma_start(out=f1bf_dram.ap()[i, :, half * 2048:(half + 1) * 2048],
                                    in_=hf1), "sg16", 16)
            mark("f1prep", "sg16")
            hf2b = h_fm[:, 5:7, :].rearrange("p a b -> p (a b)")
            for m in range(32):
                inc(g.dma_start(out=wstage[:, 0:1024],
                                in_=f2w_d[m * 128:(m + 1) * 128, :]), "sg16", 16)
                g.wait_ge(sems["sg16"], cnt["sg16"])
                inc(g.tensor_copy(hf2b[:, 0:1024], wstage[:, 0:1024]), "sgc")
                g.wait_ge(sems["sgc"], cnt["sgc"])
                inc(g.dma_start(out=f2bf_dram.ap()[m, :, :], in_=hf2b[:, 0:1024]),
                    "sg16", 16)
            mark("f2prep", "sg16")

    if "D" in stages:
        p1p.close()
        p2p = ExitStack()
        f2ps = [p2p.enter_context(nc.psum_tensor(f"f2ps{i}", [128, 1024], F32))
                for i in range(2)]
        tr2_ps = p2p.enter_context(nc.psum_tensor("tr2_ps", [128, 4096], BF16))
        f1ps = [f2ps[0][:, 0:512], f2ps[0][:, 512:1024]]

        for j in range(4):
            for ci in range(4):
                c = 4 * j + ci

                @block.vector
                def _(v, c=c, j=j):
                    wait(v, f"y1_{c}")
                    if j > 0:
                        wait(v, f"y2_{4*(j-1)+3}")

                stats_chain(y_t[:, c, 0:512], y_t[:, c, 512:1024], tag=f"d{c}",
                            sc=scol_t[:, c, :], bg=bnag_t[:, c, :])

                @block.vector
                def _(v, c=c):
                    wait(v, f"ssdd{c}")
                    v.wait_ge(sems["sv"], cnt["sv"])
                    v.tensor_scalar(tmpA[:], y_t[:, c, :], scol_t[:, c, 0:1],
                                    scol_t[:, c, 1:2], op0=ALU.mult, op1=ALU.add)
                    v.drain()
                    v.tensor_tensor(tmpB[:], tmpA[:], g2rep[:], op=ALU.mult)
                    v.drain()
                    if c >= 2:
                        wait(v, f"ztrt{c-2}")
                    inc(v.tensor_tensor(xnbf[:, c % 2, :], tmpB[:], b2rep[:], op=ALU.add), "sv")
                    mark(f"z2v{c}", "sv")

                @block.tensor
                def _(t, c=c, ci=ci, j=j):
                    wait(t, f"z2v{c}")
                    if c > 0:
                        wait(t, f"ztr{c-1}")
                    if j > 0:
                        wait(t, f"f1mm{j-1}_31")
                    for i in range(8):
                        col = (i % 4) * 1024 + (i // 4) * 128
                        ti = t.transpose(tr2_ps[:, col:col + 128],
                                         xnbf[:, c % 2, i * 128:(i + 1) * 128], identb[:])
                        if i == 3:
                            inc(ti, "stt")
                            t.wait_ge(sems["stt"], cnt["stt"])
                    inc(t.nop(), "stt")
                    mark(f"ztrt{c}", "stt")

                @block.vector
                def _(v, c=c, ci=ci):
                    wait(v, f"ztrt{c}")
                    for b in range(4):
                        cp = v.tensor_copy(
                            zfm[:, b:8:4, ci * 128:(ci + 1) * 128],
                            tr2_ps[:, b * 1024:b * 1024 + 256]
                            .rearrange("p (a b) -> p a b", a=2))
                        if b == 3:
                            inc(cp, "sv")
                    mark(f"ztr{c}", "sv")

            for m in range(32):
                if m % 2 == 0:
                    m2 = m // 2
                    g1 = j * 16 + m2
                    @block.gpsimd
                    def _(g, j=j, m2=m2, g1=g1):
                        if g1 == 0:
                            wait(g, "f2prep")
                        if g1 >= 3:
                            pj, pm2 = (g1 - 3) // 16, (g1 - 3) % 16
                            wait(g, f"f1use{pj}_{2*pm2+1}")
                        src = f1bf_dram.ap()[:, :, m2 * 256:(m2 + 1) * 256] \
                            .rearrange("i p f -> p i f")
                        inc(g.dma_start(out=f1roll[:, g1 % 3, :, :], in_=src),
                            f"f1s{g1 % 3}", 16)
                        mark(f"f1ld{j}_{m2}", f"f1s{g1 % 3}")

                @block.tensor
                def _(t, j=j, m=m):
                    if m == 0:
                        wait(t, "gm_f32")
                        wait(t, f"ztr{4*j+3}")
                        if j > 0:
                            wait(t, f"y2_{4*(j-1)+3}")
                    wait(t, f"f1ld{j}_{m//2}")
                    if m >= 2:
                        wait(t, f"gelu{j}_{m-2}")
                    elif j > 0:
                        wait(t, f"gelu{j-1}_{30+m}")
                    for i in range(8):
                        inc(t.matmul(f1ps[m % 2],
                                 f1roll[:, (j * 16 + m // 2) % 3, i,
                                        (m % 2) * 128:(m % 2 + 1) * 128],
                                 zfm[:, i, :],
                                 start=(i == 0), stop=(i == 7)), "stt")
                    mark(f"f1mm{j}_{m}", "stt")
                    mark(f"f1use{j}_{m}", "stt")

                @block.scalar
                def _(s, j=j, m=m):
                    wait(s, f"f1mm{j}_{m}")
                    if j > 0:
                        wait(s, f"f2u{(j-1)*64+32+m}")
                    inc(s.activation(h_fm[:, m, :], f1ps[m % 2], AF.Gelu,
                                 bias=f1b_cols[:, m:m + 1], scale=1.0), "ss")
                    mark(f"gelu{j}_{m}", "ss")

            for sub in range(2):
                for m in range(32):
                    u = (j * 2 + sub) * 32 + m
                    if m % 2 == 0:
                        u2 = ((j * 2 + sub) * 32 + m) // 2

                        @block.sync
                        def _(sy, u2=u2, m=m):
                            if u2 == 0:
                                wait(sy, "f2prep")
                            if u2 >= 3:
                                wait(sy, f"f2u{2*(u2-3)+1}")
                            inc(sy.dma_start(
                                out=f2roll[:, u2 % 3, :, :],
                                in_=f2bf_dram.ap()[m:m + 2, :, :]
                                .rearrange("m p f -> p m f")), f"f2s{u2 % 3}", 16)
                            mark(f"f2ld{u2}", f"f2s{u2 % 3}")

                    @block.tensor
                    def _(t, j=j, sub=sub, m=m, u=u):
                        if m == 0:
                            wait(t, f"gelu{j}_31")
                            if sub == 0 and j > 0:
                                wait(t, f"y2_{4*(j-1)+3}")
                            if sub == 1:
                                wait(t, f"y2_{4*j+1}")
                        wait(t, f"f2ld{u//2}")
                        for ch2 in range(2):
                            for half in range(2):
                                inc(t.matmul(
                                    f2ps[ch2][:, half * 512:(half + 1) * 512],
                                    h_fm[:, m, (sub * 2 + ch2) * 128:(sub * 2 + ch2 + 1) * 128],
                                    f2roll[:, (u // 2) % 3, m % 2,
                                           half * 512:(half + 1) * 512],
                                    start=(m == 0), stop=False), "stt")
                        mark(f"f2u{u}", "stt")

                @block.tensor
                def _(t, j=j, sub=sub):
                    for ch2 in range(2):
                        for half in range(2):
                            inc(t.matmul(f2ps[ch2][:, half * 512:(half + 1) * 512],
                                     ones_row_bf[:],
                                     f2b_row[0:1, half * 512:(half + 1) * 512],
                                     start=False, stop=True), "stt")
                    mark(f"f2mm{j}_{sub}", "stt")

                for ch2 in range(2):
                    @block.vector
                    def _(v, j=j, sub=sub, ch2=ch2):
                        wait(v, f"f2mm{j}_{sub}")
                        c = 4 * j + sub * 2 + ch2
                        inc(v.scalar_tensor_tensor(y_t[:, c, :], f2ps[ch2][:], 1.0,
                                               y_t[:, c, :],
                                               op0=ALU.mult, op1=ALU.add), "sv")
                        mark(f"y2_{c}", "sv")

        if "tap_y" in tap_d and "E" not in stages:
            @block.gpsimd
            def _(g):
                wait(g, f"y2_{NCH-1}")
                for c in range(NCH):
                    inc(g.dma_start(out=tap_d["tap_y"][c * 128:(c + 1) * 128, :],
                                    in_=y_t[:, c, :]), "sg16", 16)
        p2p.close()

    if "E" in stages:
        p4p = ExitStack()
        wc = [p4p.enter_context(nc.psum_tensor(f"wc{i}", [128, 1024], F32))
              for i in range(2)]
        trp4_ps = p4p.enter_context(nc.psum_tensor("trp4_ps", [128, 4096], BF16))

        @block.scalar
        def _(s):
            for c in range(2):
                inc(s.dma_start(out=xs_st[:, c, :], in_=x_d[c * 128:(c + 1) * 128, :]),
                    f"sx{c % 2}", 16)
                mark(f"xld2_{c}", f"sx{c % 2}")

        for c in range(NCH):
            buf = c % 2

            @block.vector
            def _(v, c=c):
                wait(v, f"y2_{c}")

            stats_chain(y_t[:, c, 0:512], y_t[:, c, 512:1024], tag=f"e{c}",
                        sc=scol_t[:, c, :], bg=bnag_t[:, c, :])

            @block.vector
            def _(v, c=c, buf=buf):
                wait(v, f"ssde{c}")
                v.wait_ge(sems["sv"], cnt["sv"])
                v.tensor_scalar(tmpA[:], y_t[:, c, :], scol_t[:, c, 0:1],
                                scol_t[:, c, 1:2], op0=ALU.mult, op1=ALU.add)
                v.drain()
                wait(v, f"xld2_{c}")
                inc(v.tensor_tensor(y_t[:, c, :], tmpA[:], xs_st[:, buf, :], op=ALU.add), "sv")
                mark(f"x2_{c}", "sv")

            if c + 2 < NCH:
                @block.sync
                def _(sy, c=c):
                    wait(sy, f"x2_{c}")
                    inc(sy.dma_start(out=xs_st[:, c % 2, :],
                                     in_=x_d[(c + 2) * 128:(c + 3) * 128, :]),
                        f"sx{c % 2}", 16)
                    mark(f"xld2_{c+2}", f"sx{c % 2}")

            if "F" not in stages:
                continue

            stats_chain(y_t[:, c, 0:512], y_t[:, c, 512:1024], tag=f"f{c}",
                        sc=scol_t[:, c, :], bg=bnag_t[:, c, :])

            @block.vector
            def _(v, c=c):
                wait(v, f"ssdf{c}")
                v.wait_ge(sems["sv"], cnt["sv"])
                if c >= 2:
                    wait(v, f"wtrt{c-2}")
                inc(v.tensor_scalar(xnbf[:, c % 2, :], y_t[:, c, :], scol_t[:, c, 0:1],
                                    scol_t[:, c, 1:2], op0=ALU.mult, op1=ALU.add), "sv")
                mark(f"z4v{c}", "sv")

            @block.tensor
            def _(t, c=c):
                wait(t, f"z4v{c}")
                if c > 0:
                    wait(t, f"wtr{c-1}")
                elif "D" in stages:
                    wait(t, "f1mm3_31")
                if c >= 4:
                    wait(t, f"wcmm{c-4}")
                for i in range(8):
                    col = (i % 4) * 1024 + (i // 4) * 128
                    ti = t.transpose(trp4_ps[:, col:col + 128],
                                     xnbf[:, c % 2, i * 128:(i + 1) * 128], identb[:])
                    if i == 3:
                        inc(ti, "stt")
                        t.wait_ge(sems["stt"], cnt["stt"])
                inc(t.nop(), "stt")
                mark(f"wtrt{c}", "stt")

            @block.vector
            def _(v, c=c):
                wait(v, f"wtrt{c}")
                for b in range(4):
                    cp = v.tensor_copy(
                        zfm[:, b:8:4, (c % 4) * 128:(c % 4 + 1) * 128],
                        trp4_ps[:, b * 1024:b * 1024 + 256]
                        .rearrange("p (a b) -> p a b", a=2))
                    if b == 3:
                        inc(cp, "sv")
                mark(f"wtr{c}", "sv")

            @block.tensor
            def _(t, c=c):
                wait(t, f"wtr{c}")
                if c == 0:
                    wait(t, "mc_loaded")
                if c > 1:
                    wait(t, f"wfin{c-2}")
                for i in range(8):
                    for half in range(2):
                        t.matmul(wc[c % 2][:, half * 512:(half + 1) * 512],
                                 zfm[:, i, (c % 4) * 128:(c % 4 + 1) * 128],
                                 mc_bf[:, i, half * 512:(half + 1) * 512],
                                 start=(i == 0), stop=False)
                for half in range(2):
                    inc(t.matmul(wc[c % 2][:, half * 512:(half + 1) * 512], ones_row_bf[:],
                             fbias_row[0:1, half * 512:(half + 1) * 512],
                             start=False, stop=True), "stt")
                mark(f"wcmm{c}", "stt")

            @block.vector
            def _(v, c=c):
                wait(v, f"wcmm{c}")

            stats_chain(wc[c % 2][:, 0:512], wc[c % 2][:, 512:1024], tag=f"w{c}",
                        sc=scol_t[:, c, :], bg=bnag_t[:, c, :])

            @block.vector
            def _(v, c=c):
                wait(v, f"ssdw{c}")
                v.wait_ge(sems["sv"], cnt["sv"])
                v.tensor_scalar(tmpA_w, wc[c % 2][:], scol_t[:, c, 0:1],
                                scol_t[:, c, 1:2], op0=ALU.mult, op1=ALU.add)
                v.drain()
                if c >= 2:
                    wait(v, f"outst{c-2}")
                inc(v.tensor_tensor(out_sb[:, (c % 2) * 1024:(c % 2 + 1) * 1024], tmpA_w, y_t[:, c, :], op=ALU.add), "sv")
                mark(f"wfin{c}", "sv")

            @block.sync
            def _(sy, c=c):
                wait(sy, f"wfin{c}")
                inc(sy.dma_start(out=out_d[c * 128:(c + 1) * 128, :],
                                 in_=out_sb[:, (c % 2) * 1024:(c % 2 + 1) * 1024]), f"so{c % 2}", 16)
                mark(f"outst{c}", f"so{c % 2}")

        if "tap_y" in tap_d and "F" not in stages:
            @block.gpsimd
            def _(g):
                wait(g, f"x2_{NCH-1}")
                for c in range(NCH):
                    inc(g.dma_start(out=tap_d["tap_y"][c * 128:(c + 1) * 128, :],
                                    in_=y_t[:, c, :]), "sg16", 16)

        if "F" in stages:
            @block.vector
            def _(v):
                v.wait_ge(sems["so0"], cnt["so0"])
                v.wait_ge(sems["so1"], cnt["so1"])
        p4p.close()

    @block.gpsimd
    def _(g):
        g.wait_ge(sems["sg16"], cnt["sg16"])

    st.close()
    return nc


def _prep_inmaps(inputs):
    f32 = np.float32
    x = np.asarray(inputs["L0_attn_x"], f32)
    qw = np.asarray(inputs["L0_attn_q_w"], f32)
    kw = np.asarray(inputs["L0_attn_k_w"], f32)
    vw = np.asarray(inputs["L0_attn_v_w"], f32)
    ow = np.asarray(inputs["L0_attn_o_w"], f32)
    f1w = np.asarray(inputs["L0_attn_f1_w"], f32)
    f2w = np.asarray(inputs["L0_attn_f2_w"], f32)
    qkw_t = np.ascontiguousarray(np.concatenate([qw, kw], axis=0).T)
    vw_t = np.ascontiguousarray(vw.T)
    ow_t = np.ascontiguousarray(ow.T)
    f1w_t = np.ascontiguousarray(f1w.T)
    f2w_t = np.ascontiguousarray(f2w.T)

    rows = np.zeros((9, 1024), f32)
    rows[0] = np.asarray(inputs["L0_attn_ln1_w"], f32)
    rows[1] = np.asarray(inputs["L0_attn_ln1_b"], f32)
    rows[2] = np.asarray(inputs["L0_attn_ln2_w"], f32)
    rows[3] = np.asarray(inputs["L0_attn_ln2_b"], f32)
    rows[4] = np.asarray(inputs["L0_attn_o_b"], f32)
    rows[5] = np.asarray(inputs["L0_attn_v_b"], f32)
    rows[6] = np.asarray(inputs["L0_attn_f2_b"], f32)
    rows[7] = np.concatenate([np.asarray(inputs["L0_ffn_bias_real"], f32),
                              np.asarray(inputs["L0_ffn_bias_imag"], f32)])
    rows[8, 0:128] = np.concatenate([np.asarray(inputs["L0_attn_q_b"], f32),
                                     np.asarray(inputs["L0_attn_k_b"], f32)])

    f1b_cols = np.ascontiguousarray(
        np.asarray(inputs["L0_attn_f1_b"], f32).reshape(32, 128).T)

    G1R = np.asarray(inputs["L0_ffn_G1_R"], f32)
    G2R = np.asarray(inputs["L0_ffn_G2_R"], f32)
    G1I = np.asarray(inputs["L0_ffn_G1_I"], f32)
    G2I = np.asarray(inputs["L0_ffn_G2_I"], f32)
    g1m = np.ascontiguousarray(np.concatenate(
        [G1R[0].transpose(1, 0, 2).reshape(4, 1024),
         G1I[0].transpose(1, 0, 2).reshape(4, 1024)], axis=1))
    g2m = np.ascontiguousarray(np.concatenate(
        [G2R[:, :, 0, :].transpose(0, 2, 1).reshape(4, 256),
         G2I[:, :, 0, :].transpose(0, 2, 1).reshape(4, 256)], axis=1))

    in_maps = []
    for core in range(NCORES):
        b, q = core // QPER, core % QPER
        w_pre = np.zeros((128, 8), f32)
        for cc in range(NCORES):
            if cc // QPER == b and cc % QPER < q:
                w_pre[:, cc] = 1.0
        in_maps.append({
            "x": np.ascontiguousarray(x[b, q * T:(q + 1) * T]),
            "qkw_t": qkw_t, "vw_t": vw_t, "ow_t": ow_t,
            "f1w_t": f1w_t, "f2w_t": f2w_t,
            "g1m": g1m, "g2m": g2m, "rows": rows, "f1b_cols": f1b_cols,
            "w_pre": w_pre,
        })
    return in_maps


_NC_CACHE = {}


def kernel(**inputs):
    key = ("ABCDEF", ())
    if key not in _NC_CACHE:
        _NC_CACHE[key] = build_nc(taps=(), stages="ABCDEF")
    nc = _NC_CACHE[key]
    in_maps = _prep_inmaps(inputs)
    res = run_bass_kernel_spmd(nc, in_maps, core_ids=list(range(NCORES)))
    out = np.zeros((B, N, D), np.float32)
    for core in range(NCORES):
        b, q = core // QPER, core % QPER
        out[b, q * T:(q + 1) * T] = res.results[core]["out"]
    return out


if __name__ == "__main__":
    nc = build_nc()
    print("build ok")


# version 09 (final)
# speedup vs baseline: 1.1215x; 1.0606x over previous
import numpy as np
from contextlib import ExitStack

import concourse.bass as bass
import concourse.mybir as mybir
from concourse.bass_utils import run_bass_kernel_spmd

F32 = mybir.dt.float32
BF16 = mybir.dt.bfloat16
I32 = mybir.dt.int32
AF = mybir.ActivationFunctionType
ALU = mybir.AluOpType
AX = mybir.AxisListType

LN_EPS = 1e-5
ATTN_EPS = 1e-6

B, N, D, FD = 2, 8192, 1024, 64
NCORES = 8
QPER = 4
T = N // QPER
NCH = T // 128
DH = D // 128
MH = (4 * D) // 128


def build_nc(taps=(), stages="ABCDEF"):
    nc = bass.Bass("TRN2", target_bir_lowering=False,
                   detect_race_conditions=False)

    x_d = nc.dram_tensor("x", [T, D], F32, kind="ExternalInput").ap()
    qkw_d = nc.dram_tensor("qkw_t", [D, 128], F32, kind="ExternalInput").ap()
    vw_d = nc.dram_tensor("vw_t", [D, D], F32, kind="ExternalInput").ap()
    ow_d = nc.dram_tensor("ow_t", [D, D], F32, kind="ExternalInput").ap()
    f1w_d = nc.dram_tensor("f1w_t", [D, 4 * D], F32, kind="ExternalInput").ap()
    f2w_d = nc.dram_tensor("f2w_t", [4 * D, D], F32, kind="ExternalInput").ap()
    g1m_d = nc.dram_tensor("g1m", [4, 2048], F32, kind="ExternalInput").ap()
    g2m_d = nc.dram_tensor("g2m", [4, 512], F32, kind="ExternalInput").ap()
    rows_d = nc.dram_tensor("rows", [9, 1024], F32, kind="ExternalInput").ap()
    f1bc_d = nc.dram_tensor("f1b_cols", [128, 32], F32, kind="ExternalInput").ap()
    wpre_d = nc.dram_tensor("w_pre", [128, 8], F32, kind="ExternalInput").ap()
    out_d = nc.dram_tensor("out", [T, D], F32, kind="ExternalOutput").ap()

    tap_d = {}
    for tname, shape, dt in [("tap_y", [T, D], F32), ("tap_pay", [128, 576], F32),
                             ("tap_xnfm", [128, 8, 2048], BF16),
                             ("tap_kvpre", [64, 1024], BF16),
                             ("tap_qke", [128, NCH * 128], BF16),
                             ("tap_mc", [128, 8, 1024], BF16)]:
        if tname in taps:
            tap_d[tname] = nc.dram_tensor(tname, shape, dt, kind="ExternalOutput").ap()

    pay_dram = nc.dram_tensor("pay_dram", [128, 576], F32, kind="Internal")
    ag_dram = nc.dram_tensor("ag_dram", [8, 128, 576], F32, kind="Internal")
    mc_dram = nc.dram_tensor("mc_dram", [1024, 1024], BF16, kind="Internal")
    f1bf_dram = nc.dram_tensor("f1bf_dram", [8, 128, 4096], BF16, kind="Internal")
    f2bf_dram = nc.dram_tensor("f2bf_dram", [32, 128, 1024], BF16, kind="Internal")

    st = ExitStack()
    e = st.enter_context

    y_t = e(nc.sbuf_tensor("y_t", [128, NCH, 1024], F32))
    qke_t = e(nc.sbuf_tensor("qke_t", [128, NCH, 128], BF16))
    qf32_t = e(nc.sbuf_tensor("qf32_t", [128, NCH, 64], F32))
    tmpA_w = qf32_t[:].rearrange("p a b -> p (a b)")
    g1rep = e(nc.sbuf_tensor("g1rep", [128, 1024], BF16))
    b1rep = e(nc.sbuf_tensor("b1rep", [128, 1024], BF16))
    g2rep = e(nc.sbuf_tensor("g2rep", [128, 1024], BF16))
    b2rep = e(nc.sbuf_tensor("b2rep", [128, 1024], BF16))
    ob_rep = e(nc.sbuf_tensor("ob_rep", [128, 1024], BF16))
    f1b_cols = e(nc.sbuf_tensor("f1b_colsb", [128, 32], F32))
    mask_bf = e(nc.sbuf_tensor("mask_bf", [128, 128], BF16))
    identb = e(nc.sbuf_tensor("identb", [128, 128], BF16))
    ones_row_bf = e(nc.sbuf_tensor("ones_row_bf", [1, 128], BF16))
    ones_col_bf = e(nc.sbuf_tensor("ones_col_bf", [128, 1], BF16))
    ones_row_f32 = e(nc.sbuf_tensor("ones_row_f32", [1, 128], F32))
    qkb_row = e(nc.sbuf_tensor("qkb_row", [1, 128], BF16))
    vb_row = e(nc.sbuf_tensor("vb_row", [1, 1024], BF16))
    f2b_row = e(nc.sbuf_tensor("f2b_row", [1, 1024], BF16))
    fbias_row = e(nc.sbuf_tensor("fbias_row", [1, 1024], BF16))
    wpre_sb = e(nc.sbuf_tensor("wpre_sb", [128, 8], F32))
    kv_sb = e(nc.sbuf_tensor("kv_sb", [64, 1024], BF16))
    kvacc = e(nc.sbuf_tensor("kvacc", [64, 1024], F32))
    ks_run = e(nc.sbuf_tensor("ks_run", [1, 64], F32))
    ks_rep = e(nc.sbuf_tensor("ks_rep", [128, 64], F32))
    q0pre_bf = e(nc.sbuf_tensor("q0pre_bf", [1, 64], BF16))
    bnst = e(nc.sbuf_tensor("bnst", [128, 2, 6], F32))
    scol_t = e(nc.sbuf_tensor("scol_t", [128, NCH, 8], F32))
    bnag_t = e(nc.sbuf_tensor("bnag_t", [128, NCH, 2], F32))
    elu_sc = e(nc.sbuf_tensor("elu_sc", [128, 256], F32))
    eps_col = e(nc.sbuf_tensor("eps_col", [128, 1], F32))
    den_f = e(nc.sbuf_tensor("den_f", [128, 1], F32))
    rden = e(nc.sbuf_tensor("rden", [128, 1], F32))
    den2 = e(nc.sbuf_tensor("den2", [128, 1], F32))
    tmpA = e(nc.sbuf_tensor("tmpA", [128, 1024], F32))
    tmpB = e(nc.sbuf_tensor("tmpB", [128, 1024], F32))
    xs_st = e(nc.sbuf_tensor("xs_st", [128, 2, 1024], F32))
    wstage = e(nc.sbuf_tensor("wstage", [128, 2048], F32))
    xnbf = e(nc.sbuf_tensor("xnbf", [128, 2, 1024], BF16))

    p1s = ExitStack()
    iota_t = p1s.enter_context(nc.sbuf_tensor("iota_t", [128, 128], I32))
    qzk = p1s.enter_context(nc.sbuf_tensor("qzk", [128, 128], BF16))
    am_bf = p1s.enter_context(nc.sbuf_tensor("am_bf", [128, 128], BF16))
    vtok = p1s.enter_context(nc.sbuf_tensor("vtok", [128, 2, 1024], BF16))
    num_fm = p1s.enter_context(nc.sbuf_tensor("num_fm", [128, 8, 128], BF16))
    qfm_sb = p1s.enter_context(nc.sbuf_tensor("qfm_sb", [64, 128], BF16))
    kfm_sb = p1s.enter_context(nc.sbuf_tensor("kfm_sb", [64, 128], BF16))
    xn_fm = p1s.enter_context(nc.sbuf_tensor("xn_fm", [128, 8, 2048], BF16))
    vw_bf = p1s.enter_context(nc.sbuf_tensor("vw_bf", [128, 8, 1024], BF16))
    ow_bf = p1s.enter_context(nc.sbuf_tensor("ow_bf", [128, 8, 1024], BF16))
    qkw_bf = p1s.enter_context(nc.sbuf_tensor("qkw_bf", [128, 8, 128], BF16))
    ttx = p1s.enter_context(nc.sbuf_tensor("ttx", [128, 256], BF16))
    gm_sb = p1s.enter_context(nc.sbuf_tensor("gm_sb", [4, 2560], BF16))
    ptpre = p1s.enter_context(nc.sbuf_tensor("ptpre", [128, 8, 64], F32))
    ptpre_bf = num_fm[:, :, 0:64]
    p1s.close()

    mc_bf = e(nc.sbuf_tensor("mc_bf", [128, 8, 1024], BF16))
    zfm = e(nc.sbuf_tensor("zfm", [128, 8, 512], BF16))
    h_fm = e(nc.sbuf_tensor("h_fm", [128, 32, 512], BF16))
    f1roll = e(nc.sbuf_tensor("f1roll", [128, 3, 8, 256], BF16))
    f2roll = e(nc.sbuf_tensor("f2roll", [128, 3, 2, 1024], BF16))
    out_sb = wstage

    pAp = ExitStack()
    repvp_ps = pAp.enter_context(nc.psum_tensor("repvp_ps", [128, 1024], F32))
    pt_ps = pAp.enter_context(nc.psum_tensor("pt_ps", [128, 1024], F32))
    trA_ps = pAp.enter_context(nc.psum_tensor("trA_ps", [128, 4096], BF16))
    vp_ps = repvp_ps

    semnames = ["sx0", "sx1", "sg16", "sgm", "smc", "str_", "so0", "so1", "sv", "ss", "stt", "sgc", "scc", "f1s0", "f1s1", "f1s2", "f1s3", "f2s0", "f2s1", "f2s2", "f2s3", "wp0", "wp1", "po0", "po1"]
    sems = {n: e(nc.semaphore(n)) for n in semnames}
    cnt = {n: 0 for n in semnames}
    land = {}

    def inc(instr, s, amt=1):
        cnt[s] += amt
        instr.then_inc(sems[s], amt)
        return cnt[s]

    def mark(key, s):
        land[key] = (s, cnt[s])

    agstage = tmpA[:, 0:576]
    payload = tmpB[:, 0:576]
    row_st = wstage[0:1, 1024:2048]

    def wait(eng, key):
        s, v = land[key]
        eng.wait_ge(sems[s], v)

    block = e(nc.Block())

    @block.gpsimd
    def _(g):
        inc(g.iota(iota_t[:], pattern=[[1, 128]], base=0, channel_multiplier=-1), "sgc")
        inc(g.memset(qzk[:], 0.0), "sgc")
        inc(g.memset(ones_row_bf[:], 1.0), "sgc")
        inc(g.memset(ones_col_bf[:], 1.0), "sgc")
        inc(g.memset(ones_row_f32[:], 1.0), "sgc")
        inc(g.memset(ks_run[:], 0.0), "sgc")
        inc(g.memset(eps_col[:], LN_EPS), "sgc")
        mark("iota", "sgc")
        inc(g.dma_start(out=f1b_cols[:], in_=f1bc_d), "sgm", 16)
        inc(g.dma_start(out=wpre_sb[:], in_=wpre_d), "sgm", 16)
        inc(g.dma_start(out=tmpA[0:4, :], in_=g1m_d[:, 0:1024]), "sgm", 16)
        inc(g.dma_start(out=tmpB[0:4, :], in_=g1m_d[:, 1024:2048]), "sgm", 16)
        inc(g.dma_start(out=xs_st[0:4, 0, 0:512], in_=g2m_d), "sgm", 16)
        mark("gm_f32", "sgm")
        inc(g.dma_start(out=wstage[:, 0:1024].rearrange("p (c m) -> p c m", c=8),
                        in_=qkw_d.rearrange("(c k) m -> k c m", k=128)), "sg16", 16)
        mark("qkw_f32", "sg16")
        g.wait_ge(sems["sg16"], cnt["sg16"])
        inc(g.tensor_copy(qkw_bf[:], wstage[:, 0:1024]
                          .rearrange("p (c m) -> p c m", c=8)), "sgc")
        g.wait_ge(sems["sgc"], cnt["sgc"])
        inc(g.nop(), "sgc")
        mark("qkw_bf", "sgc")

    @block.scalar
    def _(s):
        wait(s, "iota")

    @block.vector
    def _(v):
        wait(v, "iota")
        v.tensor_scalar(mask_bf[:], iota_t[:], 0, None, op0=ALU.is_ge)
        inc(v.tensor_scalar(identb[:], iota_t[:], 0, None, op0=ALU.is_equal), "sv")
        mark("masks", "sv")
        wait(v, "gm_f32")
        v.tensor_copy(gm_sb[0:4, 0:1024], tmpA[0:4, :])
        v.tensor_copy(gm_sb[0:4, 1024:2048], tmpB[0:4, :])
        inc(v.tensor_copy(gm_sb[0:4, 2048:2560], xs_st[0:4, 0, 0:512]), "sv")
        mark("gm_bf", "sv")

    rep_specs = [(0, g1rep), (1, b1rep), (2, g2rep), (3, b2rep), (4, ob_rep)]
    for ri, (rowi, dst) in enumerate(rep_specs):
        @block.gpsimd
        def _(g, ri=ri, rowi=rowi):
            if ri > 0:
                wait(g, f"rep{ri-1}_t")
            inc(g.dma_start(out=row_st[:], in_=rows_d[rowi:rowi + 1, :]), "sg16", 16)
            mark(f"rep{ri}_row", "sg16")

        @block.tensor
        def _(t, ri=ri):
            wait(t, f"rep{ri}_row")
            if ri == 0:
                wait(t, "iota")
            if ri > 0:
                wait(t, f"rep{ri-1}_v")
            for half in range(2):
                inc(t.matmul(vp_ps[:, half * 512:(half + 1) * 512], ones_row_f32[:],
                         row_st[0:1, half * 512:(half + 1) * 512], start=True, stop=True), "stt")
            mark(f"rep{ri}_t", "stt")

        @block.vector
        def _(v, ri=ri, dst=dst):
            wait(v, f"rep{ri}_t")
            inc(v.tensor_copy(dst[:], vp_ps[:]), "sv")
            mark(f"rep{ri}_v", "sv")

    bfrow_specs = [(8, qkb_row, 128), (5, vb_row, 1024), (6, f2b_row, 1024),
                   (7, fbias_row, 1024)]
    for bi, (rowi, dst, w) in enumerate(bfrow_specs):
        @block.gpsimd
        def _(g, bi=bi, rowi=rowi):
            wait(g, "rep4_v" if bi == 0 else f"brow{bi-1}_v")
            inc(g.dma_start(out=row_st[:], in_=rows_d[rowi:rowi + 1, :]), "sg16", 16)
            mark(f"brow{bi}", "sg16")

        @block.vector
        def _(v, bi=bi, dst=dst, w=w):
            wait(v, f"brow{bi}")
            inc(v.tensor_copy(dst[:], row_st[0:1, 0:w]), "sv")
            mark(f"brow{bi}_v", "sv")
    ROWS_DONE = f"brow{len(bfrow_specs)-1}_v"

    mc_view = mc_dram.ap().rearrange("(rh d g) (ch b e) -> rh ch b d g e",
                                     rh=2, d=32, g=16, ch=2, b=32, e=16)

    def tt_chain(imag):
        g1o = 1024 if imag else 0
        g2o = 2304 if imag else 2048
        tag = "i" if imag else "r"
        for p in range(8):
            @block.tensor
            def _(t, p=p):
                if p == 0 and imag:
                    wait(t, "gm_bf")
                    wait(t, "rep4_v")
                if p == 0 and not imag:
                    wait(t, "ttevi7")
                if p > 0:
                    wait(t, f"ttev{tag}{p-1}")
                inc(t.matmul(vp_ps[:, 0:256], gm_sb[0:4, g1o + p * 128:g1o + (p + 1) * 128],
                         gm_sb[0:4, g2o:g2o + 256], start=True, stop=True), "stt")
                mark(f"ttmm{tag}{p}", "stt")

            @block.vector
            def _(v, p=p):
                wait(v, f"ttmm{tag}{p}")
                if p > 0:
                    wait(v, f"ttst{tag}{p-1}")
                elif not imag:
                    wait(v, "ttsti7")
                inc(v.tensor_copy(ttx[:], vp_ps[:, 0:256]), "sv")
                mark(f"ttev{tag}{p}", "sv")

            if imag:
                @block.gpsimd
                def _(g, p=p):
                    wait(g, f"ttev{tag}{p}")
                    for bq in range(4):
                        src_ = ttx[bq * 32:(bq + 1) * 32, :].rearrange(
                            "d (g e) -> d g e", g=16)
                        inc(g.dma_start(out=mc_view[0, 1, 4 * p + bq, :, :, :],
                                        in_=src_), "sg16", 16)
                    mark(f"ttstwi{p}", "sg16")

                @block.scalar
                def _(s, p=p):
                    wait(s, f"ttstwi{p}")
                    inc(s.mul(ttx[:], ttx[:], -1.0), "ss")
                    mark(f"ttneg{p}", "ss")

                @block.gpsimd
                def _(g, p=p):
                    wait(g, f"ttneg{p}")
                    for bq in range(4):
                        src_ = ttx[bq * 32:(bq + 1) * 32, :].rearrange(
                            "d (g e) -> d g e", g=16)
                        inc(g.dma_start(out=mc_view[1, 0, 4 * p + bq, :, :, :],
                                        in_=src_), "sg16", 16)
                    mark(f"ttst{tag}{p}", "sg16")
            else:
                @block.gpsimd
                def _(g, p=p):
                    wait(g, f"ttev{tag}{p}")
                    for bq in range(4):
                        src_ = ttx[bq * 32:(bq + 1) * 32, :].rearrange(
                            "d (g e) -> d g e", g=16)
                        inc(g.dma_start(out=mc_view[0, 0, 4 * p + bq, :, :, :],
                                        in_=src_), "sg16", 16)
                        inc(g.dma_start(out=mc_view[1, 1, 4 * p + bq, :, :, :],
                                        in_=src_), "sg16", 16)
                    mark(f"ttst{tag}{p}", "sg16")

    tt_chain(imag=True)
    tt_chain(imag=False)

    @block.gpsimd
    def _(g):
        mark("st_wr", "sg16")
        for nm, src_d, dst in [("vw", vw_d, vw_bf), ("ow", ow_d, ow_bf)]:
            for i0 in range(0, 8, 2):
                g.wait_ge(sems["sgc"], cnt["sgc"])
                inc(g.dma_start(out=wstage[:].rearrange("p (c m) -> p c m", c=2),
                                in_=src_d[i0 * 128:(i0 + 2) * 128, :]
                                .rearrange("(c k) m -> k c m", k=128)), "sg16", 16)
                g.wait_ge(sems["sg16"], cnt["sg16"])
                inc(g.tensor_copy(dst[:, i0:i0 + 2, :],
                                  wstage[:].rearrange("p (c m) -> p c m", c=2)), "sgc")
        g.wait_ge(sems["sgc"], cnt["sgc"])
        inc(g.nop(), "sgc")
        mark("vwow_bf", "sgc")

    if "D" in stages:
        pstage = num_fm[:].rearrange("p a b -> p (a b)")
        preps = []
        for i in range(8):
            for q in range(4):
                preps.append((f1w_d[i * 128:(i + 1) * 128, q * 1024:(q + 1) * 1024],
                              f1bf_dram.ap()[i, :, q * 1024:(q + 1) * 1024]))
        for m in range(32):
            preps.append((f2w_d[m * 128:(m + 1) * 128, :], f2bf_dram.ap()[m, :, :]))

        for k, (srcp, dstp) in enumerate(preps):
            sl = k % 2

            @block.sync
            def _(sy, k=k, sl=sl, srcp=srcp):
                if k == 0:
                    wait(sy, "vwow_bf")
                    wait(sy, ROWS_DONE)
                if k >= 2:
                    wait(sy, f"cst{k-2}")
                inc(sy.dma_start(out=wstage[:, sl * 1024:(sl + 1) * 1024], in_=srcp),
                    f"wp{sl}", 16)
                mark(f"win{k}", f"wp{sl}")

            @block.gpsimd
            def _(g, k=k, sl=sl, dstp=dstp):
                wait(g, f"win{k}")
                if k >= 1:
                    wait(g, f"pout{k-1}")
                inc(g.tensor_copy(pstage, wstage[:, sl * 1024:(sl + 1) * 1024]),
                    "sgc")
                mark(f"cst{k}", "sgc")
                g.wait_ge(sems["sgc"], cnt["sgc"])
                inc(g.dma_start(out=dstp, in_=pstage), f"po{sl}", 16)
                mark(f"pout{k}", f"po{sl}")

        @block.gpsimd
        def _(g):
            wait(g, "pout62")
            wait(g, "pout63")
            inc(g.nop(), "sg16")
            mark("f2prep", "sg16")

    def stats_chain(src_hi, src_lo, tag, sc, bg, s2=False):
        @block.vector
        def _(v):
            v.wait_ge(sems["sv"], cnt["sv"])
            v.bn_stats(bnst[:, 0, :], src_hi)
            inc(v.bn_stats(bnst[:, 1, :], src_lo), "sv")
            v.drain()
            v.wait_ge(sems["sv"], cnt["sv"])
            inc(v.bn_aggr(bg, bnst[:]), "sv")
            mark(f"sa{tag}", "sv")

        @block.scalar
        def _(s):
            wait(s, f"sa{tag}")
            inc(s.activation(sc[:, 4:5], bg[:, 1:2], AF.Sqrt, bias=eps_col[:]), "ss")
            mark(f"sq{tag}", "ss")

        @block.vector
        def _(v):
            wait(v, f"sq{tag}")
            v.reciprocal(sc[:, 0:1], sc[:, 4:5])
            v.drain()
            inc(v.scalar_tensor_tensor(sc[:, 1:2], bg[:, 0:1], -1.0, sc[:, 0:1],
                                       op0=ALU.mult, op1=ALU.mult), "sv")
            if s2:
                v.drain()
                v.wait_ge(sems["sv"], cnt["sv"])
                v.scalar_tensor_tensor(sc[:, 5:6], bg[:, 1:2], 1.0, sc[:, 0:1],
                                       op0=ALU.mult, op1=ALU.mult)
                v.drain()
                inc(v.scalar_tensor_tensor(sc[:, 6:7], sc[:, 5:6], 1.0, sc[:, 0:1],
                                           op0=ALU.mult, op1=ALU.mult), "sv")
                mark(f"su{tag}", "sv")
            else:
                mark(f"ssd{tag}", "sv")

        if s2:
            @block.scalar
            def _(s):
                wait(s, f"su{tag}")
                inc(s.activation(sc[:, 7:8], sc[:, 6:7], AF.Sqrt, bias=eps_col[:]), "ss")
                mark(f"ssd{tag}", "ss")

    if "A" in stages:
        @block.scalar
        def _(s):
            for c in range(2):
                inc(s.dma_start(out=xs_st[:, c, :], in_=x_d[c * 128:(c + 1) * 128, :]),
                    f"sx{c % 2}", 16)
                mark(f"xld{c}", f"sx{c % 2}")

        for c in range(NCH):
            buf = c % 2

            @block.vector
            def _(v, c=c):
                wait(v, f"xld{c}")
                if c == 0:
                    wait(v, "rep4_v")
                    wait(v, "gm_bf")

            stats_chain(xs_st[:, buf, 0:512], xs_st[:, buf, 512:1024],
                        tag=f"a{c}", sc=scol_t[:, c, :], bg=bnag_t[:, c, :], s2=True)
            @block.vector
            def _(v, c=c, buf=buf):
                wait(v, f"ssda{c}")
                v.wait_ge(sems["sv"], cnt["sv"])
                v.reciprocal(scol_t[:, c, 2:3], scol_t[:, c, 7:8])
                v.tensor_scalar(tmpA[:], xs_st[:, buf, :], scol_t[:, c, 0:1],
                                scol_t[:, c, 1:2], op0=ALU.mult, op1=ALU.add)
                v.drain()
                v.tensor_tensor(y_t[:, c, :], tmpA[:], ob_rep[:], op=ALU.add)
                v.scalar_tensor_tensor(tmpB[:], tmpA[:], scol_t[:, c, 2:3], g1rep[:],
                                       op0=ALU.mult, op1=ALU.mult)
                v.drain()
                if c >= 2:
                    wait(v, f"xtrt{c-2}")
                    wait(v, f"pt{c-2}")
                inc(v.tensor_tensor(xnbf[:, c % 2, :], tmpB[:], b1rep[:], op=ALU.add), "sv")
                mark(f"xnv{c}", "sv")

            if c + 2 < NCH:
                @block.scalar
                def _(s, c=c):
                    wait(s, f"xnv{c}")
                    inc(s.dma_start(out=xs_st[:, c % 2, :],
                                    in_=x_d[(c + 2) * 128:(c + 3) * 128, :]),
                        f"sx{c % 2}", 16)
                    mark(f"xld{c+2}", f"sx{c % 2}")

            @block.tensor
            def _(t, c=c):
                wait(t, f"xnv{c}")
                if c == 0:
                    wait(t, "masks")
                if c > 0:
                    wait(t, f"xtr{c-1}")
                for i in range(8):
                    col = (i % 4) * 1024 + (i // 4) * 128
                    ti = t.transpose(trA_ps[:, col:col + 128],
                                     xnbf[:, c % 2, i * 128:(i + 1) * 128], identb[:])
                    if i == 3:
                        inc(ti, "stt")
                        t.wait_ge(sems["stt"], cnt["stt"])
                inc(t.nop(), "stt")
                mark(f"xtrt{c}", "stt")

            @block.vector
            def _(v, c=c):
                wait(v, f"xtrt{c}")
                for b in range(4):
                    cp = v.tensor_copy(
                        xn_fm[:, b:8:4, c * 128:(c + 1) * 128],
                        trA_ps[:, b * 1024:b * 1024 + 256]
                        .rearrange("p (a b) -> p a b", a=2))
                    if b == 3:
                        inc(cp, "sv")
                mark(f"xtr{c}", "sv")

            @block.tensor
            def _(t, c=c):
                wait(t, f"xtr{c}")
                if c == 0:
                    wait(t, "qkw_bf")
                    wait(t, "brow0_v")
                    wait(t, "iota")
                if c == 0:
                    wait(t, "ttevr7")
                if c > 0:
                    wait(t, f"eluv{c-1}")
                for i in range(8):
                    t.matmul(repvp_ps[:, 0:128], xn_fm[:, i, c * 128:(c + 1) * 128],
                             qkw_bf[:, i, :], start=(i == 0), stop=False)
                inc(t.matmul(repvp_ps[:, 0:128], ones_row_bf[:], qkb_row[:],
                         start=False, stop=True), "stt")
                mark(f"qkmm{c}", "stt")

            @block.vector
            def _(v, c=c):
                wait(v, f"qkmm{c}")
                inc(v.tensor_scalar(elu_sc[:, 0:128], repvp_ps[:, 0:128], 0.0, None, op0=ALU.min), "sv")
                mark(f"elumin{c}", "sv")

            @block.scalar
            def _(s, c=c):
                wait(s, f"elumin{c}")
                inc(s.activation(elu_sc[:, 128:256], elu_sc[:, 0:128], AF.Exp), "ss")
                mark(f"eluexp{c}", "ss")

            @block.vector
            def _(v, c=c):
                wait(v, f"eluexp{c}")
                v.scalar_tensor_tensor(elu_sc[:, 0:128], repvp_ps[:, 0:128], 0.0,
                                       elu_sc[:, 128:256], op0=ALU.max, op1=ALU.add)
                v.drain()
                v.tensor_copy(qke_t[:, c, :], elu_sc[:, 0:128])
                inc(v.tensor_copy(qf32_t[:, c, :], elu_sc[:, 0:64]), "sv")
                mark(f"eluv{c}", "sv")

            @block.tensor
            def _(t, c=c):
                wait(t, f"eluv{c}")
                for i in range(8):
                    t.matmul(pt_ps[:, i * 64:(i + 1) * 64],
                             xnbf[:, c % 2, i * 128:(i + 1) * 128],
                             qke_t[:, c, 64:128],
                             start=(c == 0 and i == 0),
                             stop=(c == NCH - 1 and i == 7))
                inc(t.matmul(pt_ps[0:1, 512:576], ones_col_bf[:], qke_t[:, c, 64:128],
                         start=(c == 0), stop=(c == NCH - 1)), "stt")
                mark(f"pt{c}", "stt")

        @block.vector
        def _(v):
            wait(v, f"pt{NCH-1}")
            v.tensor_copy(payload[:, 0:512], pt_ps[:, 0:512])
            inc(v.tensor_copy(payload[0:1, 512:576], pt_ps[0:1, 512:576]), "sv")
            mark("payv", "sv")

        if "tap_xnfm" in tap_d:
            @block.gpsimd
            def _(g):
                wait(g, f"pt{NCH-1}")
                inc(g.dma_start(out=tap_d["tap_xnfm"], in_=xn_fm[:]), "sg16", 16)

        if "tap_pay" in tap_d:
            @block.gpsimd
            def _(g):
                wait(g, "payv")
                inc(g.dma_start(out=tap_d["tap_pay"], in_=payload[:]), "sg16", 16)
        if "tap_qke" in tap_d:
            @block.gpsimd
            def _(g):
                wait(g, f"eluv{NCH-1}")
                inc(g.dma_start(out=tap_d["tap_qke"]
                                .rearrange("p (a b) -> p a b", b=128),
                                in_=qke_t[:]), "sg16", 16)
        if "tap_y" in tap_d and "C" not in stages:
            @block.gpsimd
            def _(g):
                wait(g, f"xnv{NCH-1}")
                for c in range(NCH):
                    inc(g.dma_start(out=tap_d["tap_y"][c * 128:(c + 1) * 128, :],
                                    in_=y_t[:, c, :]), "sg16", 16)

    if "B" in stages or "C" in stages:
        pAp.close()
        p1p = ExitStack()
        kv_ps = p1p.enter_context(nc.psum_tensor("kv_ps", [64, 1024], F32))
        vp_ps = p1p.enter_context(nc.psum_tensor("vp_ps", [128, 1024], F32))
        no_ps = p1p.enter_context(nc.psum_tensor("no_ps", [128, 1024], F32))
        m1_ps = p1p.enter_context(nc.psum_tensor("m1_ps", [128, 512], F32))
        trB_ps = p1p.enter_context(nc.psum_tensor("trB_ps", [128, 1024], BF16))

    if "B" in stages:
        @block.gpsimd
        def _(g):
            wait(g, "payv")
            inc(g.dma_start(out=pay_dram.ap(), in_=payload[:]), "sg16", 16)
            mark("paydram", "sg16")
            wait(g, "paydram")
            inc(g.collective_compute(
                "AllGather", ALU.bypass, replica_groups=[list(range(NCORES))],
                ins=[pay_dram.ap().opt()], outs=[ag_dram.ap().opt()]), "scc")
            mark("ag", "scc")
            wait(g, "ag")

        for cc in range(NCORES):
            @block.gpsimd
            def _(g, cc=cc):
                if cc > 0:
                    wait(g, f"agacc{cc-1}")
                inc(g.dma_start(out=agstage[:], in_=ag_dram.ap()[cc, :, :]), "sg16", 16)
                mark(f"agld{cc}", "sg16")

            @block.vector
            def _(v, cc=cc):
                wait(v, f"agld{cc}")
                if cc == 0:
                    wait(v, "gm_f32")
                pt_flat = ptpre[:].rearrange("p a b -> p (a b)")
                if cc == 0:
                    v.tensor_scalar(pt_flat, agstage[:, 0:512], wpre_sb[:, 0:1],
                                    None, op0=ALU.mult)
                    inc(v.tensor_scalar(ks_run[:], agstage[0:1, 512:576],
                                        wpre_sb[0:1, 0:1], None, op0=ALU.mult), "sv")
                else:
                    v.scalar_tensor_tensor(pt_flat, agstage[:, 0:512],
                                           wpre_sb[:, cc:cc + 1], pt_flat,
                                           op0=ALU.mult, op1=ALU.add)
                    inc(v.scalar_tensor_tensor(ks_run[:], agstage[0:1, 512:576],
                                           wpre_sb[0:1, cc:cc + 1], ks_run[:],
                                           op0=ALU.mult, op1=ALU.add), "sv")
                mark(f"agacc{cc}", "sv")

        @block.vector
        def _(v):
            wait(v, f"agacc{NCORES-1}")
            if "D" in stages:
                wait(v, "f2prep")
            v.drain()
            v.tensor_copy(ptpre_bf, ptpre[:])
            inc(v.tensor_copy(q0pre_bf[:], ks_run[:]), "sv")
            mark("ptbf", "sv")

        @block.tensor
        def _(t):
            wait(t, "ptbf")
            wait(t, "vwow_bf")
            for i in range(8):
                for half in range(2):
                    t.matmul(kv_ps[:, half * 512:(half + 1) * 512], num_fm[:, i, 0:64],
                             vw_bf[:, i, half * 512:(half + 1) * 512],
                             start=(i == 0), stop=False)
            for half in range(2):
                inc(t.matmul(kv_ps[:, half * 512:(half + 1) * 512], q0pre_bf[:],
                         vb_row[0:1, half * 512:(half + 1) * 512],
                         start=False, stop=True), "stt")
            mark("kvfold", "stt")

        @block.vector
        def _(v):
            wait(v, "kvfold")
            v.tensor_copy(kvacc[:], kv_ps[:])
            inc(v.tensor_copy(kv_sb[:], kv_ps[:]), "sv")
            mark("kvsb_-1", "sv")

        @block.tensor
        def _(t):
            wait(t, "ptbf")
            inc(t.matmul(m1_ps[:, 256:320], ones_row_f32[:], ks_run[:], start=True, stop=True), "stt")
            mark("ksrept_-1", "stt")

        @block.vector
        def _(v):
            wait(v, "ksrept_-1")
            inc(v.tensor_copy(ks_rep[:], m1_ps[:, 256:320]), "sv")
            mark("ksrepv_-1", "sv")

        if "tap_kvpre" in tap_d:
            @block.gpsimd
            def _(g):
                wait(g, "kvsb_-1")
                inc(g.dma_start(out=tap_d["tap_kvpre"], in_=kv_sb[:]), "sg16", 16)

    if "C" in stages:
        for c in range(NCH):
            buf = c % 2

            @block.tensor
            def _(t, c=c, buf=buf):
                if c == 0:
                    wait(t, f"xtr{NCH-1}")
                if c > 0:
                    wait(t, f"trevac{c-1}")
                inc(t.transpose(trB_ps[:, 0:128], qke_t[:, c, :], identb[:]), "stt")
                mark(f"qtr{c}", "stt")
                if c == 0:
                    wait(t, "vwow_bf")
                    wait(t, ROWS_DONE)
                    wait(t, "ttevr7")
                    wait(t, f"eluv{NCH-1}")
                else:
                    wait(t, f"vtok_s{c-1}")
                for half in range(2):
                    for i in range(8):
                        t.matmul(vp_ps[:, half * 512:(half + 1) * 512],
                                 xn_fm[:, i, c * 128:(c + 1) * 128],
                                 vw_bf[:, i, half * 512:(half + 1) * 512],
                                 start=(i == 0), stop=False)
                    inc(t.matmul(vp_ps[:, half * 512:(half + 1) * 512], ones_row_bf[:],
                             vb_row[0:1, half * 512:(half + 1) * 512],
                             start=False, stop=True), "stt")
                mark(f"vp{c}", "stt")

            @block.scalar
            def _(s, c=c, buf=buf):
                wait(s, f"vp{c}")
                inc(s.activation(vtok[:, buf, :], vp_ps[:], AF.Identity), "ss")
                mark(f"vtok_s{c}", "ss")

            @block.vector
            def _(v, c=c):
                if c > 0:
                    wait(v, f"tr{c-1}")
                inc(v.tensor_copy(qzk[:, 0:64], qke_t[:, c, 64:128]), "sv")
                mark(f"qzkv{c}", "sv")

            @block.tensor
            def _(t, c=c):
                wait(t, f"qzkv{c}")
                t.wait_ge(sems["stt"], land[f"qtr{c}"][1])
                inc(t.transpose(trB_ps[:, 512:640], qzk[:], identb[:]), "stt")
                mark(f"tr{c}", "stt")

            @block.vector
            def _(v, c=c):
                wait(v, f"tr{c}")
                v.tensor_copy(qfm_sb[:], trB_ps[0:64, 0:128])
                inc(v.tensor_copy(kfm_sb[:], trB_ps[0:64, 512:640]), "sv")
                mark(f"trevac{c}", "sv")

            @block.tensor
            def _(t, c=c):
                wait(t, f"trevac{c}")
                wait(t, f"ksrepv_{c-1}" if c == 0 else f"ksrepv{c-1}")
                inc(t.matmul(m1_ps[:, 0:128], kfm_sb[:], qfm_sb[:], start=True, stop=True), "stt")
                mark(f"A{c}", "stt")

            @block.vector
            def _(v, c=c):
                wait(v, f"A{c}")
                inc(v.tensor_tensor(am_bf[:], m1_ps[:, 0:128], mask_bf[:], op=ALU.mult), "sv")
                mark(f"am{c}", "sv")

            @block.tensor
            def _(t, c=c):
                wait(t, f"am{c}")
                t.matmul(m1_ps[:, 128:129], am_bf[:], ones_col_bf[:], start=True, stop=True)
                inc(t.matmul(m1_ps[0:1, 192:256], ones_col_bf[:], qke_t[:, c, 64:128],
                         start=True, stop=True), "stt")
                mark(f"dent{c}", "stt")

            @block.vector
            def _(v, c=c):
                wait(v, f"dent{c}")
                v.scalar_tensor_tensor(tmpB[:, 0:64], qf32_t[:, c, :], 1.0,
                                       ks_rep[:], op0=ALU.mult, op1=ALU.mult,
                                       accum_out=den2[:])
                v.scalar_tensor_tensor(ks_run[:], m1_ps[0:1, 192:256], 1.0, ks_run[:],
                                       op0=ALU.mult, op1=ALU.add)
                v.drain()
                v.scalar_tensor_tensor(den_f[:], m1_ps[:, 128:129], ATTN_EPS, den2[:],
                                       op0=ALU.add, op1=ALU.add)
                v.drain()
                inc(v.reciprocal(rden[:], den_f[:]), "sv")
                mark(f"denv{c}", "sv")

            @block.tensor
            def _(t, c=c):
                wait(t, f"denv{c}")
                inc(t.matmul(m1_ps[:, 256:320], ones_row_f32[:], ks_run[:], start=True, stop=True), "stt")
                mark(f"ksrept{c}", "stt")

            @block.vector
            def _(v, c=c):
                wait(v, f"ksrept{c}")
                inc(v.tensor_copy(ks_rep[:], m1_ps[:, 256:320]), "sv")
                mark(f"ksrepv{c}", "sv")

            @block.tensor
            def _(t, c=c, buf=buf):
                wait(t, f"vtok_s{c}")
                wait(t, f"am{c}")
                wait(t, "kvsb_-1" if c == 0 else f"kvsb_{c-1}")
                if c > 0:
                    wait(t, f"y1_{c-1}")
                for i in range(8):
                    t.matmul(no_ps[:, i * 128:(i + 1) * 128],
                             vtok[:, buf, i * 128:(i + 1) * 128], am_bf[:],
                             start=True, stop=False)
                    inc(t.matmul(no_ps[:, i * 128:(i + 1) * 128],
                             kv_sb[:, i * 128:(i + 1) * 128], qfm_sb[:],
                             start=False, stop=True), "stt")
                mark(f"num{c}", "stt")

            @block.vector
            def _(v, c=c):
                wait(v, f"num{c}")
                inc(v.tensor_copy(num_fm[:].rearrange("p a b -> p (a b)"), no_ps[:]), "sv")
                mark(f"numev{c}", "sv")

            @block.tensor
            def _(t, c=c, buf=buf):
                for half in range(2):
                    inc(t.matmul(kv_ps[:, half * 512:(half + 1) * 512],
                             qke_t[:, c, 64:128],
                             vtok[:, buf, half * 512:(half + 1) * 512],
                             start=True, stop=True), "stt")
                mark(f"kvupd{c}", "stt")

            @block.vector
            def _(v, c=c):
                wait(v, f"kvupd{c}")
                v.scalar_tensor_tensor(kvacc[:], kv_ps[:], 1.0, kvacc[:],
                                       op0=ALU.mult, op1=ALU.add)
                v.drain()
                inc(v.tensor_copy(kv_sb[:], kvacc[:]), "sv")
                mark(f"kvsb_{c}", "sv")

            @block.tensor
            def _(t, c=c):
                wait(t, f"numev{c}")
                for half in range(2):
                    for i in range(8):
                        inc(t.matmul(no_ps[:, half * 512:(half + 1) * 512], num_fm[:, i, :],
                                 ow_bf[:, i, half * 512:(half + 1) * 512],
                                 start=(i == 0), stop=(i == 7)), "stt")
                mark(f"O{c}", "stt")

            @block.vector
            def _(v, c=c):
                wait(v, f"O{c}")
                inc(v.scalar_tensor_tensor(y_t[:, c, :], no_ps[:], rden[:], y_t[:, c, :],
                                       op0=ALU.mult, op1=ALU.add), "sv")
                mark(f"y1_{c}", "sv")

        @block.gpsimd
        def _(g):
            wait(g, "ttstr7")
            wait(g, f"O{NCH-1}")
            inc(g.dma_start(out=mc_bf[:],
                            in_=mc_dram.ap().rearrange("(i p) n -> p i n", p=128)),
                "smc", 16)
            mark("mc_loaded", "smc")

        if "tap_mc" in tap_d:
            @block.gpsimd
            def _(g):
                wait(g, "mc_loaded")
                inc(g.dma_start(out=tap_d["tap_mc"], in_=mc_bf[:]), "sg16", 16)

        if "tap_y" in tap_d and "D" not in stages:
            @block.gpsimd
            def _(g):
                wait(g, f"y1_{NCH-1}")
                for c in range(NCH):
                    inc(g.dma_start(out=tap_d["tap_y"][c * 128:(c + 1) * 128, :],
                                    in_=y_t[:, c, :]), "sg16", 16)


    if "D" in stages:
        p1p.close()
        p2p = ExitStack()
        f2ps = [p2p.enter_context(nc.psum_tensor(f"f2ps{i}", [128, 1024], F32))
                for i in range(2)]
        tr2_ps = p2p.enter_context(nc.psum_tensor("tr2_ps", [128, 4096], BF16))
        f1ps = [f2ps[0][:, 0:512], f2ps[0][:, 512:1024]]

        for j in range(4):
            for ci in range(4):
                c = 4 * j + ci

                @block.vector
                def _(v, c=c, j=j):
                    wait(v, f"y1_{c}")
                    if j > 0:
                        wait(v, f"y2_{4*(j-1)+3}")

                stats_chain(y_t[:, c, 0:512], y_t[:, c, 512:1024], tag=f"d{c}",
                            sc=scol_t[:, c, :], bg=bnag_t[:, c, :])

                @block.vector
                def _(v, c=c):
                    wait(v, f"ssdd{c}")
                    v.wait_ge(sems["sv"], cnt["sv"])
                    v.tensor_scalar(tmpA[:], y_t[:, c, :], scol_t[:, c, 0:1],
                                    scol_t[:, c, 1:2], op0=ALU.mult, op1=ALU.add)
                    v.drain()
                    v.tensor_tensor(tmpB[:], tmpA[:], g2rep[:], op=ALU.mult)
                    v.drain()
                    if c >= 2:
                        wait(v, f"ztrt{c-2}")
                    inc(v.tensor_tensor(xnbf[:, c % 2, :], tmpB[:], b2rep[:], op=ALU.add), "sv")
                    mark(f"z2v{c}", "sv")

                @block.tensor
                def _(t, c=c, ci=ci, j=j):
                    wait(t, f"z2v{c}")
                    if c > 0:
                        wait(t, f"ztr{c-1}")
                    if j > 0:
                        wait(t, f"f1mm{j-1}_31")
                    for i in range(8):
                        col = (i % 4) * 1024 + (i // 4) * 128
                        ti = t.transpose(tr2_ps[:, col:col + 128],
                                         xnbf[:, c % 2, i * 128:(i + 1) * 128], identb[:])
                        if i == 3:
                            inc(ti, "stt")
                            t.wait_ge(sems["stt"], cnt["stt"])
                    inc(t.nop(), "stt")
                    mark(f"ztrt{c}", "stt")

                @block.vector
                def _(v, c=c, ci=ci):
                    wait(v, f"ztrt{c}")
                    for b in range(4):
                        cp = v.tensor_copy(
                            zfm[:, b:8:4, ci * 128:(ci + 1) * 128],
                            tr2_ps[:, b * 1024:b * 1024 + 256]
                            .rearrange("p (a b) -> p a b", a=2))
                        if b == 3:
                            inc(cp, "sv")
                    mark(f"ztr{c}", "sv")

            for m in range(32):
                if m % 2 == 0:
                    m2 = m // 2
                    g1 = j * 16 + m2
                    @block.gpsimd
                    def _(g, j=j, m2=m2, g1=g1):
                        if g1 == 0:
                            wait(g, "f2prep")
                        if g1 >= 3:
                            pj, pm2 = (g1 - 3) // 16, (g1 - 3) % 16
                            wait(g, f"f1use{pj}_{2*pm2+1}")
                        src = f1bf_dram.ap()[:, :, m2 * 256:(m2 + 1) * 256] \
                            .rearrange("i p f -> p i f")
                        inc(g.dma_start(out=f1roll[:, g1 % 3, :, :], in_=src),
                            f"f1s{g1 % 3}", 16)
                        mark(f"f1ld{j}_{m2}", f"f1s{g1 % 3}")

                @block.tensor
                def _(t, j=j, m=m):
                    if m == 0:
                        wait(t, "gm_f32")
                        wait(t, f"ztr{4*j+3}")
                        if j > 0:
                            wait(t, f"y2_{4*(j-1)+3}")
                    wait(t, f"f1ld{j}_{m//2}")
                    if m >= 2:
                        wait(t, f"gelu{j}_{m-2}")
                    elif j > 0:
                        wait(t, f"gelu{j-1}_{30+m}")
                    for i in range(8):
                        inc(t.matmul(f1ps[m % 2],
                                 f1roll[:, (j * 16 + m // 2) % 3, i,
                                        (m % 2) * 128:(m % 2 + 1) * 128],
                                 zfm[:, i, :],
                                 start=(i == 0), stop=(i == 7)), "stt")
                    mark(f"f1mm{j}_{m}", "stt")
                    mark(f"f1use{j}_{m}", "stt")

                @block.scalar
                def _(s, j=j, m=m):
                    wait(s, f"f1mm{j}_{m}")
                    if j > 0:
                        wait(s, f"f2u{(j-1)*64+32+m}")
                    inc(s.activation(h_fm[:, m, :], f1ps[m % 2], AF.Gelu,
                                 bias=f1b_cols[:, m:m + 1], scale=1.0), "ss")
                    mark(f"gelu{j}_{m}", "ss")

            for sub in range(2):
                for m in range(32):
                    u = (j * 2 + sub) * 32 + m
                    if m % 2 == 0:
                        u2 = ((j * 2 + sub) * 32 + m) // 2

                        @block.sync
                        def _(sy, u2=u2, m=m):
                            if u2 == 0:
                                wait(sy, "f2prep")
                                wait(sy, f"O{NCH-1}")
                            if u2 >= 3:
                                wait(sy, f"f2u{2*(u2-3)+1}")
                            inc(sy.dma_start(
                                out=f2roll[:, u2 % 3, :, :],
                                in_=f2bf_dram.ap()[m:m + 2, :, :]
                                .rearrange("m p f -> p m f")), f"f2s{u2 % 3}", 16)
                            mark(f"f2ld{u2}", f"f2s{u2 % 3}")

                    @block.tensor
                    def _(t, j=j, sub=sub, m=m, u=u):
                        if m == 0:
                            wait(t, f"gelu{j}_31")
                            if sub == 0 and j > 0:
                                wait(t, f"y2_{4*(j-1)+3}")
                            if sub == 1:
                                wait(t, f"y2_{4*j+1}")
                        wait(t, f"f2ld{u//2}")
                        for ch2 in range(2):
                            for half in range(2):
                                inc(t.matmul(
                                    f2ps[ch2][:, half * 512:(half + 1) * 512],
                                    h_fm[:, m, (sub * 2 + ch2) * 128:(sub * 2 + ch2 + 1) * 128],
                                    f2roll[:, (u // 2) % 3, m % 2,
                                           half * 512:(half + 1) * 512],
                                    start=(m == 0), stop=False), "stt")
                        mark(f"f2u{u}", "stt")

                @block.tensor
                def _(t, j=j, sub=sub):
                    for ch2 in range(2):
                        for half in range(2):
                            inc(t.matmul(f2ps[ch2][:, half * 512:(half + 1) * 512],
                                     ones_row_bf[:],
                                     f2b_row[0:1, half * 512:(half + 1) * 512],
                                     start=False, stop=True), "stt")
                    mark(f"f2mm{j}_{sub}", "stt")

                for ch2 in range(2):
                    @block.vector
                    def _(v, j=j, sub=sub, ch2=ch2):
                        wait(v, f"f2mm{j}_{sub}")
                        c = 4 * j + sub * 2 + ch2
                        inc(v.scalar_tensor_tensor(y_t[:, c, :], f2ps[ch2][:], 1.0,
                                               y_t[:, c, :],
                                               op0=ALU.mult, op1=ALU.add), "sv")
                        mark(f"y2_{c}", "sv")

        if "tap_y" in tap_d and "E" not in stages:
            @block.gpsimd
            def _(g):
                wait(g, f"y2_{NCH-1}")
                for c in range(NCH):
                    inc(g.dma_start(out=tap_d["tap_y"][c * 128:(c + 1) * 128, :],
                                    in_=y_t[:, c, :]), "sg16", 16)
        p2p.close()

    if "E" in stages:
        p4p = ExitStack()
        wc = [p4p.enter_context(nc.psum_tensor(f"wc{i}", [128, 1024], F32))
              for i in range(2)]
        trp4_ps = p4p.enter_context(nc.psum_tensor("trp4_ps", [128, 4096], BF16))

        @block.scalar
        def _(s):
            for c in range(2):
                inc(s.dma_start(out=xs_st[:, c, :], in_=x_d[c * 128:(c + 1) * 128, :]),
                    f"sx{c % 2}", 16)
                mark(f"xld2_{c}", f"sx{c % 2}")

        for c in range(NCH):
            buf = c % 2

            @block.vector
            def _(v, c=c):
                wait(v, f"y2_{c}")

            stats_chain(y_t[:, c, 0:512], y_t[:, c, 512:1024], tag=f"e{c}",
                        sc=scol_t[:, c, :], bg=bnag_t[:, c, :])

            @block.vector
            def _(v, c=c, buf=buf):
                wait(v, f"ssde{c}")
                v.wait_ge(sems["sv"], cnt["sv"])
                v.tensor_scalar(tmpA[:], y_t[:, c, :], scol_t[:, c, 0:1],
                                scol_t[:, c, 1:2], op0=ALU.mult, op1=ALU.add)
                v.drain()
                wait(v, f"xld2_{c}")
                inc(v.tensor_tensor(y_t[:, c, :], tmpA[:], xs_st[:, buf, :], op=ALU.add), "sv")
                mark(f"x2_{c}", "sv")

            if c + 2 < NCH:
                @block.sync
                def _(sy, c=c):
                    wait(sy, f"x2_{c}")
                    inc(sy.dma_start(out=xs_st[:, c % 2, :],
                                     in_=x_d[(c + 2) * 128:(c + 3) * 128, :]),
                        f"sx{c % 2}", 16)
                    mark(f"xld2_{c+2}", f"sx{c % 2}")

            if "F" not in stages:
                continue

            stats_chain(y_t[:, c, 0:512], y_t[:, c, 512:1024], tag=f"f{c}",
                        sc=scol_t[:, c, :], bg=bnag_t[:, c, :])

            @block.vector
            def _(v, c=c):
                wait(v, f"ssdf{c}")
                v.wait_ge(sems["sv"], cnt["sv"])
                if c >= 2:
                    wait(v, f"wtrt{c-2}")
                inc(v.tensor_scalar(xnbf[:, c % 2, :], y_t[:, c, :], scol_t[:, c, 0:1],
                                    scol_t[:, c, 1:2], op0=ALU.mult, op1=ALU.add), "sv")
                mark(f"z4v{c}", "sv")

            @block.tensor
            def _(t, c=c):
                wait(t, f"z4v{c}")
                if c > 0:
                    wait(t, f"wtr{c-1}")
                elif "D" in stages:
                    wait(t, "f1mm3_31")
                if c >= 4:
                    wait(t, f"wcmm{c-4}")
                for i in range(8):
                    col = (i % 4) * 1024 + (i // 4) * 128
                    ti = t.transpose(trp4_ps[:, col:col + 128],
                                     xnbf[:, c % 2, i * 128:(i + 1) * 128], identb[:])
                    if i == 3:
                        inc(ti, "stt")
                        t.wait_ge(sems["stt"], cnt["stt"])
                inc(t.nop(), "stt")
                mark(f"wtrt{c}", "stt")

            @block.vector
            def _(v, c=c):
                wait(v, f"wtrt{c}")
                for b in range(4):
                    cp = v.tensor_copy(
                        zfm[:, b:8:4, (c % 4) * 128:(c % 4 + 1) * 128],
                        trp4_ps[:, b * 1024:b * 1024 + 256]
                        .rearrange("p (a b) -> p a b", a=2))
                    if b == 3:
                        inc(cp, "sv")
                mark(f"wtr{c}", "sv")

            @block.tensor
            def _(t, c=c):
                wait(t, f"wtr{c}")
                if c == 0:
                    wait(t, "mc_loaded")
                if c > 1:
                    wait(t, f"wfin{c-2}")
                for i in range(8):
                    for half in range(2):
                        t.matmul(wc[c % 2][:, half * 512:(half + 1) * 512],
                                 zfm[:, i, (c % 4) * 128:(c % 4 + 1) * 128],
                                 mc_bf[:, i, half * 512:(half + 1) * 512],
                                 start=(i == 0), stop=False)
                for half in range(2):
                    inc(t.matmul(wc[c % 2][:, half * 512:(half + 1) * 512], ones_row_bf[:],
                             fbias_row[0:1, half * 512:(half + 1) * 512],
                             start=False, stop=True), "stt")
                mark(f"wcmm{c}", "stt")

            @block.vector
            def _(v, c=c):
                wait(v, f"wcmm{c}")

            stats_chain(wc[c % 2][:, 0:512], wc[c % 2][:, 512:1024], tag=f"w{c}",
                        sc=scol_t[:, c, :], bg=bnag_t[:, c, :])

            @block.vector
            def _(v, c=c):
                wait(v, f"ssdw{c}")
                v.wait_ge(sems["sv"], cnt["sv"])
                v.tensor_scalar(tmpA_w, wc[c % 2][:], scol_t[:, c, 0:1],
                                scol_t[:, c, 1:2], op0=ALU.mult, op1=ALU.add)
                v.drain()
                if c >= 2:
                    wait(v, f"outst{c-2}")
                inc(v.tensor_tensor(out_sb[:, (c % 2) * 1024:(c % 2 + 1) * 1024], tmpA_w, y_t[:, c, :], op=ALU.add), "sv")
                mark(f"wfin{c}", "sv")

            @block.sync
            def _(sy, c=c):
                wait(sy, f"wfin{c}")
                inc(sy.dma_start(out=out_d[c * 128:(c + 1) * 128, :],
                                 in_=out_sb[:, (c % 2) * 1024:(c % 2 + 1) * 1024]), f"so{c % 2}", 16)
                mark(f"outst{c}", f"so{c % 2}")

        if "tap_y" in tap_d and "F" not in stages:
            @block.gpsimd
            def _(g):
                wait(g, f"x2_{NCH-1}")
                for c in range(NCH):
                    inc(g.dma_start(out=tap_d["tap_y"][c * 128:(c + 1) * 128, :],
                                    in_=y_t[:, c, :]), "sg16", 16)

        if "F" in stages:
            @block.vector
            def _(v):
                v.wait_ge(sems["so0"], cnt["so0"])
                v.wait_ge(sems["so1"], cnt["so1"])
        p4p.close()

    @block.gpsimd
    def _(g):
        g.wait_ge(sems["sg16"], cnt["sg16"])

    st.close()
    return nc


def _prep_inmaps(inputs):
    f32 = np.float32
    x = np.asarray(inputs["L0_attn_x"], f32)
    qw = np.asarray(inputs["L0_attn_q_w"], f32)
    kw = np.asarray(inputs["L0_attn_k_w"], f32)
    vw = np.asarray(inputs["L0_attn_v_w"], f32)
    ow = np.asarray(inputs["L0_attn_o_w"], f32)
    f1w = np.asarray(inputs["L0_attn_f1_w"], f32)
    f2w = np.asarray(inputs["L0_attn_f2_w"], f32)
    qkw_t = np.ascontiguousarray(np.concatenate([qw, kw], axis=0).T)
    vw_t = np.ascontiguousarray(vw.T)
    ow_t = np.ascontiguousarray(ow.T)
    f1w_t = np.ascontiguousarray(f1w.T)
    f2w_t = np.ascontiguousarray(f2w.T)

    rows = np.zeros((9, 1024), f32)
    rows[0] = np.asarray(inputs["L0_attn_ln1_w"], f32)
    rows[1] = np.asarray(inputs["L0_attn_ln1_b"], f32)
    rows[2] = np.asarray(inputs["L0_attn_ln2_w"], f32)
    rows[3] = np.asarray(inputs["L0_attn_ln2_b"], f32)
    rows[4] = np.asarray(inputs["L0_attn_o_b"], f32)
    rows[5] = np.asarray(inputs["L0_attn_v_b"], f32)
    rows[6] = np.asarray(inputs["L0_attn_f2_b"], f32)
    rows[7] = np.concatenate([np.asarray(inputs["L0_ffn_bias_real"], f32),
                              np.asarray(inputs["L0_ffn_bias_imag"], f32)])
    rows[8, 0:128] = np.concatenate([np.asarray(inputs["L0_attn_q_b"], f32),
                                     np.asarray(inputs["L0_attn_k_b"], f32)])

    f1b_cols = np.ascontiguousarray(
        np.asarray(inputs["L0_attn_f1_b"], f32).reshape(32, 128).T)

    G1R = np.asarray(inputs["L0_ffn_G1_R"], f32)
    G2R = np.asarray(inputs["L0_ffn_G2_R"], f32)
    G1I = np.asarray(inputs["L0_ffn_G1_I"], f32)
    G2I = np.asarray(inputs["L0_ffn_G2_I"], f32)
    g1m = np.ascontiguousarray(np.concatenate(
        [G1R[0].transpose(1, 0, 2).reshape(4, 1024),
         G1I[0].transpose(1, 0, 2).reshape(4, 1024)], axis=1))
    g2m = np.ascontiguousarray(np.concatenate(
        [G2R[:, :, 0, :].transpose(0, 2, 1).reshape(4, 256),
         G2I[:, :, 0, :].transpose(0, 2, 1).reshape(4, 256)], axis=1))

    in_maps = []
    for core in range(NCORES):
        b, q = core // QPER, core % QPER
        w_pre = np.zeros((128, 8), f32)
        for cc in range(NCORES):
            if cc // QPER == b and cc % QPER < q:
                w_pre[:, cc] = 1.0
        in_maps.append({
            "x": np.ascontiguousarray(x[b, q * T:(q + 1) * T]),
            "qkw_t": qkw_t, "vw_t": vw_t, "ow_t": ow_t,
            "f1w_t": f1w_t, "f2w_t": f2w_t,
            "g1m": g1m, "g2m": g2m, "rows": rows, "f1b_cols": f1b_cols,
            "w_pre": w_pre,
        })
    return in_maps


_NC_CACHE = {}


def kernel(**inputs):
    key = ("ABCDEF", ())
    if key not in _NC_CACHE:
        _NC_CACHE[key] = build_nc(taps=(), stages="ABCDEF")
    nc = _NC_CACHE[key]
    in_maps = _prep_inmaps(inputs)
    res = run_bass_kernel_spmd(nc, in_maps, core_ids=list(range(NCORES)))
    out = np.zeros((B, N, D), np.float32)
    for core in range(NCORES):
        b, q = core // QPER, core % QPER
        out[b, q * T:(q + 1) * T] = res.results[core]["out"]
    return out


if __name__ == "__main__":
    nc = build_nc()
    print("build ok")
